# version 01
# baseline (speedup 1.0000x reference)
import numpy as np
import ml_dtypes
import concourse.mybir as mybir
from concourse.bacc import Bacc
from concourse.bass_utils import run_bass_kernel_spmd
from contextlib import ExitStack
from bass_rust import ReduceOp

F32 = mybir.dt.float32
BF16 = mybir.dt.bfloat16
I16 = mybir.dt.int16
U8 = mybir.dt.uint8
U32 = mybir.dt.uint32
AF = mybir.ActivationFunctionType
ALU = mybir.AluOpType
AX = mybir.AxisListType
BF = ml_dtypes.bfloat16

B = 8
DM = 1024
NH = 16
LAT = 256
NE = 16
EPC = 2
DC = DM // 128
LC = LAT // 128
EPS = 1e-5
T = 4096
TB = 512
NTB = T // TB
NTC = TB // 128
TALL = B * T
ISZ = 4096
SHZ = 2048
ICH = ISZ // 128
SIC = SHZ // 128
CAPK = 4096
GCH = 512
NGC = CAPK // GCH
CAPQ = 128
CAP2 = 320
BISN = 50
NX = T // 128
FQ = TALL // 16 // 4


class Rec:
    STREAMS = ("sync", "scalar", "vector", "tensor", "gpsimd")

    def __init__(self, nc, ctx):
        self.nc = nc
        self.sem = {}
        for s in ("sync", "scalar", "vector", "tensor", "gpsimd",
                  "qsync", "qscal", "qgp", "cc"):
            self.sem[s] = ctx.enter_context(nc.semaphore("sem_" + s))
        self.cnt = {k: 0 for k in self.sem}
        self.chain_stream = {}
        self.tasks = {s: [] for s in self.STREAMS}
        self.writer = {}
        self.readers = {}

    def task(self, stream, emit, reads=(), writes=(), sem=None, inc=1):
        sname = sem or stream
        deps = {}
        for r in reads:
            w = self.writer.get(r)
            if w:
                deps[w[0]] = max(deps.get(w[0], 0), w[1])
        for r in writes:
            w = self.writer.get(r)
            if w:
                deps[w[0]] = max(deps.get(w[0], 0), w[1])
            for rd in self.readers.get(r, ()):
                deps[rd[0]] = max(deps.get(rd[0], 0), rd[1])
        waits, drain = [], False
        for dsem, dval in deps.items():
            if dsem == stream:
                if stream != "tensor":
                    drain = True
            else:
                waits.append((dsem, dval))
        self.cnt[sname] += inc
        val = self.cnt[sname]
        chain = None
        if sname[0] == "q" or sname == "cc":
            prev = self.chain_stream.setdefault(sname, stream)
            assert prev == stream, (sname, prev, stream)
            if val - inc > 0:
                chain = (sname, val - inc)
        self.tasks[stream].append((waits, drain, emit, self.sem[sname], inc, chain))
        token = (sname, val)
        for r in writes:
            self.writer[r] = token
            self.readers[r] = []
        for r in reads:
            self.readers.setdefault(r, []).append(token)
        return token

    def replay(self, block):
        for stream in self.STREAMS:
            tasks = self.tasks[stream]
            if not tasks:
                continue

            def mk(tasks):
                def body(eng):
                    for waits, drain, emit, semh, inc, chain in tasks:
                        for dsem, dval in waits:
                            eng.wait_ge(self.sem[dsem], dval)
                        if chain is not None:
                            eng.wait_ge(self.sem[chain[0]], chain[1])
                        if drain:
                            eng.drain()
                        instr = emit(eng)
                        if instr is not None:
                            instr.then_inc(semh, inc)
                        else:
                            eng.nop().then_inc(semh, inc)
                return body
            getattr(block, stream)(mk(tasks))


def build(dbg=None):
    nc = Bacc("TRN2", target_bir_lowering=False, debug=False)
    ctx = ExitStack()
    sb = lambda name, shape, dt: ctx.enter_context(nc.sbuf_tensor(name, shape, dt))
    ps = lambda name: ctx.enter_context(nc.psum_tensor(name, [128, 512], F32))

    di = lambda n, s: nc.dram_tensor(n, s, F32, kind="ExternalInput").ap()
    d_x = di("x", [T, DM])
    d_wdq = di("wdq", [LAT, DM])
    d_wdqT = di("wdqT", [DM, LAT])
    d_wuq = di("wuq", [DM, LAT])
    d_wuk = di("wuk", [DM, LAT])
    d_wdkvT = di("wdkvT", [DM, LAT])
    d_wuv = di("wuv", [DM, LAT])
    d_woT = di("woT", [DM, DM])
    d_gwT = di("gwT", [DM, NE])
    d_wgT = di("wgT", [EPC, DM, ISZ])
    d_wuT = di("wuT", [EPC, DM, ISZ])
    d_wdT = di("wdT", [EPC, ISZ, DM])
    d_sgT = di("sgT", [DM, SHZ])
    d_suT = di("suT", [DM, SHZ])
    d_sdT = di("sdT", [SHZ, DM])
    d_masks = nc.dram_tensor("masks", [128, 896], BF16, kind="ExternalInput").ap()
    d_identf = di("identf", [128, 128])
    d_tabN = di("tabN", [16, TALL // 16])
    d_tabP = di("tabP", [16, TALL // 16])
    d_slotpos = di("slotpos", [16, CAPQ])

    d_out = nc.dram_tensor("out", [T, DM], F32, kind="ExternalOutput").ap()

    d_x1f = nc.dram_tensor("x1f", [T, DM], F32, kind="Internal").ap()
    d_agin = nc.dram_tensor("agin", [T, DM], BF16, kind="Internal").ap()
    d_x1all = nc.dram_tensor("x1all", [TALL, DM], BF16, kind="Internal").ap()
    d_scT = nc.dram_tensor("scT", [NE, T], F32, kind="Internal").ap()
    d_scA = nc.dram_tensor("scA", [B * NE, T], F32, kind="Internal").ap()
    d_scB = nc.dram_tensor("scB", [EPC, B, T], F32, kind="Internal").ap()
    d_eidx = nc.dram_tensor("eidx", [128, 8], I16, kind="ExternalInput").ap()
    d_acc = nc.dram_tensor("acc", [TALL, DM], BF16, kind="Internal").ap()
    d_moe = nc.dram_tensor("moe", [T, DM], BF16, kind="Internal").ap()
    d_sh = nc.dram_tensor("sh", [T, DM], BF16, kind="Internal").ap()
    d_wgB = nc.dram_tensor("wgB", [EPC, DM, ISZ], BF16, kind="Internal").ap()
    d_wuB = nc.dram_tensor("wuB", [EPC, DM, ISZ], BF16, kind="Internal").ap()
    d_wdB = nc.dram_tensor("wdB", [EPC, ISZ, DM], BF16, kind="Internal").ap()
    d_sgB = nc.dram_tensor("sgB", [DM, SHZ], BF16, kind="Internal").ap()
    d_suB = nc.dram_tensor("suB", [DM, SHZ], BF16, kind="Internal").ap()
    d_sdB = nc.dram_tensor("sdB", [SHZ, DM], BF16, kind="Internal").ap()
    d_wflat = nc.dram_tensor("wflat", [EPC, CAPK], F32, kind="Internal").ap()

    RGRP = [list(range(B))]

    t_big = sb("t_big", [128, DC * T], BF16)
    bigv = t_big.rearrange("p (c t) -> p c t", c=DC)
    t_ckvT = sb("t_ckvT", [128, LC * T], BF16)
    ckvT = t_ckvT.rearrange("p (c s) -> p c s", c=LC)
    wd0b = t_ckvT.rearrange("p (c d) -> p c d", c=16)
    t_ckvtm = sb("t_ckvtm", [128, NX * LAT], BF16)
    ckvtm = t_ckvtm.rearrange("p (c l) -> p c l", l=LAT)
    eov = t_ckvtm[:, 0:NTC * DM].rearrange("p (c d) -> p c d", c=NTC)
    hsl = t_ckvtm.rearrange("p (c t) -> p c t", c=SIC)
    t_cqT = sb("t_cqT", [128, LC * T], BF16)
    cqT = t_cqT.rearrange("p (c t) -> p c t", c=LC)
    xgv = t_cqT.rearrange("p (a c t) -> p a c t", a=2, c=DC)
    t_mh = sb("t_mh", [128, NH * 2 * LAT], BF16)
    mhv = t_mh.rearrange("p (h a l) -> p h a l", h=NH, a=2)
    wd0a = t_mh.rearrange("p (c d) -> p c d", c=16)
    t_hal = sb("t_hal", [128, 8192], BF16)
    wmvv = t_hal.rearrange("p (c d) -> p c d", c=DC)
    amatv = t_hal[:, 0:512].rearrange("p (c l) -> p c l", c=2)
    keffv = t_hal[:, 512:2560].rearrange("p (c l) -> p c l", c=DC)
    t_y = sb("t_y", [128, NTC * DM], BF16)
    yv = t_y.rearrange("p (c d) -> p c d", c=NTC)
    t_masks = sb("t_masks", [128, 896], BF16)

    def maskap(m):
        return t_masks[:, (3 - m) * 128:(3 - m) * 128 + TB]
    t_veff = sb("t_veff", [128, LC * DM], BF16)
    veffv = t_veff.rearrange("p (c d) -> p c d", c=LC)

    def Hs(ic):
        if ic < 16:
            return t_hal[:, ic * GCH:(ic + 1) * GCH]
        if ic < 24:
            return t_y[:, (ic - 16) * GCH:(ic - 15) * GCH]
        if ic < 28:
            return t_veff[:, (ic - 24) * GCH:(ic - 23) * GCH]
        if ic < 30:
            return t_qlat[:, (ic - 28) * GCH:(ic - 27) * GCH]
        return t_E[:, (ic - 30) * GCH:(ic - 29) * GCH]

    t_qlat = sb("t_qlat", [128, LC * TB], BF16)
    qlatv = t_qlat.rearrange("p (c t) -> p c t", c=LC)
    mov = t_hal.rearrange("p (a d) -> p a d", a=8)
    t_E = sb("t_E", [128, 2 * TB], BF16)
    Ev = t_E.rearrange("p (a t) -> p a t", a=2)
    t_ctx = sb("t_ctx", [128, LC * TB], BF16)
    ctxv = t_ctx.rearrange("p (c t) -> p c t", c=LC)

    def Shb(par):
        return (t_E if par == 0 else t_ctx)[:, 0:DM]

    t_acc = sb("t_acc", [128, TB], F32)
    t_r1 = sb("t_r1", [128, DM], F32)
    t_x1b = sb("t_x1b", [128, DM], BF16)
    t_sq = sb("t_sq", [128, DM], F32)
    scov = t_sq[:, 0:NX * NE].rearrange("p (c e) -> p c e", e=NE)
    cmpS = t_sq[:, NX * NE:NX * NE + 256]
    t_st = sb("t_st", [128, 16], F32)
    t_ones = sb("t_ones", [128, 4], F32)
    t_dcol = sb("t_dcol", [128, NTC], F32)
    t_identb = sb("t_identb", [128, 128], BF16)
    t_identf = sb("t_identf", [128, 128], F32)
    t_zero = sb("t_zero", [128, 512], BF16)
    t_gw = sb("t_gw", [128, DC * NE], BF16)
    gwv = t_gw.rearrange("p (c e) -> p c e", c=DC)
    t_pack = sb("t_pack", [128, 4096], F32)
    valq = t_pack[0:16, 0:FQ]
    s1q = t_pack[0:16, 512:1024]
    s2q = t_pack[0:16, 1024:1024 + CAP2]
    slpq = t_pack[0:16, 1408:1408 + CAPQ]
    neg1 = t_pack[0:16, 1536:2048]
    sctq = t_pack[0:16, 2048:2176]
    scwv = t_pack[32:48, :].rearrange("q (e f) -> q e f", e=EPC)
    tabv = t_pack[64:80, :].rearrange("q (a f) -> q a f", a=2)
    t_scp = sb("t_scp", [128, EPC * 256], F32)
    scpv = t_scp.rearrange("p (e f) -> p e f", e=EPC)
    t_msk = sb("t_msk", [16, FQ], U8)
    t_msk1 = sb("t_msk1", [16, 4 * CAPQ], U8)
    t_lo = sb("t_lo", [128, EPC], F32)
    t_hi = sb("t_hi", [128, EPC], F32)
    t_mid = sb("t_mid", [128, EPC], F32)
    t_prt = sb("t_prt", [128, EPC], F32)
    t_tot = sb("t_tot", [128, EPC], F32)
    t_pru = sb("t_pru", [128, EPC], U8)
    t_npr = sb("t_npr", [128, EPC], U8)
    t_nf = sb("t_nf", [1, 16], U32)
    t_nff = sb("t_nff", [1, 4], F32)
    t_nfb = sb("t_nfb", [128, 4], F32)
    t_eidx = sb("t_eidx", [128, 8], I16)
    t_idx = sb("t_idx", [128, EPC * 2 * (CAPK // 16)], I16)
    idxv = t_idx.rearrange("p (e a f) -> p e a f", e=EPC, a=2)
    t_wcol = sb("t_wcol", [128, EPC * (CAPK // 128)], F32)
    wcolv = t_wcol.rearrange("p (e f) -> p e f", e=EPC)
    t_wg0 = sb("t_wg0", [128, 2 * DC * 128], BF16)
    wg0 = t_wg0.rearrange("p (a c i) -> p a c i", a=2, c=DC)
    wlv = t_wg0.rearrange("p (c l) -> p c l", c=DC)
    t_wu0 = sb("t_wu0", [128, 2 * DC * 128], BF16)
    wu0 = t_wu0.rearrange("p (a c i) -> p a c i", a=2, c=DC)
    wl2v = t_wu0.rearrange("p (c l) -> p c l", c=DC)
    wl2q = t_wu0.rearrange("p (c d) -> p c d", c=2)

    p_s0, p_s1 = ps("p_s0"), ps("p_s1")
    p_c0, p_c1 = ps("p_c0"), ps("p_c1")
    p_g0, p_g1 = ps("p_g0"), ps("p_g1")
    p_y0, p_y1 = ps("p_y0"), ps("p_y1")
    PB = {"ps0": p_s0, "ps1": p_s1, "pc0": p_c0, "pc1": p_c1,
          "pg0": p_g0, "pg1": p_g1, "py0": p_y0, "py1": p_y1}

    R = Rec(nc, ctx)
    TK = R.task

    def FENCE(keys):
        TK("vector", lambda e: None, writes=list(keys))

    accz = d_acc.rearrange("(n p a) d -> n p (a d)", p=128, a=1).rearrange("n p f -> n p f") if False else d_acc.rearrange("(n p) d -> n p d", p=64).rearrange("n p (a f) -> n (p a) f", a=2)
    NZERO = TALL * DM // (128 * 512)
    zq = [0]

    def emit_zero(n):
        for _ in range(n):
            i = zq[0]
            if i >= NZERO:
                return
            zq[0] += 1
            TK("scalar", (lambda i: lambda e: e.dma_start(out=accz[i, :, :], in_=t_zero[:, :]))(i),
               reads=["zero"], writes=["accD"], sem="qscal", inc=16)

    TK("sync", lambda e: e.dma_start(out=t_masks[:, :], in_=d_masks[:, :]),
       writes=["masks"], sem="qsync", inc=16)
    TK("sync", lambda e: e.dma_start(out=t_identf[:, :], in_=d_identf[:, :]),
       writes=["identf"], sem="qsync", inc=16)
    TK("sync", lambda e: e.dma_start(out=tabv[:, 0, :], in_=d_tabN[:, :]),
       writes=["tabN"], sem="qsync", inc=16)
    TK("sync", lambda e: e.dma_start(out=tabv[:, 1, :], in_=d_tabP[:, :]),
       writes=["tabP"], sem="qsync", inc=16)
    TK("sync", lambda e: e.dma_start(out=slpq, in_=d_slotpos[:, :]),
       writes=["slotpos"], sem="qsync", inc=16)
    TK("vector", lambda e: e.memset(t_ones[:, :], 1.0), writes=["ones"])
    TK("vector", lambda e: e.memset(neg1, -1.0), writes=["neg1"])
    TK("vector", lambda e: e.memset(t_zero[:, :], 0.0), writes=["zero"])
    TK("vector", lambda e: e.memset(t_st[:, 8:9], EPS), writes=["epsc"])
    TK("vector", lambda e: e.tensor_copy(out=t_identb[:, :], in_=t_identf[:, :]),
       reads=["identf"], writes=["identb"])

    for ch in range(NX):
        xsrc = t_r1 if ch % 2 == 0 else t_sq
        xkey = "r1" if ch % 2 == 0 else "sq"
        TK("sync", (lambda ch, xsrc: lambda e: e.dma_start(
            out=xsrc[:, :], in_=d_x[ch * 128:(ch + 1) * 128, :]))(ch, xsrc),
            writes=[xkey], sem="qsync", inc=16)
        pka, pkb = ("pg0", "pg1") if ch % 2 == 0 else ("py0", "py1")

        def mk_tr(ch, xsrc, pka, pkb):
            def f(e):
                pa, pb = PB[pka], PB[pkb]
                mm = None
                for dc in range(DC):
                    dst = (pa if dc < 4 else pb)[:, (dc % 4) * 128:(dc % 4) * 128 + 128]
                    mm = e.transpose(dst, xsrc[:, dc * 128:(dc + 1) * 128], t_identf[:, :])
                return mm
            return f
        TK("tensor", mk_tr(ch, xsrc, pka, pkb), reads=[xkey, "identf"],
           writes=[pka, pkb])

        def mk_ev(ch, pka, pkb):
            def f(e):
                e.tensor_copy(out=bigv[:, 0:4, ch * 128:(ch + 1) * 128],
                              in_=PB[pka][:, :].rearrange("p (c t) -> p c t", c=4))
                e.drain()
                return e.tensor_copy(out=bigv[:, 4:8, ch * 128:(ch + 1) * 128],
                                     in_=PB[pkb][:, :].rearrange("p (c t) -> p c t", c=4))
            return f
        TK("vector", mk_ev(ch, pka, pkb), reads=[pka, pkb], writes=[f"xT{ch}"])

    TK("gpsimd", lambda e: e.dma_start(out=wl2v[:, :, :], in_=d_wuv.rearrange("(c p) l -> p c l", p=128)),
       writes=["wl2"], sem="qgp", inc=16)
    TK("gpsimd", lambda e: e.dma_start(out=wmvv[:, :, :], in_=d_woT.rearrange("(c p) d -> p c d", p=128)),
       writes=["wmv"], sem="qgp", inc=16)
    for lc in range(LC):
        for dh in range(2):
            def mk_ve(lc, dh):
                def f(e):
                    mm = None
                    for dc in range(DC):
                        mm = e.matmul(p_c1[:, :], wl2v[:, dc, lc * 128:(lc + 1) * 128],
                                      wmvv[:, dc, dh * 512:(dh + 1) * 512],
                                      start=(dc == 0), stop=(dc == DC - 1))
                    return mm
                return f
            TK("tensor", mk_ve(lc, dh), reads=["wl2", "wmv"], writes=["pc1"])
            TK("vector", (lambda lc, dh: lambda e: e.tensor_copy(
                out=veffv[:, lc, dh * 512:(dh + 1) * 512], in_=p_c1[:, :]))(lc, dh),
                reads=["pc1"], writes=["veff"])

    TK("gpsimd", lambda e: e.dma_start(out=wlv[:, :, :], in_=d_wuq.rearrange("(c p) l -> p c l", p=128)),
       writes=["wl"], sem="qgp", inc=16)
    TK("gpsimd", lambda e: e.dma_start(out=wl2v[:, :, :], in_=d_wuk.rearrange("(c p) l -> p c l", p=128)),
       writes=["wl2"], sem="qgp", inc=16)

    def mm_amat(e):
        mm = None
        for qc in range(2):
            for dc in range(DC):
                mm = e.matmul(p_c0[:, qc * 256:(qc + 1) * 256],
                              wlv[:, dc, qc * 128:(qc + 1) * 128], wl2v[:, dc, :],
                              start=(dc == 0), stop=(dc == DC - 1))
        return mm
    TK("tensor", mm_amat, reads=["wl", "wl2"], writes=["pc0"])
    TK("vector", lambda e: e.tensor_copy(out=t_hal[:, 0:512], in_=p_c0[:, :]),
       reads=["pc0"], writes=["amat", "wmv"])

    TK("gpsimd", lambda e: e.dma_start(out=wl2q[:, :, :], in_=d_wdq.rearrange("(c p) d -> p c d", p=128)),
       writes=["wl2"], sem="qgp", inc=16)
    for dc in range(DC):
        def mk_keff(dc):
            def f(e):
                mm = None
                for qc in range(2):
                    mm = e.matmul(p_c1[:, (dc % 2) * 256:(dc % 2) * 256 + 256],
                                  wl2q[:, qc, dc * 128:(dc + 1) * 128], amatv[:, qc, :],
                                  start=(qc == 0), stop=(qc == 1))
                return mm
            return f
        TK("tensor", mk_keff(dc), reads=["wl2", "amat"], writes=["pc1"])
        TK("vector", (lambda dc: lambda e: e.tensor_copy(
            out=keffv[:, dc, :], in_=p_c1[:, (dc % 2) * 256:(dc % 2) * 256 + 256]))(dc),
            reads=["pc1"], writes=[f"keff{dc}", "wmv"])

    for h in range(NH):
        hc, hp = h // 2, (h % 2) * 64
        for qh in range(2):
            def mk_mh(hc, hp, qh):
                def f(e):
                    return e.matmul(p_c0[:, (qh % 2) * 256:(qh % 2) * 256 + 256],
                                    wlv[hp:hp + 64, hc, qh * 128:(qh + 1) * 128],
                                    keffv[hp:hp + 64, hc, :], start=True, stop=True)
                return f
            TK("tensor", mk_mh(hc, hp, qh), reads=["wl", f"keff{hc}"], writes=["pc0"])
            TK("vector", (lambda h, qh: lambda e: e.tensor_copy(
                out=mhv[:, h, qh, :], in_=p_c0[:, (qh % 2) * 256:(qh % 2) * 256 + 256]))(h, qh),
                reads=["pc0"], writes=["mh"])

    TK("gpsimd", lambda e: e.dma_start(out=wlv[:, :, :], in_=d_wdqT.rearrange("(c p) l -> p c l", p=128)),
       writes=["wl"], sem="qgp", inc=16)
    TK("gpsimd", lambda e: e.dma_start(out=wl2v[:, :, :], in_=d_wdkvT.rearrange("(c p) l -> p c l", p=128)),
       writes=["wl2"], sem="qgp", inc=16)
    TK("gpsimd", lambda e: e.dma_start(out=gwv[:, :, :], in_=d_gwT.rearrange("(c p) e -> p c e", p=128)),
       writes=["gw"], sem="qgp", inc=16)

    for name, wkey, wsrc, dst, dkey in (("cq", "wl", wlv, cqT, "cqT"),
                                        ("ckv", "wl2", wl2v, ckvT, "ckvT")):
        for qc in range(LC):
            for tbl in range(T // 512):
                pk = "pg0" if tbl % 2 == 0 else "pg1"

                def mk_cq(wsrc, qc, tbl, pk):
                    def f(e):
                        mm = None
                        for dc in range(DC):
                            mm = e.matmul(PB[pk][:, :],
                                          wsrc[:, dc, qc * 128:(qc + 1) * 128],
                                          bigv[:, dc, tbl * 512:(tbl + 1) * 512],
                                          start=(dc == 0), stop=(dc == DC - 1))
                        return mm
                    return f
                rd = [wkey] + [f"xT{ch}" for ch in range(tbl * 4, tbl * 4 + 4)]
                TK("tensor", mk_cq(wsrc, qc, tbl, pk), reads=rd, writes=[pk])
                TK("vector", (lambda dst, qc, tbl, pk: lambda e: e.tensor_copy(
                    out=dst[:, qc, tbl * 512:(tbl + 1) * 512], in_=PB[pk][:, :]))(dst, qc, tbl, pk),
                    reads=[pk], writes=[f"{dkey}{qc}_{tbl}"])

    for sc in range(NX):
        pk = "py0" if sc % 2 == 0 else "py1"

        def mk_tm(sc, pk):
            def f(e):
                mm = None
                for dc in range(DC):
                    mm = e.matmul(PB[pk][:, 0:LAT],
                                  bigv[:, dc, sc * 128:(sc + 1) * 128],
                                  wl2v[:, dc, :],
                                  start=(dc == 0), stop=(dc == DC - 1))
                return mm
            return f
        TK("tensor", mk_tm(sc, pk),
           reads=["wl2", f"xT{sc}"], writes=[pk])
        TK("vector", (lambda sc, pk: lambda e: e.tensor_copy(
            out=ckvtm[:, sc, :], in_=PB[pk][:, 0:LAT]))(sc, pk),
            reads=[pk], writes=[f"ckvtm{sc}"])

    for nm, src, dstd in (("wgB", d_wgT, d_wgB), ("wuB", d_wuT, d_wuB), ("wdB", d_wdT, d_wdB)):
        n0 = src.tensor.shape[1]
        for ee in range(EPC):
            for half in range(4):
                TK("gpsimd", (lambda src, dstd, ee, half, n0: lambda e: e.dma_start(
                    out=dstd[ee, half * (n0 // 4):(half + 1) * (n0 // 4), :],
                    in_=src[ee, half * (n0 // 4):(half + 1) * (n0 // 4), :],
                    max_dma_last_dim=8192))(src, dstd, ee, half, n0),
                    sem="qgp", inc=16, writes=[f"{nm}{ee}"])
    TK("gpsimd", lambda e: e.dma_start(out=d_sgB[:, :], in_=d_sgT[:, :], max_dma_last_dim=8192),
       sem="qgp", inc=16, writes=["sgB"])
    TK("gpsimd", lambda e: e.dma_start(out=d_suB[:, :], in_=d_suT[:, :], max_dma_last_dim=8192),
       sem="qgp", inc=16, writes=["suB"])
    TK("gpsimd", lambda e: e.dma_start(out=d_sdB[:, :], in_=d_sdT[:, :], max_dma_last_dim=8192),
       sem="qgp", inc=16, writes=["sdB"])

    INV8 = 1.0 / np.sqrt(DM / NH)
    for i in range(NTB):
        nsc = (i + 1) * NTC
        for h in range(NH):
            par = (i * NH + h) % 2
            qpk = "pg0" if par == 0 else "pg1"
            for lcc in range(LC):
                def mk_ql(i, h, lcc, qpk):
                    def f(e):
                        mm = None
                        for qh in range(2):
                            mm = e.matmul(PB[qpk][:, 0:TB],
                                          mhv[:, h, qh, lcc * 128:(lcc + 1) * 128],
                                          cqT[:, qh, i * TB:(i + 1) * TB],
                                          start=(qh == 0), stop=(qh == 1))
                        return mm
                    return f
                TK("tensor", mk_ql(i, h, lcc, qpk),
                   reads=["mh", f"cqT0_{i}", f"cqT1_{i}"], writes=[qpk])
                TK("vector", (lambda lcc, qpk: lambda e: e.tensor_copy(
                    out=qlatv[:, lcc, :], in_=PB[qpk][:, 0:TB]))(lcc, qpk),
                    reads=[qpk], writes=["qlat"])

            for sc in range(nsc):
                spk = "ps0" if sc % 2 == 0 else "ps1"
                epar = sc % 2

                def mk_sc(sc, spk):
                    def f(e):
                        mm = None
                        for lcc in range(LC):
                            mm = e.matmul(PB[spk][:, 0:TB],
                                          ckvT[:, lcc, sc * 128:(sc + 1) * 128],
                                          qlatv[:, lcc, :],
                                          start=(lcc == 0), stop=(lcc == LC - 1))
                        return mm
                    return f
                TK("tensor", mk_sc(sc, spk),
                   reads=["qlat", f"ckvT0_{sc // 4}", f"ckvT1_{sc // 4}"], writes=[spk])
                TK("scalar", (lambda spk, epar: lambda e: e.activation(
                    out=Ev[:, epar, :], in_=PB[spk][:, 0:TB], func=AF.Exp, scale=INV8))(spk, epar),
                    reads=[spk], writes=[f"E{epar}"])
                diag = sc >= 4 * i

                def mk_dv(i, sc, epar, diag, first):
                    def f(e):
                        if diag:
                            e.tensor_tensor(out=Ev[:, epar, :], in0=Ev[:, epar, :],
                                            in1=maskap(sc - 4 * i), op=ALU.mult)
                            e.drain()
                        if first:
                            return e.tensor_copy(out=t_acc[:, :], in_=Ev[:, epar, :])
                        return e.tensor_tensor(out=t_acc[:, :], in0=t_acc[:, :],
                                               in1=Ev[:, epar, :], op=ALU.add)
                    return f
                TK("vector", mk_dv(i, sc, epar, diag, sc == 0),
                   reads=["masks"], writes=[f"E{epar}", "accS"])

                def mk_cx(sc, epar, nsc):
                    def f(e):
                        mm = None
                        for lcc in range(LC):
                            mm = e.matmul((p_c0 if lcc == 0 else p_c1)[:, 0:TB],
                                          ckvtm[:, sc, lcc * 128:(lcc + 1) * 128],
                                          Ev[:, epar, :],
                                          start=(sc == 0), stop=(sc == nsc - 1))
                        return mm
                    return f
                TK("tensor", mk_cx(sc, epar, nsc),
                   reads=[f"E{epar}", f"ckvtm{sc}"], writes=["pc0", "pc1"])

            def mk_evcx(e):
                e.tensor_copy(out=ctxv[:, 0, :], in_=p_c0[:, 0:TB])
                e.drain()
                return e.tensor_copy(out=ctxv[:, 1, :], in_=p_c1[:, 0:TB])
            TK("vector", mk_evcx, reads=["pc0", "pc1"], writes=["ctx"])

            for tc in range(NTC):
                dpk = "py0" if tc % 2 == 0 else "py1"
                TK("tensor", (lambda tc, dpk: lambda e: e.matmul(
                    PB[dpk][:, 0:1], t_acc[:, tc * 128:(tc + 1) * 128], t_ones[:, 0:1],
                    start=True, stop=True))(tc, dpk),
                    reads=["accS", "ones"], writes=[dpk])
                TK("vector", (lambda tc, dpk: lambda e: e.reciprocal(
                    out=t_dcol[:, tc:tc + 1], in_=PB[dpk][:, 0:1]))(tc, dpk),
                    reads=[dpk], writes=[f"dcol{tc}"])
            for tc in range(NTC):
                ypk = "py0" if tc % 2 == 0 else "py1"

                def mk_y(h, tc, ypk):
                    def f(e):
                        mm = None
                        for lcc in range(LC):
                            mm = e.matmul(PB[ypk][:, 64:128],
                                          ctxv[:, lcc, tc * 128:(tc + 1) * 128],
                                          veffv[:, lcc, h * 64:(h + 1) * 64],
                                          start=(lcc == 0), stop=(lcc == 1))
                        return mm
                    return f
                TK("tensor", mk_y(h, tc, ypk), reads=["ctx", "veff"], writes=[ypk])
                TK("vector", (lambda h, tc, ypk: lambda e: e.tensor_scalar(
                    out=yv[:, tc, h * 64:(h + 1) * 64], in0=PB[ypk][:, 64:128],
                    scalar1=t_dcol[:, tc:tc + 1], scalar2=None, op0=ALU.mult))(h, tc, ypk),
                    reads=[ypk, f"dcol{tc}"], writes=[f"ybuf{tc}"])
            emit_zero(4)

        for tc in range(NTC):
            gtc = i * NTC + tc
            TK("sync", (lambda gtc: lambda e: e.dma_start(
                out=t_r1[:, :], in_=d_x[gtc * 128:(gtc + 1) * 128, :]))(gtc),
                writes=["r1"], sem="qsync", inc=16)

            def mk_lnA(tc):
                def f(e):
                    e.tensor_tensor(out=t_r1[:, :], in0=t_r1[:, :], in1=yv[:, tc, :], op=ALU.add)
                    e.drain()
                    e.reduce_sum(out=t_st[:, 0:1], in_=t_r1[:, :], axis=AX.X)
                    e.drain()
                    e.tensor_scalar(out=t_st[:, 1:2], in0=t_st[:, 0:1], scalar1=1.0 / DM,
                                    scalar2=None, op0=ALU.mult)
                    e.drain()
                    return e.tensor_scalar(out=t_sq[:, :], in0=t_r1[:, :], scalar1=t_st[:, 1:2],
                                           scalar2=None, op0=ALU.subtract)
                return f
            TK("vector", mk_lnA(tc), reads=[f"ybuf{tc}"], writes=["r1", "sq", "st"])

            def mk_lnB(e):
                e.activation(out=t_sq[:, :], in_=t_sq[:, :], func=AF.Square,
                             accum_out=t_st[:, 2:3])
                e.drain()
                e.activation(out=t_st[:, 3:4], in_=t_st[:, 2:3], func=AF.Ln,
                             scale=1.0 / DM, bias=t_st[:, 8:9])
                e.drain()
                return e.activation(out=t_st[:, 4:5], in_=t_st[:, 3:4], func=AF.Exp, scale=-0.5)
            TK("scalar", mk_lnB, reads=["sq", "epsc"], writes=["sq", "st"])

            def mk_lnC(e):
                e.tensor_scalar(out=t_r1[:, :], in0=t_r1[:, :], scalar1=t_st[:, 1:2],
                                scalar2=t_st[:, 4:5], op0=ALU.subtract, op1=ALU.mult)
                e.drain()
                return e.tensor_copy(out=t_x1b[:, :], in_=t_r1[:, :])
            TK("vector", mk_lnC, reads=["st"], writes=["r1", "x1b"])

            TK("sync", (lambda gtc: lambda e: e.dma_start(
                out=d_x1f[gtc * 128:(gtc + 1) * 128, :], in_=t_r1[:, :]))(gtc),
                reads=["r1"], writes=["x1fD"], sem="qsync", inc=16)
            if dbg == "x1":
                TK("sync", (lambda gtc: lambda e: e.dma_start(
                    out=d_out[gtc * 128:(gtc + 1) * 128, :], in_=t_r1[:, :]))(gtc),
                    reads=["r1"], writes=["outD"], sem="qsync", inc=16)
            TK("sync", (lambda gtc: lambda e: e.dma_start(
                out=d_agin[gtc * 128:(gtc + 1) * 128, :], in_=t_x1b[:, :]))(gtc),
                reads=["x1b"], writes=[f"aginD{i}"], sem="qsync", inc=16)

            pka, pkb = ("pg0", "pg1") if gtc % 2 == 0 else ("py0", "py1")

            def mk_tx1(pka, pkb):
                def f(e):
                    mm = None
                    for dc in range(DC):
                        dst = (PB[pka] if dc < 4 else PB[pkb])[:, (dc % 4) * 128:(dc % 4) * 128 + 128]
                        mm = e.transpose(dst, t_r1[:, dc * 128:(dc + 1) * 128], t_identf[:, :])
                    return mm
                return f
            TK("tensor", mk_tx1(pka, pkb), reads=["r1", "identf"], writes=[pka, pkb])

            def mk_evx1(gtc, pka, pkb):
                def f(e):
                    e.tensor_copy(out=bigv[:, 0:4, gtc * 128:(gtc + 1) * 128],
                                  in_=PB[pka][:, :].rearrange("p (c t) -> p c t", c=4))
                    e.drain()
                    return e.tensor_copy(out=bigv[:, 4:8, gtc * 128:(gtc + 1) * 128],
                                         in_=PB[pkb][:, :].rearrange("p (c t) -> p c t", c=4))
                return f
            TK("vector", mk_evx1(gtc, pka, pkb), reads=[pka, pkb], writes=[f"xT{gtc}"])

        TK("gpsimd", (lambda i: lambda e: e.collective_compute(
            "AllGather", ALU.bypass, replica_groups=RGRP,
            ins=[d_agin[i * TB:(i + 1) * TB, :]],
            outs=[d_x1all[i * B * TB:(i + 1) * B * TB, :]]))(i),
            reads=[f"aginD{i}"], writes=[f"x1allD{i}"], sem="cc", inc=1)

    if dbg == "x1":
        TK("sync", lambda e: None, reads=["outD", "x1fD"])
        block = ctx.enter_context(nc.Block())
        R.replay(block)
        ctx.close()
        nc.finalize()
        return nc

    FENCE(["sq", "r1"])
    for tch in range(NX):
        gpk = "pg0" if tch % 2 == 0 else "pg1"

        def mk_lg(tch, gpk):
            def f(e):
                mm = None
                for dc in range(DC):
                    mm = e.matmul(PB[gpk][:, 0:NE], bigv[:, dc, tch * 128:(tch + 1) * 128],
                                  gwv[:, dc, :], start=(dc == 0), stop=(dc == DC - 1))
                return mm
            return f
        TK("tensor", mk_lg(tch, gpk), reads=[f"xT{tch}", "gw"], writes=[gpk])
        TK("vector", (lambda gpk: lambda e: e.reduce_max(
            out=t_st[:, 5:6], in_=PB[gpk][:, 0:NE], axis=AX.X, negate=True))(gpk),
            reads=[gpk], writes=["st"])
        TK("scalar", (lambda tch, gpk: lambda e: e.activation(
            out=scov[:, tch, :], in_=PB[gpk][:, 0:NE], func=AF.Exp,
            bias=t_st[:, 5:6], accum_out=t_st[:, 6:7]))(tch, gpk),
            reads=[gpk, "st"], writes=[f"sco{tch}", "st"])

        def mk_sms(tch):
            def f(e):
                e.reciprocal(out=t_st[:, 7:8], in_=t_st[:, 6:7])
                e.drain()
                return e.tensor_scalar(out=scov[:, tch, :], in0=scov[:, tch, :],
                                       scalar1=t_st[:, 7:8], scalar2=None, op0=ALU.mult)
            return f
        TK("vector", mk_sms(tch), reads=["st"], writes=[f"sco{tch}", "st"])
        spk2 = "py0" if tch % 2 == 0 else "py1"
        TK("tensor", (lambda tch, spk2: lambda e: e.transpose(
            PB[spk2][0:16, 0:128], scov[:, tch, :], t_identf[:, :]))(tch, spk2),
            reads=[f"sco{tch}", "identf"], writes=[spk2])
        TK("vector", (lambda spk2: lambda e: e.tensor_copy(
            out=sctq, in_=PB[spk2][0:16, 0:128]))(spk2),
            reads=[spk2], writes=["sct"])
        TK("sync", (lambda tch: lambda e: e.dma_start(
            out=d_scT[:, tch * 128:(tch + 1) * 128], in_=sctq))(tch),
            reads=["sct"], writes=["scTD"], sem="qsync", inc=16)

    TK("gpsimd", lambda e: e.collective_compute(
        "AllGather", ALU.bypass, replica_groups=RGRP, ins=[d_scT[:, :]], outs=[d_scA[:, :]]),
        reads=["scTD"], writes=["scAD"], sem="cc", inc=1)
    TK("sync", lambda e: e.dma_start(out=t_eidx[:, :], in_=d_eidx[:, :]),
       writes=["eidx"], sem="qsync", inc=16)
    FENCE(["accS", "gsc"])
    TK("gpsimd", lambda e: e.dma_gather(
        t_acc.rearrange("p (a f) -> p a f", a=1),
        d_scA.rearrange("r (k f) -> (r k) f", k=8),
        t_eidx[:, :], 128, 128, 512, transpose=False, single_packet=False),
        reads=["scAD", "eidx"], writes=["gsc"], sem="qgp", inc=16)
    TK("sync", lambda e: e.dma_start(
        out=d_scB.rearrange("e b (k f) -> (e b k) f", k=8), in_=t_acc[:, :]),
        reads=["gsc"], writes=["scBD"], sem="qsync", inc=16)
    for e_ in range(EPC):
        for b_ in range(B):
            TK("sync", (lambda e_, b_: lambda e: e.dma_start(
                out=scwv[2 * b_:2 * b_ + 2, e_, :],
                in_=d_scB[e_, b_, :].rearrange("(q f) -> q f", q=2)))(e_, b_),
                reads=["scBD"], writes=["scw"], sem="qsync", inc=16)
            TK("sync", (lambda e_, b_: lambda e: e.dma_start(
                out=scpv[16 * b_:16 * b_ + 16, e_, :],
                in_=d_scB[e_, b_, :].rearrange("(q f) -> q f", q=16)))(e_, b_),
                reads=["scBD"], writes=["scp"], sem="qsync", inc=16)

    TK("vector", lambda e: e.memset(t_lo[:, :], 0.0), writes=["lo"])
    TK("vector", lambda e: e.memset(t_hi[:, :], 1.0), writes=["hi"])
    for it in range(BISN):
        def mk_mid(e):
            e.tensor_tensor(out=t_mid[:, :], in0=t_lo[:, :], in1=t_hi[:, :], op=ALU.add)
            e.drain()
            return e.tensor_scalar(out=t_mid[:, :], in0=t_mid[:, :], scalar1=0.5,
                                   scalar2=None, op0=ALU.mult)
        TK("vector", mk_mid, reads=["lo", "hi"], writes=["mid"])
        for e_ in range(EPC):
            TK("vector", (lambda e_: lambda e: e.tensor_scalar(
                out=cmpS, in0=scpv[:, e_, :], scalar1=t_mid[:, e_:e_ + 1],
                scalar2=None, op0=ALU.is_ge, op1=ALU.add, accum_out=t_prt[:, e_:e_ + 1]))(e_),
                reads=["scp", "mid"], writes=["cmpSk", "prt"])
        TK("gpsimd", lambda e: e.partition_all_reduce(
            out_ap=t_tot[:, :], in_ap=t_prt[:, :], channels=128, reduce_op=ReduceOp.add),
            reads=["prt"], writes=["tot"])

        def mk_sel(e):
            e.tensor_scalar(out=t_pru[:, :], in0=t_tot[:, :], scalar1=float(CAPK),
                            scalar2=None, op0=ALU.is_ge)
            e.drain()
            e.tensor_scalar(out=t_npr[:, :], in0=t_tot[:, :], scalar1=float(CAPK),
                            scalar2=None, op0=ALU.is_lt)
            e.drain()
            e.copy_predicated(out=t_lo[:, :], mask=t_pru[:, :], data=t_mid[:, :])
            e.drain()
            return e.copy_predicated(out=t_hi[:, :], mask=t_npr[:, :], data=t_mid[:, :])
        TK("vector", mk_sel, reads=["tot", "mid"], writes=["lo", "hi", "pru", "npr"])

    for e_ in range(EPC):
        for s_ in range(3):
            for q in range(4):
                def mk_val(e_, s_, q):
                    def f(e):
                        e.tensor_scalar(out=t_msk[:, :], in0=scwv[:, e_, q * FQ:(q + 1) * FQ],
                                        scalar1=t_lo[0:16, e_:e_ + 1],
                                        scalar2=None, op0=ALU.is_lt)
                        e.drain()
                        if s_ == 0:
                            e.tensor_copy(out=valq, in_=tabv[:, 0, q * FQ:(q + 1) * FQ])
                        elif s_ == 1:
                            e.tensor_copy(out=valq, in_=tabv[:, 1, q * FQ:(q + 1) * FQ])
                        else:
                            e.tensor_copy(out=valq, in_=scwv[:, e_, q * FQ:(q + 1) * FQ])
                        e.drain()
                        return e.copy_predicated(out=valq, mask=t_msk[:, :], data=neg1)
                    return f
                TK("vector", mk_val(e_, s_, q),
                   reads=["tabN", "tabP", "scw", "neg1", "lo"], writes=["val", "msk"])
                TK("gpsimd", (lambda q: lambda e: e.sparse_gather(
                    s1q[:, q * CAPQ:(q + 1) * CAPQ], valq,
                    num_found=t_nf[:, q:q + 1]))(q),
                    reads=["val"], writes=["s1", "nf"])
            if s_ == 0:
                TK("vector", lambda e: e.tensor_copy(out=t_nff[:, :], in_=t_nf[:, 0:4]),
                   reads=["nf"], writes=["nff"])
                TK("gpsimd", lambda e: e.partition_broadcast(out_ap=t_nfb[:, :], in_ap=t_nff[:, :]),
                   reads=["nff"], writes=["nfb"])
                for q in range(4):
                    TK("vector", (lambda q: lambda e: e.tensor_scalar(
                        out=t_msk1[:, q * CAPQ:(q + 1) * CAPQ], in0=slpq,
                        scalar1=t_nfb[0:16, q:q + 1], scalar2=None, op0=ALU.is_ge))(q),
                        reads=["nfb", "slotpos"], writes=["msk1"])
            for q in range(4):
                TK("vector", (lambda q: lambda e: e.copy_predicated(
                    out=s1q[:, q * CAPQ:(q + 1) * CAPQ],
                    mask=t_msk1[:, q * CAPQ:(q + 1) * CAPQ], data=neg1[:, 0:CAPQ]))(q),
                    reads=["msk1", "neg1"], writes=["s1"])
            TK("gpsimd", (lambda s_: lambda e: e.sparse_gather(
                s2q, s1q, num_found=t_nf[:, 12 + s_:13 + s_]))(s_),
                reads=["s1"], writes=["s2", "nf"])
            if s_ < 2:
                TK("vector", (lambda e_, s_: lambda e: e.tensor_copy(
                    out=idxv[0:16, e_, s_, :], in_=s2q[:, 0:CAPK // 16]))(e_, s_),
                    reads=["s2"], writes=["idx"])
            else:
                TK("sync", (lambda e_: lambda e: e.dma_start(
                    out=d_wflat[e_, :].rearrange("(f q) -> q f", q=16),
                    in_=s2q[:, 0:CAPK // 16],
                    allow_slow_non_contiguous=True))(e_),
                    reads=["s2"], writes=["wflatD"], sem="qsync", inc=16)
                TK("sync", (lambda e_: lambda e: e.dma_start(
                    out=wcolv[:, e_, :], in_=d_wflat[e_, :].rearrange("(q p) -> p q", p=128),
                    allow_slow_non_contiguous=True))(e_),
                    reads=["wflatD"], writes=["wcol"], sem="qsync", inc=16)
    for grp in range(1, 8):
        TK("sync", (lambda grp: lambda e: e.dma_start(
            out=t_idx[grp * 16:(grp + 1) * 16, :], in_=t_idx[0:16, :]))(grp),
            reads=["idx"], writes=["idxR"], sem="qsync", inc=16)

    FENCE(["wmv"] + [f"keff{d}" for d in range(DC)] + ["amat"]
          + [f"cqT{q}_{t}" for q in range(LC) for t in range(NTB)]
          + [f"ckvT{q}_{t}" for q in range(LC) for t in range(NTB)]
          + ["mh"] + [f"ckvtm{s}" for s in range(NX)]
          + [f"ybuf{q}" for q in range(4)] + ["masks", "veff", "wl", "wl2"]
          + [f"H{ic}" for ic in range(ICH)]
          + ["xg0", "xg1", "wdT0", "wdT1", "wgT0", "wgT1", "wuT0", "wuT1", "eoS"])
    allAG = [f"x1allD{i}" for i in range(NTB)]
    EOPK = ["ps0", "ps1", "pc0", "pc1"]

    def emit_gather(e_, sc8):
        par = (e_ * NGC + sc8) % 2
        TK("gpsimd", (lambda e_, sc8, par: lambda e: e.dma_gather(
            xgv[:, par, :, :], d_x1all[:, :], idxv[:, e_, 1, sc8 * 32:(sc8 + 1) * 32],
            GCH, GCH, DM, transpose=True, single_packet=False))(e_, sc8, par),
            reads=allAG + ["idx", "idxR"], writes=[f"xg{par}"], sem="qgp", inc=16)

    emit_gather(0, 0)
    for e_ in range(EPC):
        for sc8 in range(NGC):
            par = (e_ * NGC + sc8) % 2
            for ic in range(ICH):
                wpar = ic % 2
                TK("sync", (lambda e_, ic, wpar: lambda e: e.dma_start(
                    out=wg0[:, wpar, :, :],
                    in_=d_wgB[e_, :, ic * 128:(ic + 1) * 128].rearrange("(c p) i -> p c i", p=128)))(e_, ic, wpar),
                    reads=[f"wgB{e_}"], writes=[f"wgT{wpar}"], sem="qsync", inc=16)
                TK("sync", (lambda e_, ic, wpar: lambda e: e.dma_start(
                    out=wu0[:, wpar, :, :],
                    in_=d_wuB[e_, :, ic * 128:(ic + 1) * 128].rearrange("(c p) i -> p c i", p=128)))(e_, ic, wpar),
                    reads=[f"wuB{e_}"], writes=[f"wuT{wpar}"], sem="qsync", inc=16)

                def mk_g(par, wpar, wv, pk):
                    def f(e):
                        mm = None
                        for dc in range(DC):
                            mm = e.matmul(PB[pk][:, 0:GCH], wv[:, wpar, dc, :],
                                          xgv[:, par, dc, :],
                                          start=(dc == 0), stop=(dc == DC - 1))
                        return mm
                    return f
                TK("tensor", mk_g(par, wpar, wg0, "pg0"),
                   reads=[f"xg{par}", f"wgT{wpar}"], writes=["pg0"])
                TK("tensor", mk_g(par, wpar, wu0, "pg1"),
                   reads=[f"xg{par}", f"wuT{wpar}"], writes=["pg1"])
                TK("scalar", (lambda ic: lambda e: e.activation(
                    out=Hs(ic), in_=p_g0[:, 0:GCH], func=AF.Gelu))(ic),
                    reads=["pg0"], writes=[f"H{ic}"])
                TK("vector", (lambda ic: lambda e: e.tensor_tensor(
                    out=Hs(ic), in0=Hs(ic), in1=p_g1[:, 0:GCH], op=ALU.mult))(ic),
                    reads=["pg1"], writes=[f"H{ic}"])
            if sc8 + 1 < NGC:
                emit_gather(e_, sc8 + 1)
            elif e_ + 1 < EPC:
                emit_gather(e_ + 1, 0)
            for dh in range(2):
                for ich in range(2):
                    wdpar = ich
                    TK("sync", (lambda e_, dh, ich, wdpar: lambda e: e.dma_start(
                        out=(wd0a if wdpar == 0 else wd0b)[:, :, :],
                        in_=d_wdB[e_, ich * 2048:(ich + 1) * 2048, dh * 512:(dh + 1) * 512]
                        .rearrange("(c p) d -> p c d", p=128)))(e_, dh, ich, wdpar),
                        reads=[f"wdB{e_}"], writes=[f"wdT{wdpar}"], sem="qsync", inc=16)
                    for tc4 in range(NTC):
                        def mk_dn(ich, wdpar, tc4, pk):
                            def f(e):
                                wsrc = wd0a if wdpar == 0 else wd0b
                                mm = None
                                for k in range(16):
                                    mm = e.matmul(PB[pk][:, 0:512],
                                                  Hs(ich * 16 + k)[:, tc4 * 128:(tc4 + 1) * 128],
                                                  wsrc[:, k, :],
                                                  start=(ich == 0 and k == 0),
                                                  stop=(ich == 1 and k == 15))
                                return mm
                            return f
                        TK("tensor", mk_dn(ich, wdpar, tc4, EOPK[tc4]),
                           reads=[f"H{ich * 16 + k}" for k in range(16)] + [f"wdT{wdpar}"],
                           writes=[EOPK[tc4]])
                for tc4 in range(NTC):
                    TK("vector", (lambda e_, sc8, dh, tc4: lambda e: e.tensor_scalar(
                        out=eov[:, tc4, dh * 512:(dh + 1) * 512],
                        in0=PB[EOPK[tc4]][:, 0:512],
                        scalar1=wcolv[:, e_, sc8 * NTC + tc4:sc8 * NTC + tc4 + 1],
                        scalar2=None, op0=ALU.mult))(e_, sc8, dh, tc4),
                        reads=[EOPK[tc4], "wcol"], writes=["eoS"])
            TK("gpsimd", (lambda e_, sc8: lambda e: e.dma_scatter_add(
                d_acc[:, :], eov[:, :, :],
                idxv[:, e_, 0, sc8 * 32:(sc8 + 1) * 32],
                GCH, GCH, DM, single_packet=False))(e_, sc8),
                reads=["eoS", "idx", "idxR"], writes=["accD"], sem="qgp", inc=16)

    TK("gpsimd", lambda e: e.collective_compute(
        "ReduceScatter", ALU.add, replica_groups=RGRP, ins=[d_acc[:, :]], outs=[d_moe[:, :]]),
        reads=["accD"], writes=["moeD"], sem="cc", inc=1)

    FENCE(["eoS"] + [f"hs{k}" for k in range(SIC)] + [f"H{ic}" for ic in range(ICH)]
          + ["E0", "E1", "ctx", "shb0", "shb1", "qlat"])
    for i8 in range(NTB):
        for ic in range(SIC):
            wpar = ic % 2
            TK("sync", (lambda ic, wpar: lambda e: e.dma_start(
                out=wg0[:, wpar, :, :],
                in_=d_sgB[:, ic * 128:(ic + 1) * 128].rearrange("(c p) i -> p c i", p=128)))(ic, wpar),
                reads=["sgB"], writes=[f"wgT{wpar}"], sem="qsync", inc=16)
            TK("sync", (lambda ic, wpar: lambda e: e.dma_start(
                out=wu0[:, wpar, :, :],
                in_=d_suB[:, ic * 128:(ic + 1) * 128].rearrange("(c p) i -> p c i", p=128)))(ic, wpar),
                reads=["suB"], writes=[f"wuT{wpar}"], sem="qsync", inc=16)

            def mk_gs(i8, wpar, wv, pk):
                def f(e):
                    mm = None
                    for dc in range(DC):
                        mm = e.matmul(PB[pk][:, 0:TB], wv[:, wpar, dc, :],
                                      bigv[:, dc, i8 * TB:(i8 + 1) * TB],
                                      start=(dc == 0), stop=(dc == DC - 1))
                    return mm
                return f
            xTr = [f"xT{i8 * NTC + q}" for q in range(NTC)]
            TK("tensor", mk_gs(i8, wpar, wg0, "pg0"), reads=xTr + [f"wgT{wpar}"], writes=["pg0"])
            TK("tensor", mk_gs(i8, wpar, wu0, "pg1"), reads=xTr + [f"wuT{wpar}"], writes=["pg1"])
            TK("scalar", (lambda ic: lambda e: e.activation(
                out=hsl[:, ic, :], in_=p_g0[:, 0:TB], func=AF.Gelu))(ic),
                reads=["pg0"], writes=[f"hs{ic}"])
            TK("vector", (lambda ic: lambda e: e.tensor_tensor(
                out=hsl[:, ic, :], in0=hsl[:, ic, :], in1=p_g1[:, 0:TB], op=ALU.mult))(ic),
                reads=["pg1"], writes=[f"hs{ic}"])
        for tc4 in range(NTC):
            for dh in range(2):
                wdpar = (tc4 * 2 + dh) % 2
                spk = "py0" if wdpar == 0 else "py1"
                TK("sync", (lambda dh, wdpar: lambda e: e.dma_start(
                    out=(wd0a if wdpar == 0 else wd0b)[:, 0:SIC, :],
                    in_=d_sdB[:, dh * 512:(dh + 1) * 512].rearrange("(c p) d -> p c d", p=128)))(dh, wdpar),
                    reads=["sdB"], writes=[f"wdT{wdpar}"], sem="qsync", inc=16)

                def mk_ds(tc4, dh, wdpar, spk):
                    def f(e):
                        wsrc = wd0a if wdpar == 0 else wd0b
                        mm = None
                        for k in range(SIC):
                            mm = e.matmul(PB[spk][:, 0:512],
                                          hsl[:, k, tc4 * 128:(tc4 + 1) * 128],
                                          wsrc[:, k, :],
                                          start=(k == 0), stop=(k == SIC - 1))
                        return mm
                    return f
                TK("tensor", mk_ds(tc4, dh, wdpar, spk),
                   reads=[f"hs{k}" for k in range(SIC)] + [f"wdT{wdpar}"], writes=[spk])
                TK("vector", (lambda tc4, dh, spk: lambda e: e.tensor_copy(
                    out=Shb(tc4 % 2)[:, dh * 512:(dh + 1) * 512], in_=PB[spk][:, 0:512]))(tc4, dh, spk),
                    reads=[spk], writes=[f"shb{tc4 % 2}"])
            gtc = i8 * NTC + tc4
            TK("sync", (lambda gtc, tc4: lambda e: e.dma_start(
                out=d_sh[gtc * 128:(gtc + 1) * 128, :], in_=Shb(tc4 % 2)))(gtc, tc4),
                reads=[f"shb{tc4 % 2}"], writes=["shD"], sem="qsync", inc=16)

    FENCE([f"sco{q}" for q in range(NX)] + ["cmpSk", "sq", "r1", "mo0", "mo1",
                                            "shb0", "shb1", "H0", "H1", "H2", "H3"])
    for gtc in range(NX):
        par = gtc % 2
        TK("sync", (lambda gtc: lambda e: e.dma_start(
            out=t_r1[:, :], in_=d_x1f[gtc * 128:(gtc + 1) * 128, :]))(gtc),
            reads=["x1fD"], writes=["r1"], sem="qsync", inc=16)
        TK("sync", (lambda gtc, par: lambda e: e.dma_start(
            out=mov[:, par, :], in_=d_moe[gtc * 128:(gtc + 1) * 128, :]))(gtc, par),
            reads=["moeD"], writes=[f"mo{par}"], sem="qsync", inc=16)
        TK("sync", (lambda gtc, par: lambda e: e.dma_start(
            out=Shb(par), in_=d_sh[gtc * 128:(gtc + 1) * 128, :]))(gtc, par),
            reads=["shD"], writes=[f"shb{par}"], sem="qsync", inc=16)

        def mk_cmb(par):
            def f(e):
                e.tensor_tensor(out=t_r1[:, :], in0=t_r1[:, :], in1=mov[:, par, :], op=ALU.add)
                e.drain()
                e.tensor_tensor(out=t_r1[:, :], in0=t_r1[:, :], in1=Shb(par), op=ALU.add)
                e.drain()
                e.reduce_sum(out=t_st[:, 0:1], in_=t_r1[:, :], axis=AX.X)
                e.drain()
                e.tensor_scalar(out=t_st[:, 1:2], in0=t_st[:, 0:1], scalar1=1.0 / DM,
                                scalar2=None, op0=ALU.mult)
                e.drain()
                return e.tensor_scalar(out=t_sq[:, :], in0=t_r1[:, :], scalar1=t_st[:, 1:2],
                                       scalar2=None, op0=ALU.subtract)
            return f
        TK("vector", mk_cmb(par), reads=[f"mo{par}", f"shb{par}"],
           writes=["r1", "sq", "st"])

        def mk_ln2B(e):
            e.activation(out=t_sq[:, :], in_=t_sq[:, :], func=AF.Square,
                         accum_out=t_st[:, 2:3])
            e.drain()
            e.activation(out=t_st[:, 3:4], in_=t_st[:, 2:3], func=AF.Ln,
                         scale=1.0 / DM, bias=t_st[:, 8:9])
            e.drain()
            return e.activation(out=t_st[:, 4:5], in_=t_st[:, 3:4], func=AF.Exp, scale=-0.5)
        TK("scalar", mk_ln2B, reads=["sq", "epsc"], writes=["sq", "st"])
        TK("vector", lambda e: e.tensor_scalar(
            out=t_r1[:, :], in0=t_r1[:, :], scalar1=t_st[:, 1:2],
            scalar2=t_st[:, 4:5], op0=ALU.subtract, op1=ALU.mult),
            reads=["st"], writes=["r1"])
        TK("sync", (lambda gtc: lambda e: e.dma_start(
            out=d_out[gtc * 128:(gtc + 1) * 128, :], in_=t_r1[:, :]))(gtc),
            reads=["r1"], writes=["outD"], sem="qsync", inc=16)

    TK("sync", lambda e: None, reads=["outD"])

    block = ctx.enter_context(nc.Block())
    R.replay(block)
    ctx.close()
    nc.finalize()
    return nc


def _host_tables():
    p = np.arange(128)[:, None]
    v = np.arange(896)[None, :]
    masks = ((p + 384) <= v).astype(np.float32).astype(BF)
    identf = np.eye(128, dtype=np.float32)
    P, Fw = 16, TALL // 16
    t_of = (np.arange(P)[:, None] * Fw + np.arange(Fw)[None, :]).astype(np.int64)
    tabN = t_of.astype(np.float32)
    tt = t_of.ravel()
    c = tt // T
    ib = (tt % T) // TB
    r = tt % TB
    rows = ib * (B * TB) + c * TB + r
    tabP = rows.reshape(P, Fw).astype(np.float32)
    sp = np.zeros((16, CAPQ), np.float32)
    for j in range(16 * CAPQ):
        sp[j % 16, j // 16] = j
    return masks, identf, tabN, tabP, sp


def _eidx_table(c):
    j = np.arange(128)
    e_, b, k = j // 64, (j // 8) % 8, j % 8
    rows = ((b * NE + 2 * c + e_) * 8 + k).astype(np.int16)
    wrap = np.zeros((16, 8), np.int16)
    wrap[j % 16, j // 16] = rows
    return np.tile(wrap, (8, 1))


_NCCACHE = {}


def kernel(**inputs):
    x = np.ascontiguousarray(np.asarray(inputs["L0_attn_x"], dtype=np.float32))
    W_dq = np.asarray(inputs["L0_attn_W_dq"], np.float32)
    W_uq = np.asarray(inputs["L0_attn_W_uq"], np.float32)
    W_dkv = np.asarray(inputs["L0_attn_W_dkv"], np.float32)
    W_uk = np.asarray(inputs["L0_attn_W_uk"], np.float32)
    W_uv = np.asarray(inputs["L0_attn_W_uv"], np.float32)
    W_o = np.asarray(inputs["L0_attn_W_o"], np.float32)
    gate_w = np.asarray(inputs["L0_ffn_gate_w"], np.float32)
    Wg = np.asarray(inputs["L0_ffn_Wg"], np.float32)
    Wu = np.asarray(inputs["L0_ffn_Wu"], np.float32)
    Wd = np.asarray(inputs["L0_ffn_Wd"], np.float32)
    Sg = np.asarray(inputs["L0_ffn_Sg"], np.float32)
    Su = np.asarray(inputs["L0_ffn_Su"], np.float32)
    Sd = np.asarray(inputs["L0_ffn_Sd"], np.float32)

    masks, identf, tabN, tabP, slotpos = _host_tables()
    cc = np.ascontiguousarray
    shared_in = {
        "wdq": cc(W_dq), "wdqT": cc(W_dq.T), "wuq": cc(W_uq), "wuk": cc(W_uk),
        "wdkvT": cc(W_dkv.T), "wuv": cc(W_uv), "woT": cc(W_o.T),
        "sgT": cc(Sg.T), "suT": cc(Su.T), "sdT": cc(Sd.T), "gwT": cc(gate_w.T),
        "masks": masks, "identf": identf, "tabN": tabN, "tabP": tabP,
        "slotpos": slotpos,
    }
    in_maps = []
    for c in range(B):
        m = dict(shared_in)
        m["x"] = cc(x[c])
        m["wgT"] = cc(Wg[2 * c:2 * c + 2].transpose(0, 2, 1))
        m["wuT"] = cc(Wu[2 * c:2 * c + 2].transpose(0, 2, 1))
        m["wdT"] = cc(Wd[2 * c:2 * c + 2].transpose(0, 2, 1))
        m["eidx"] = _eidx_table(c)
        in_maps.append(m)

    if "full" not in _NCCACHE:
        _NCCACHE["full"] = build()
    nc = _NCCACHE["full"]
    res = run_bass_kernel_spmd(nc, in_maps, core_ids=list(range(B)))
    out = np.empty((B, T, DM), np.float32)
    for c in range(B):
        out[c] = res.results[c]["out"]
    return out


# version 02
# speedup vs baseline: 1.4432x; 1.4432x over previous
import numpy as np
import ml_dtypes
import concourse.mybir as mybir
from concourse.bacc import Bacc
from concourse.bass_utils import run_bass_kernel_spmd
from contextlib import ExitStack
from bass_rust import ReduceOp

F32 = mybir.dt.float32
BF16 = mybir.dt.bfloat16
I16 = mybir.dt.int16
U8 = mybir.dt.uint8
U32 = mybir.dt.uint32
AF = mybir.ActivationFunctionType
ALU = mybir.AluOpType
AX = mybir.AxisListType
BF = ml_dtypes.bfloat16

B = 8
DM = 1024
NH = 16
LAT = 256
NE = 16
EPC = 2
DC = DM // 128
LC = LAT // 128
EPS = 1e-5
T = 4096
TB = 512
NTB = T // TB
NTC = TB // 128
TALL = B * T
ISZ = 4096
SHZ = 2048
ICH = ISZ // 128
SIC = SHZ // 128
CAPK = 4096
GCH = 512
NGC = CAPK // GCH
CAPQ = 128
CAP2 = 320
BISN = 50
NX = T // 128
FQ = TALL // 16 // 4


class Rec:
    STREAMS = ("sync", "scalar", "vector", "tensor", "gpsimd")

    def __init__(self, nc, ctx):
        self.nc = nc
        self.sem = {}
        for s in ("sync", "scalar", "vector", "tensor", "gpsimd",
                  "qsync", "qscal", "qgp", "cc", "fence"):
            self.sem[s] = ctx.enter_context(nc.semaphore("sem_" + s))
        self.cnt = {k: 0 for k in self.sem}
        self.chain_stream = {}
        self.tasks = {s: [] for s in self.STREAMS}
        self.writer = {}
        self.readers = {}

    def task(self, stream, emit, reads=(), writes=(), sem=None, inc=1):
        sname = sem or stream
        deps = {}
        for r in reads:
            w = self.writer.get(r)
            if w:
                deps[w[0]] = max(deps.get(w[0], 0), w[1])
        for r in writes:
            w = self.writer.get(r)
            if w:
                deps[w[0]] = max(deps.get(w[0], 0), w[1])
            for rd in self.readers.get(r, ()):
                deps[rd[0]] = max(deps.get(rd[0], 0), rd[1])
        waits, drain = [], False
        for dsem, dval in deps.items():
            if dsem == stream:
                if stream != "tensor":
                    drain = True
            else:
                waits.append((dsem, dval))
        self.cnt[sname] += inc
        val = self.cnt[sname]
        chain = None
        if sname[0] == "q" or sname == "cc":
            prev = self.chain_stream.setdefault(sname, stream)
            assert prev == stream, (sname, prev, stream)
            if val - inc > 0:
                chain = (sname, val - inc)
        self.tasks[stream].append((waits, drain, emit, self.sem[sname], inc, chain))
        token = (sname, val)
        for r in writes:
            self.writer[r] = token
            self.readers[r] = []
        for r in reads:
            self.readers.setdefault(r, []).append(token)
        return token

    def replay(self, block):
        for stream in self.STREAMS:
            tasks = self.tasks[stream]
            if not tasks:
                continue

            def mk(tasks):
                def body(eng):
                    for waits, drain, emit, semh, inc, chain in tasks:
                        for dsem, dval in waits:
                            eng.wait_ge(self.sem[dsem], dval)
                        if chain is not None:
                            eng.wait_ge(self.sem[chain[0]], chain[1])
                        if drain:
                            eng.drain()
                        instr = emit(eng)
                        if instr is not None:
                            instr.then_inc(semh, inc)
                        else:
                            eng.nop().then_inc(semh, inc)
                return body
            getattr(block, stream)(mk(tasks))


def build(dbg=None):
    nc = Bacc("TRN2", target_bir_lowering=False, debug=False)
    ctx = ExitStack()
    sb = lambda name, shape, dt: ctx.enter_context(nc.sbuf_tensor(name, shape, dt))
    ps = lambda name: ctx.enter_context(nc.psum_tensor(name, [128, 512], F32))

    di = lambda n, s: nc.dram_tensor(n, s, F32, kind="ExternalInput").ap()
    d_x = di("x", [T, DM])
    d_wdq = di("wdq", [LAT, DM])
    d_wdqT = di("wdqT", [DM, LAT])
    d_wuq = di("wuq", [DM, LAT])
    d_wuk = di("wuk", [DM, LAT])
    d_wdkvT = di("wdkvT", [DM, LAT])
    d_wuv = di("wuv", [DM, LAT])
    d_woT = di("woT", [DM, DM])
    d_gwT = di("gwT", [DM, NE])
    d_wgT = di("wgT", [EPC, DM, ISZ])
    d_wuT = di("wuT", [EPC, DM, ISZ])
    d_wdT = di("wdT", [EPC, ISZ, DM])
    d_sgT = di("sgT", [DM, SHZ])
    d_suT = di("suT", [DM, SHZ])
    d_sdT = di("sdT", [SHZ, DM])
    d_masks = nc.dram_tensor("masks", [128, 896], BF16, kind="ExternalInput").ap()
    d_identf = di("identf", [128, 128])
    d_tabN = di("tabN", [16, TALL // 16])
    d_tabP = di("tabP", [16, TALL // 16])
    d_slotpos = di("slotpos", [16, CAPQ])

    d_out = nc.dram_tensor("out", [T, DM], F32, kind="ExternalOutput").ap()

    d_x1f = nc.dram_tensor("x1f", [T, DM], F32, kind="Internal").ap()
    d_agin = nc.dram_tensor("agin", [T, DM], BF16, kind="Internal").ap()
    d_x1all = nc.dram_tensor("x1all", [TALL, DM], BF16, kind="Internal").ap()
    d_scT = nc.dram_tensor("scT", [NE, T], F32, kind="Internal").ap()
    d_scA = nc.dram_tensor("scA", [B * NE, T], F32, kind="Internal").ap()
    d_scB = nc.dram_tensor("scB", [EPC, B, T], F32, kind="Internal").ap()
    d_eidx = nc.dram_tensor("eidx", [128, 8], I16, kind="ExternalInput").ap()
    d_acc = nc.dram_tensor("acc", [TALL, DM], BF16, kind="Internal").ap()
    d_moe = nc.dram_tensor("moe", [T, DM], BF16, kind="Internal").ap()
    d_sh = nc.dram_tensor("sh", [T, DM], BF16, kind="Internal").ap()
    d_wgB = nc.dram_tensor("wgB", [EPC, DM, ISZ], BF16, kind="Internal").ap()
    d_wuB = nc.dram_tensor("wuB", [EPC, DM, ISZ], BF16, kind="Internal").ap()
    d_wdB = nc.dram_tensor("wdB", [EPC, ISZ, DM], BF16, kind="Internal").ap()
    d_sgB = nc.dram_tensor("sgB", [DM, SHZ], BF16, kind="Internal").ap()
    d_suB = nc.dram_tensor("suB", [DM, SHZ], BF16, kind="Internal").ap()
    d_sdB = nc.dram_tensor("sdB", [SHZ, DM], BF16, kind="Internal").ap()
    d_wflat = nc.dram_tensor("wflat", [EPC, CAPK], F32, kind="Internal").ap()

    RGRP = [list(range(B))]

    t_big = sb("t_big", [128, DC * T], BF16)
    bigv = t_big.rearrange("p (c t) -> p c t", c=DC)
    t_ckvT = sb("t_ckvT", [128, LC * T], BF16)
    ckvT = t_ckvT.rearrange("p (c s) -> p c s", c=LC)
    wd0b = t_ckvT.rearrange("p (c d) -> p c d", c=16)
    t_ckvtm = sb("t_ckvtm", [128, NX * LAT], BF16)
    ckvtm = t_ckvtm.rearrange("p (c l) -> p c l", l=LAT)
    eov = t_ckvtm[:, 0:NTC * DM].rearrange("p (c d) -> p c d", c=NTC)
    hsl = t_ckvtm.rearrange("p (c t) -> p c t", c=SIC)
    t_cqT = sb("t_cqT", [128, LC * T], BF16)
    cqT = t_cqT.rearrange("p (c t) -> p c t", c=LC)
    xgv = t_cqT.rearrange("p (a c t) -> p a c t", a=2, c=DC)
    t_mh = sb("t_mh", [128, NH * 2 * LAT], BF16)
    mhv = t_mh.rearrange("p (h a l) -> p h a l", h=NH, a=2)
    wd0a = t_mh.rearrange("p (c d) -> p c d", c=16)
    t_hal = sb("t_hal", [128, 8192], BF16)
    wmvv = t_hal.rearrange("p (c d) -> p c d", c=DC)
    amatv = t_hal[:, 0:512].rearrange("p (c l) -> p c l", c=2)
    keffv = t_hal[:, 512:2560].rearrange("p (c l) -> p c l", c=DC)
    t_y = sb("t_y", [128, NTC * DM], BF16)
    yv = t_y.rearrange("p (c d) -> p c d", c=NTC)
    t_masks = sb("t_masks", [128, 896], BF16)

    def maskap(m):
        return t_masks[:, (3 - m) * 128:(3 - m) * 128 + TB]
    t_veff = sb("t_veff", [128, LC * DM], BF16)
    veffv = t_veff.rearrange("p (c d) -> p c d", c=LC)

    def Hs(ic):
        if ic < 16:
            return t_hal[:, ic * GCH:(ic + 1) * GCH]
        if ic < 24:
            return t_y[:, (ic - 16) * GCH:(ic - 15) * GCH]
        if ic < 28:
            return t_veff[:, (ic - 24) * GCH:(ic - 23) * GCH]
        if ic < 30:
            return t_qlat[:, (ic - 28) * GCH:(ic - 27) * GCH]
        return t_E[:, (ic - 30) * GCH:(ic - 29) * GCH]

    t_qlat = sb("t_qlat", [128, LC * TB], BF16)
    qlatv = t_qlat.rearrange("p (c t) -> p c t", c=LC)
    mov = t_hal.rearrange("p (a d) -> p a d", a=8)
    t_E = sb("t_E", [128, 2 * TB], BF16)
    Ev = t_E.rearrange("p (a t) -> p a t", a=2)
    t_ctx = sb("t_ctx", [128, LC * TB], BF16)
    ctxv = t_ctx.rearrange("p (c t) -> p c t", c=LC)

    def Shb(par):
        return (t_E if par == 0 else t_ctx)[:, 0:DM]

    t_acc = sb("t_acc", [128, TB], F32)
    t_r1 = sb("t_r1", [128, DM], F32)
    t_x1b = sb("t_x1b", [128, DM], BF16)
    t_sq = sb("t_sq", [128, DM], F32)
    scov = t_sq[:, 0:NX * NE].rearrange("p (c e) -> p c e", e=NE)
    cmpS = t_sq[:, NX * NE:NX * NE + 256]
    t_st = sb("t_st", [128, 16], F32)
    t_ones = sb("t_ones", [128, 4], F32)
    t_dcol = sb("t_dcol", [128, 2 * NTC], F32)
    t_identb = sb("t_identb", [128, 128], BF16)
    t_identf = sb("t_identf", [128, 128], F32)
    t_zero = sb("t_zero", [128, 512], BF16)
    t_gw = sb("t_gw", [128, DC * NE], BF16)
    gwv = t_gw.rearrange("p (c e) -> p c e", c=DC)
    t_pack = sb("t_pack", [128, 4096], F32)
    valq = t_pack[0:16, 0:FQ]
    s1q = t_pack[0:16, 512:1024]
    s2q = t_pack[0:16, 1024:1024 + CAP2]
    slpq = t_pack[0:16, 1408:1408 + CAPQ]
    neg1 = t_pack[0:16, 1536:2048]
    sctq = t_pack[0:16, 2048:2176]
    scwv = t_pack[32:48, :].rearrange("q (e f) -> q e f", e=EPC)
    tabv = t_pack[64:80, :].rearrange("q (a f) -> q a f", a=2)
    t_scp = sb("t_scp", [128, EPC * 256], F32)
    scpv = t_scp.rearrange("p (e f) -> p e f", e=EPC)
    t_msk = sb("t_msk", [16, FQ], U8)
    t_msk1 = sb("t_msk1", [16, 4 * CAPQ], U8)
    t_lo = sb("t_lo", [128, EPC], F32)
    t_hi = sb("t_hi", [128, EPC], F32)
    t_mid = sb("t_mid", [128, EPC], F32)
    t_prt = sb("t_prt", [128, EPC], F32)
    t_tot = sb("t_tot", [128, EPC], F32)
    t_pru = sb("t_pru", [128, EPC], U8)
    t_npr = sb("t_npr", [128, EPC], U8)
    t_nf = sb("t_nf", [1, 16], U32)
    t_nff = sb("t_nff", [1, 4], F32)
    t_nfb = sb("t_nfb", [128, 4], F32)
    t_eidx = sb("t_eidx", [128, 8], I16)
    t_idx = sb("t_idx", [128, EPC * 2 * (CAPK // 16)], I16)
    idxv = t_idx.rearrange("p (e a f) -> p e a f", e=EPC, a=2)
    t_wcol = sb("t_wcol", [128, EPC * (CAPK // 128)], F32)
    wcolv = t_wcol.rearrange("p (e f) -> p e f", e=EPC)
    t_wg0 = sb("t_wg0", [128, 2 * DC * 128], BF16)
    wg0 = t_wg0.rearrange("p (a c i) -> p a c i", a=2, c=DC)
    wlv = t_wg0.rearrange("p (c l) -> p c l", c=DC)
    t_wu0 = sb("t_wu0", [128, 2 * DC * 128], BF16)
    wu0 = t_wu0.rearrange("p (a c i) -> p a c i", a=2, c=DC)
    wl2v = t_wu0.rearrange("p (c l) -> p c l", c=DC)
    wl2q = t_wu0.rearrange("p (c d) -> p c d", c=2)

    p_s0, p_s1 = ps("p_s0"), ps("p_s1")
    p_c0, p_c1 = ps("p_c0"), ps("p_c1")
    p_g0, p_g1 = ps("p_g0"), ps("p_g1")
    p_y0, p_y1 = ps("p_y0"), ps("p_y1")
    PB = {"ps0": p_s0, "ps1": p_s1, "pc0": p_c0, "pc1": p_c1,
          "pg0": p_g0, "pg1": p_g1, "py0": p_y0, "py1": p_y1}

    R = Rec(nc, ctx)
    TK = R.task

    def FENCE(keys):
        TK("vector", lambda e: None, writes=list(keys), sem="fence")

    accz = d_acc.rearrange("(n p a) d -> n p (a d)", p=128, a=1).rearrange("n p f -> n p f") if False else d_acc.rearrange("(n p) d -> n p d", p=64).rearrange("n p (a f) -> n (p a) f", a=2)
    NZERO = TALL * DM // (128 * 512)
    zq = [0]

    def emit_zero(n):
        for _ in range(n):
            i = zq[0]
            if i >= NZERO:
                return
            zq[0] += 1
            TK("scalar", (lambda i: lambda e: e.dma_start(out=accz[i, :, :], in_=t_zero[:, :]))(i),
               reads=["zero"], writes=["accD"], sem="qscal", inc=16)

    TK("sync", lambda e: e.dma_start(out=t_masks[:, :], in_=d_masks[:, :]),
       writes=["masks"], sem="qsync", inc=16)
    TK("sync", lambda e: e.dma_start(out=t_identf[:, :], in_=d_identf[:, :]),
       writes=["identf"], sem="qsync", inc=16)
    TK("sync", lambda e: e.dma_start(out=tabv[:, 0, :], in_=d_tabN[:, :]),
       writes=["tabN"], sem="qsync", inc=16)
    TK("sync", lambda e: e.dma_start(out=tabv[:, 1, :], in_=d_tabP[:, :]),
       writes=["tabP"], sem="qsync", inc=16)
    TK("sync", lambda e: e.dma_start(out=slpq, in_=d_slotpos[:, :]),
       writes=["slotpos"], sem="qsync", inc=16)
    TK("vector", lambda e: e.memset(t_ones[:, :], 1.0), writes=["ones"])
    TK("vector", lambda e: e.memset(neg1, -1.0), writes=["neg1"])
    TK("vector", lambda e: e.memset(t_zero[:, :], 0.0), writes=["zero"])
    TK("vector", lambda e: e.memset(t_st[:, 8:9], EPS), writes=["epsc"])
    TK("vector", lambda e: e.tensor_copy(out=t_identb[:, :], in_=t_identf[:, :]),
       reads=["identf"], writes=["identb"])

    for ch in range(NX):
        xsrc = t_r1 if ch % 2 == 0 else t_sq
        xkey = "r1" if ch % 2 == 0 else "sq"
        TK("sync", (lambda ch, xsrc: lambda e: e.dma_start(
            out=xsrc[:, :], in_=d_x[ch * 128:(ch + 1) * 128, :]))(ch, xsrc),
            writes=[xkey], sem="qsync", inc=16)
        pka, pkb = ("pg0", "pg1") if ch % 2 == 0 else ("py0", "py1")

        def mk_tr(ch, xsrc, pka, pkb):
            def f(e):
                pa, pb = PB[pka], PB[pkb]
                mm = None
                for dc in range(DC):
                    dst = (pa if dc < 4 else pb)[:, (dc % 4) * 128:(dc % 4) * 128 + 128]
                    mm = e.transpose(dst, xsrc[:, dc * 128:(dc + 1) * 128], t_identf[:, :])
                return mm
            return f
        TK("tensor", mk_tr(ch, xsrc, pka, pkb), reads=[xkey, "identf"],
           writes=[pka, pkb])

        def mk_ev(ch, pka, pkb):
            def f(e):
                e.tensor_copy(out=bigv[:, 0:4, ch * 128:(ch + 1) * 128],
                              in_=PB[pka][:, :].rearrange("p (c t) -> p c t", c=4))
                e.drain()
                return e.tensor_copy(out=bigv[:, 4:8, ch * 128:(ch + 1) * 128],
                                     in_=PB[pkb][:, :].rearrange("p (c t) -> p c t", c=4))
            return f
        TK("vector", mk_ev(ch, pka, pkb), reads=[pka, pkb], writes=[f"xT{ch}"])

    TK("gpsimd", lambda e: e.dma_start(out=wl2v[:, :, :], in_=d_wuv.rearrange("(c p) l -> p c l", p=128)),
       writes=["wl2"], sem="qgp", inc=16)
    TK("gpsimd", lambda e: e.dma_start(out=wmvv[:, :, :], in_=d_woT.rearrange("(c p) d -> p c d", p=128)),
       writes=["wmv"], sem="qgp", inc=16)
    for lc in range(LC):
        for dh in range(2):
            def mk_ve(lc, dh):
                def f(e):
                    mm = None
                    for dc in range(DC):
                        mm = e.matmul(p_c1[:, :], wl2v[:, dc, lc * 128:(lc + 1) * 128],
                                      wmvv[:, dc, dh * 512:(dh + 1) * 512],
                                      start=(dc == 0), stop=(dc == DC - 1))
                    return mm
                return f
            TK("tensor", mk_ve(lc, dh), reads=["wl2", "wmv"], writes=["pc1"])
            TK("vector", (lambda lc, dh: lambda e: e.tensor_copy(
                out=veffv[:, lc, dh * 512:(dh + 1) * 512], in_=p_c1[:, :]))(lc, dh),
                reads=["pc1"], writes=["veff"])

    TK("gpsimd", lambda e: e.dma_start(out=wlv[:, :, :], in_=d_wuq.rearrange("(c p) l -> p c l", p=128)),
       writes=["wl"], sem="qgp", inc=16)
    TK("gpsimd", lambda e: e.dma_start(out=wl2v[:, :, :], in_=d_wuk.rearrange("(c p) l -> p c l", p=128)),
       writes=["wl2"], sem="qgp", inc=16)

    def mm_amat(e):
        mm = None
        for qc in range(2):
            for dc in range(DC):
                mm = e.matmul(p_c0[:, qc * 256:(qc + 1) * 256],
                              wlv[:, dc, qc * 128:(qc + 1) * 128], wl2v[:, dc, :],
                              start=(dc == 0), stop=(dc == DC - 1))
        return mm
    TK("tensor", mm_amat, reads=["wl", "wl2"], writes=["pc0"])
    TK("vector", lambda e: e.tensor_copy(out=t_hal[:, 0:512], in_=p_c0[:, :]),
       reads=["pc0"], writes=["amat", "wmv"])

    TK("gpsimd", lambda e: e.dma_start(out=wl2q[:, :, :], in_=d_wdq.rearrange("(c p) d -> p c d", p=128)),
       writes=["wl2"], sem="qgp", inc=16)
    for dc in range(DC):
        def mk_keff(dc):
            def f(e):
                mm = None
                for qc in range(2):
                    mm = e.matmul(p_c1[:, (dc % 2) * 256:(dc % 2) * 256 + 256],
                                  wl2q[:, qc, dc * 128:(dc + 1) * 128], amatv[:, qc, :],
                                  start=(qc == 0), stop=(qc == 1))
                return mm
            return f
        TK("tensor", mk_keff(dc), reads=["wl2", "amat"], writes=["pc1"])
        TK("vector", (lambda dc: lambda e: e.tensor_copy(
            out=keffv[:, dc, :], in_=p_c1[:, (dc % 2) * 256:(dc % 2) * 256 + 256]))(dc),
            reads=["pc1"], writes=[f"keff{dc}", "wmv"])

    for h in range(NH):
        hc, hp = h // 2, (h % 2) * 64
        for qh in range(2):
            def mk_mh(hc, hp, qh):
                def f(e):
                    return e.matmul(p_c0[:, (qh % 2) * 256:(qh % 2) * 256 + 256],
                                    wlv[hp:hp + 64, hc, qh * 128:(qh + 1) * 128],
                                    keffv[hp:hp + 64, hc, :], start=True, stop=True)
                return f
            TK("tensor", mk_mh(hc, hp, qh), reads=["wl", f"keff{hc}"], writes=["pc0"])
            TK("vector", (lambda h, qh: lambda e: e.tensor_copy(
                out=mhv[:, h, qh, :], in_=p_c0[:, (qh % 2) * 256:(qh % 2) * 256 + 256]))(h, qh),
                reads=["pc0"], writes=["mh"])

    TK("gpsimd", lambda e: e.dma_start(out=wlv[:, :, :], in_=d_wdqT.rearrange("(c p) l -> p c l", p=128)),
       writes=["wl"], sem="qgp", inc=16)
    TK("gpsimd", lambda e: e.dma_start(out=wl2v[:, :, :], in_=d_wdkvT.rearrange("(c p) l -> p c l", p=128)),
       writes=["wl2"], sem="qgp", inc=16)
    TK("gpsimd", lambda e: e.dma_start(out=gwv[:, :, :], in_=d_gwT.rearrange("(c p) e -> p c e", p=128)),
       writes=["gw"], sem="qgp", inc=16)

    for name, wkey, wsrc, dst, dkey in (("cq", "wl", wlv, cqT, "cqT"),
                                        ("ckv", "wl2", wl2v, ckvT, "ckvT")):
        for qc in range(LC):
            for tbl in range(T // 512):
                pk = "pg0" if tbl % 2 == 0 else "pg1"

                def mk_cq(wsrc, qc, tbl, pk):
                    def f(e):
                        mm = None
                        for dc in range(DC):
                            mm = e.matmul(PB[pk][:, :],
                                          wsrc[:, dc, qc * 128:(qc + 1) * 128],
                                          bigv[:, dc, tbl * 512:(tbl + 1) * 512],
                                          start=(dc == 0), stop=(dc == DC - 1))
                        return mm
                    return f
                rd = [wkey] + [f"xT{ch}" for ch in range(tbl * 4, tbl * 4 + 4)]
                TK("tensor", mk_cq(wsrc, qc, tbl, pk), reads=rd, writes=[pk])
                TK("vector", (lambda dst, qc, tbl, pk: lambda e: e.tensor_copy(
                    out=dst[:, qc, tbl * 512:(tbl + 1) * 512], in_=PB[pk][:, :]))(dst, qc, tbl, pk),
                    reads=[pk], writes=[f"{dkey}{qc}_{tbl}"])

    for sc in range(NX):
        pk = "py0" if sc % 2 == 0 else "py1"

        def mk_tm(sc, pk):
            def f(e):
                mm = None
                for dc in range(DC):
                    mm = e.matmul(PB[pk][:, 0:LAT],
                                  bigv[:, dc, sc * 128:(sc + 1) * 128],
                                  wl2v[:, dc, :],
                                  start=(dc == 0), stop=(dc == DC - 1))
                return mm
            return f
        TK("tensor", mk_tm(sc, pk),
           reads=["wl2", f"xT{sc}"], writes=[pk])
        TK("vector", (lambda sc, pk: lambda e: e.tensor_copy(
            out=ckvtm[:, sc, :], in_=PB[pk][:, 0:LAT]))(sc, pk),
            reads=[pk], writes=[f"ckvtm{sc}"])

    for nm, src, dstd in (("wgB", d_wgT, d_wgB), ("wuB", d_wuT, d_wuB), ("wdB", d_wdT, d_wdB)):
        n0 = src.tensor.shape[1]
        for ee in range(EPC):
            for half in range(4):
                TK("gpsimd", (lambda src, dstd, ee, half, n0: lambda e: e.dma_start(
                    out=dstd[ee, half * (n0 // 4):(half + 1) * (n0 // 4), :],
                    in_=src[ee, half * (n0 // 4):(half + 1) * (n0 // 4), :],
                    max_dma_last_dim=8192))(src, dstd, ee, half, n0),
                    sem="qgp", inc=16, writes=[f"{nm}{ee}"])
    TK("gpsimd", lambda e: e.dma_start(out=d_sgB[:, :], in_=d_sgT[:, :], max_dma_last_dim=8192),
       sem="qgp", inc=16, writes=["sgB"])
    TK("gpsimd", lambda e: e.dma_start(out=d_suB[:, :], in_=d_suT[:, :], max_dma_last_dim=8192),
       sem="qgp", inc=16, writes=["suB"])
    TK("gpsimd", lambda e: e.dma_start(out=d_sdB[:, :], in_=d_sdT[:, :], max_dma_last_dim=8192),
       sem="qgp", inc=16, writes=["sdB"])

    INV8 = 1.0 / np.sqrt(DM / NH)
    E2v = t_x1b.rearrange("p (a t) -> p a t", a=2)
    accBt = t_sq[:, 0:TB]
    XKEYS = ["x1b", "EB0", "EB1", "sq", "accB"]
    qlatB = ctxv

    def emit_exp(psk, dstv, par, key):
        TK("scalar", (lambda psk, par: lambda e: e.activation(
            out=dstv[:, par, :], in_=PB[psk][:, 0:TB], func=AF.Exp, scale=INV8))(psk, par),
            reads=[psk], writes=[f"{key}{par}"])

    def emit_mask_acc(i, sc, dstv, par, key, acct, akey):
        diag = sc >= 4 * i
        if diag:
            TK("vector", (lambda i, sc, par: lambda e: e.tensor_tensor(
                out=dstv[:, par, :], in0=dstv[:, par, :],
                in1=maskap(sc - 4 * i), op=ALU.mult))(i, sc, par),
                reads=["masks"], writes=[f"{key}{par}"])
        if sc == 0:
            TK("vector", (lambda par: lambda e: e.tensor_copy(
                out=acct[:, :], in_=dstv[:, par, :]))(par),
                reads=[f"{key}{par}"], writes=[akey])
        else:
            TK("vector", (lambda par: lambda e: e.tensor_tensor(
                out=acct[:, :], in0=acct[:, :], in1=dstv[:, par, :], op=ALU.add))(par),
                reads=[f"{key}{par}"], writes=[akey])

    for i in range(NTB):
        nsc = (i + 1) * NTC
        FENCE(XKEYS)
        for hp in range(NH // 2):
            hA, hB = 2 * hp, 2 * hp + 1
            FENCE(["ctx", "qlatB"])
            for h, banks, qdst, qkey in ((hA, ("ps0", "ps1"), qlatv, "qlat"),
                                         (hB, ("py0", "py1"), qlatB, "qlatB")):
                for lcc in range(LC):
                    qpk = banks[lcc]

                    def mk_ql(h, lcc, qpk):
                        def f(e):
                            mm = None
                            for qh in range(2):
                                mm = e.matmul(PB[qpk][:, 0:TB],
                                              mhv[:, h, qh, lcc * 128:(lcc + 1) * 128],
                                              cqT[:, qh, i * TB:(i + 1) * TB],
                                              start=(qh == 0), stop=(qh == 1))
                            return mm
                        return f
                    TK("tensor", mk_ql(h, lcc, qpk),
                       reads=["mh", f"cqT0_{i}", f"cqT1_{i}"], writes=[qpk])
                    TK("vector", (lambda lcc, qpk, qdst: lambda e: e.tensor_copy(
                        out=qdst[:, lcc, :], in_=PB[qpk][:, 0:TB]))(lcc, qpk, qdst),
                        reads=[qpk], writes=[qkey])

            def mk_sc(sc, psA, psB):
                def f(e):
                    mm = None
                    for lcc in range(LC):
                        st = ckvT[:, lcc, sc * 128:(sc + 1) * 128]
                        mm = e.matmul(PB[psA][:, 0:TB], st, qlatv[:, lcc, :],
                                      start=(lcc == 0), stop=(lcc == 1))
                        mm = e.matmul(PB[psB][:, 0:TB], st, qlatB[:, lcc, :],
                                      start=(lcc == 0), stop=(lcc == 1))
                    return mm
                return f

            def mk_cx(sc, par, nsc):
                def f(e):
                    mm = None
                    for lcc in range(LC):
                        st = ckvtm[:, sc, lcc * 128:(lcc + 1) * 128]
                        pA = (p_c0 if lcc == 0 else p_c1)
                        pB = PB["pg0" if lcc == 0 else "pg1"]
                        mm = e.matmul(pA[:, 0:TB], st, Ev[:, par, :],
                                      start=(sc == 0), stop=(sc == nsc - 1))
                        mm = e.matmul(pB[:, 0:TB], st, E2v[:, par, :],
                                      start=(sc == 0), stop=(sc == nsc - 1))
                    return mm
                return f

            def emit_scores(sc):
                par = sc % 2
                psA = "ps0" if par == 0 else "ps1"
                psB = "py0" if par == 0 else "py1"
                TK("tensor", mk_sc(sc, psA, psB),
                   reads=["qlat", "qlatB", f"ckvT0_{sc // 4}", f"ckvT1_{sc // 4}"],
                   writes=[psA, psB])
                emit_exp(psA, Ev, par, "EA")
                emit_exp(psB, E2v, par, "EB")
                emit_mask_acc(i, sc, Ev, par, "EA", t_acc, "accS")
                emit_mask_acc(i, sc, E2v, par, "EB", accBt, "accB")

            def emit_ctx(sc):
                par = sc % 2
                TK("tensor", mk_cx(sc, par, nsc),
                   reads=[f"EA{par}", f"EB{par}", f"ckvtm{sc}"],
                   writes=["pc0", "pc1", "pg0", "pg1"])

            for sc in range(nsc):
                emit_scores(sc)
                if sc >= 1:
                    emit_ctx(sc - 1)
            emit_ctx(nsc - 1)

            for hh, acct, akey, coff in ((hA, t_acc, "accS", 0), (hB, accBt, "accB", NTC)):
                for tc in range(NTC):
                    dpk = "py0" if tc % 2 == 0 else "py1"
                    TK("tensor", (lambda tc, dpk, acct: lambda e: e.matmul(
                        PB[dpk][:, 0:1], acct[:, tc * 128:(tc + 1) * 128], t_ones[:, 0:1],
                        start=True, stop=True))(tc, dpk, acct),
                        reads=[akey, "ones"], writes=[dpk])
                    TK("vector", (lambda tc, dpk, coff: lambda e: e.reciprocal(
                        out=t_dcol[:, coff + tc:coff + tc + 1], in_=PB[dpk][:, 0:1]))(tc, dpk, coff),
                        reads=[dpk], writes=[f"dcol{coff + tc}"])

            FENCE(["qlatB", "ctx"])
            for hh, pk0, pk1, coff in ((hA, "pc0", "pc1", 0), (hB, "pg0", "pg1", NTC)):
                def mk_evcx(pk0, pk1):
                    def f(e):
                        e.tensor_copy(out=ctxv[:, 0, :], in_=PB[pk0][:, 0:TB])
                        e.drain()
                        return e.tensor_copy(out=ctxv[:, 1, :], in_=PB[pk1][:, 0:TB])
                    return f
                TK("vector", mk_evcx(pk0, pk1), reads=[pk0, pk1], writes=["ctx"])
                for tc in range(NTC):
                    ypk = "py0" if tc % 2 == 0 else "py1"

                    def mk_y(hh, tc, ypk):
                        def f(e):
                            mm = None
                            for lcc in range(LC):
                                mm = e.matmul(PB[ypk][:, 64:128],
                                              ctxv[:, lcc, tc * 128:(tc + 1) * 128],
                                              veffv[:, lcc, hh * 64:(hh + 1) * 64],
                                              start=(lcc == 0), stop=(lcc == 1))
                            return mm
                        return f
                    TK("tensor", mk_y(hh, tc, ypk), reads=["ctx", "veff"], writes=[ypk])
                    TK("vector", (lambda hh, tc, ypk, coff: lambda e: e.tensor_scalar(
                        out=yv[:, tc, hh * 64:(hh + 1) * 64], in0=PB[ypk][:, 64:128],
                        scalar1=t_dcol[:, coff + tc:coff + tc + 1], scalar2=None,
                        op0=ALU.mult))(hh, tc, ypk, coff),
                        reads=[ypk, f"dcol{coff + tc}"], writes=[f"ybuf{tc}"])
            emit_zero(8)

        FENCE(XKEYS)
        for tc in range(NTC):
            gtc = i * NTC + tc
            TK("sync", (lambda gtc: lambda e: e.dma_start(
                out=t_r1[:, :], in_=d_x[gtc * 128:(gtc + 1) * 128, :]))(gtc),
                writes=["r1"], sem="qsync", inc=16)

            def mk_lnA(tc):
                def f(e):
                    e.tensor_tensor(out=t_r1[:, :], in0=t_r1[:, :], in1=yv[:, tc, :], op=ALU.add)
                    e.drain()
                    e.reduce_sum(out=t_st[:, 0:1], in_=t_r1[:, :], axis=AX.X)
                    e.drain()
                    e.tensor_scalar(out=t_st[:, 1:2], in0=t_st[:, 0:1], scalar1=1.0 / DM,
                                    scalar2=None, op0=ALU.mult)
                    e.drain()
                    return e.tensor_scalar(out=t_sq[:, :], in0=t_r1[:, :], scalar1=t_st[:, 1:2],
                                           scalar2=None, op0=ALU.subtract)
                return f
            TK("vector", mk_lnA(tc), reads=[f"ybuf{tc}"], writes=["r1", "sq", "st"])

            def mk_lnB(e):
                e.activation(out=t_sq[:, :], in_=t_sq[:, :], func=AF.Square,
                             accum_out=t_st[:, 2:3])
                e.drain()
                e.activation(out=t_st[:, 3:4], in_=t_st[:, 2:3], func=AF.Ln,
                             scale=1.0 / DM, bias=t_st[:, 8:9])
                e.drain()
                return e.activation(out=t_st[:, 4:5], in_=t_st[:, 3:4], func=AF.Exp, scale=-0.5)
            TK("scalar", mk_lnB, reads=["sq", "epsc"], writes=["sq", "st"])

            def mk_lnC(e):
                e.tensor_scalar(out=t_r1[:, :], in0=t_r1[:, :], scalar1=t_st[:, 1:2],
                                scalar2=t_st[:, 4:5], op0=ALU.subtract, op1=ALU.mult)
                e.drain()
                return e.tensor_copy(out=t_x1b[:, :], in_=t_r1[:, :])
            TK("vector", mk_lnC, reads=["st"], writes=["r1", "x1b"])

            TK("sync", (lambda gtc: lambda e: e.dma_start(
                out=d_x1f[gtc * 128:(gtc + 1) * 128, :], in_=t_r1[:, :]))(gtc),
                reads=["r1"], writes=["x1fD"], sem="qsync", inc=16)
            if dbg == "x1":
                TK("sync", (lambda gtc: lambda e: e.dma_start(
                    out=d_out[gtc * 128:(gtc + 1) * 128, :], in_=t_r1[:, :]))(gtc),
                    reads=["r1"], writes=["outD"], sem="qsync", inc=16)
            TK("sync", (lambda gtc: lambda e: e.dma_start(
                out=d_agin[gtc * 128:(gtc + 1) * 128, :], in_=t_x1b[:, :]))(gtc),
                reads=["x1b"], writes=[f"aginD{i}"], sem="qsync", inc=16)

            pka, pkb = ("pg0", "pg1") if gtc % 2 == 0 else ("py0", "py1")

            def mk_tx1(pka, pkb):
                def f(e):
                    mm = None
                    for dc in range(DC):
                        dst = (PB[pka] if dc < 4 else PB[pkb])[:, (dc % 4) * 128:(dc % 4) * 128 + 128]
                        mm = e.transpose(dst, t_r1[:, dc * 128:(dc + 1) * 128], t_identf[:, :])
                    return mm
                return f
            TK("tensor", mk_tx1(pka, pkb), reads=["r1", "identf"], writes=[pka, pkb])

            def mk_evx1(gtc, pka, pkb):
                def f(e):
                    e.tensor_copy(out=bigv[:, 0:4, gtc * 128:(gtc + 1) * 128],
                                  in_=PB[pka][:, :].rearrange("p (c t) -> p c t", c=4))
                    e.drain()
                    return e.tensor_copy(out=bigv[:, 4:8, gtc * 128:(gtc + 1) * 128],
                                         in_=PB[pkb][:, :].rearrange("p (c t) -> p c t", c=4))
                return f
            TK("vector", mk_evx1(gtc, pka, pkb), reads=[pka, pkb], writes=[f"xT{gtc}"])

        TK("gpsimd", (lambda i: lambda e: e.collective_compute(
            "AllGather", ALU.bypass, replica_groups=RGRP,
            ins=[d_agin[i * TB:(i + 1) * TB, :]],
            outs=[d_x1all[i * B * TB:(i + 1) * B * TB, :]]))(i),
            reads=[f"aginD{i}"], writes=[f"x1allD{i}"], sem="cc", inc=1)

    if dbg == "x1":
        TK("sync", lambda e: None, reads=["outD", "x1fD"])
        block = ctx.enter_context(nc.Block())
        R.replay(block)
        ctx.close()
        nc.finalize()
        return nc

    FENCE(["sq", "r1"])
    for tch in range(NX):
        gpk = "pg0" if tch % 2 == 0 else "pg1"

        def mk_lg(tch, gpk):
            def f(e):
                mm = None
                for dc in range(DC):
                    mm = e.matmul(PB[gpk][:, 0:NE], bigv[:, dc, tch * 128:(tch + 1) * 128],
                                  gwv[:, dc, :], start=(dc == 0), stop=(dc == DC - 1))
                return mm
            return f
        TK("tensor", mk_lg(tch, gpk), reads=[f"xT{tch}", "gw"], writes=[gpk])
        TK("vector", (lambda gpk: lambda e: e.reduce_max(
            out=t_st[:, 5:6], in_=PB[gpk][:, 0:NE], axis=AX.X, negate=True))(gpk),
            reads=[gpk], writes=["st"])
        TK("scalar", (lambda tch, gpk: lambda e: e.activation(
            out=scov[:, tch, :], in_=PB[gpk][:, 0:NE], func=AF.Exp,
            bias=t_st[:, 5:6], accum_out=t_st[:, 6:7]))(tch, gpk),
            reads=[gpk, "st"], writes=[f"sco{tch}", "st"])

        def mk_sms(tch):
            def f(e):
                e.reciprocal(out=t_st[:, 7:8], in_=t_st[:, 6:7])
                e.drain()
                return e.tensor_scalar(out=scov[:, tch, :], in0=scov[:, tch, :],
                                       scalar1=t_st[:, 7:8], scalar2=None, op0=ALU.mult)
            return f
        TK("vector", mk_sms(tch), reads=["st"], writes=[f"sco{tch}", "st"])
        spk2 = "py0" if tch % 2 == 0 else "py1"
        TK("tensor", (lambda tch, spk2: lambda e: e.transpose(
            PB[spk2][0:16, 0:128], scov[:, tch, :], t_identf[:, :]))(tch, spk2),
            reads=[f"sco{tch}", "identf"], writes=[spk2])
        TK("vector", (lambda spk2: lambda e: e.tensor_copy(
            out=sctq, in_=PB[spk2][0:16, 0:128]))(spk2),
            reads=[spk2], writes=["sct"])
        TK("sync", (lambda tch: lambda e: e.dma_start(
            out=d_scT[:, tch * 128:(tch + 1) * 128], in_=sctq))(tch),
            reads=["sct"], writes=["scTD"], sem="qsync", inc=16)

    TK("gpsimd", lambda e: e.collective_compute(
        "AllGather", ALU.bypass, replica_groups=RGRP, ins=[d_scT[:, :]], outs=[d_scA[:, :]]),
        reads=["scTD"], writes=["scAD"], sem="cc", inc=1)
    TK("sync", lambda e: e.dma_start(out=t_eidx[:, :], in_=d_eidx[:, :]),
       writes=["eidx"], sem="qsync", inc=16)
    FENCE(["accS", "gsc"])
    TK("gpsimd", lambda e: e.dma_gather(
        t_acc.rearrange("p (a f) -> p a f", a=1),
        d_scA.rearrange("r (k f) -> (r k) f", k=8),
        t_eidx[:, :], 128, 128, 512, transpose=False, single_packet=False),
        reads=["scAD", "eidx"], writes=["gsc"], sem="qgp", inc=16)
    TK("sync", lambda e: e.dma_start(
        out=d_scB.rearrange("e b (k f) -> (e b k) f", k=8), in_=t_acc[:, :]),
        reads=["gsc"], writes=["scBD"], sem="qsync", inc=16)
    for e_ in range(EPC):
        for b_ in range(B):
            TK("sync", (lambda e_, b_: lambda e: e.dma_start(
                out=scwv[2 * b_:2 * b_ + 2, e_, :],
                in_=d_scB[e_, b_, :].rearrange("(q f) -> q f", q=2)))(e_, b_),
                reads=["scBD"], writes=["scw"], sem="qsync", inc=16)
            TK("sync", (lambda e_, b_: lambda e: e.dma_start(
                out=scpv[16 * b_:16 * b_ + 16, e_, :],
                in_=d_scB[e_, b_, :].rearrange("(q f) -> q f", q=16)))(e_, b_),
                reads=["scBD"], writes=["scp"], sem="qsync", inc=16)

    TK("vector", lambda e: e.memset(t_lo[:, :], 0.0), writes=["lo"])
    TK("vector", lambda e: e.memset(t_hi[:, :], 1.0), writes=["hi"])
    for it in range(BISN):
        def mk_mid(e):
            e.tensor_tensor(out=t_mid[:, :], in0=t_lo[:, :], in1=t_hi[:, :], op=ALU.add)
            e.drain()
            return e.tensor_scalar(out=t_mid[:, :], in0=t_mid[:, :], scalar1=0.5,
                                   scalar2=None, op0=ALU.mult)
        TK("vector", mk_mid, reads=["lo", "hi"], writes=["mid"])
        for e_ in range(EPC):
            TK("vector", (lambda e_: lambda e: e.tensor_scalar(
                out=cmpS, in0=scpv[:, e_, :], scalar1=t_mid[:, e_:e_ + 1],
                scalar2=None, op0=ALU.is_ge, op1=ALU.add, accum_out=t_prt[:, e_:e_ + 1]))(e_),
                reads=["scp", "mid"], writes=["cmpSk", "prt"])
        TK("gpsimd", lambda e: e.partition_all_reduce(
            out_ap=t_tot[:, :], in_ap=t_prt[:, :], channels=128, reduce_op=ReduceOp.add),
            reads=["prt"], writes=["tot"])

        def mk_sel(e):
            e.tensor_scalar(out=t_pru[:, :], in0=t_tot[:, :], scalar1=float(CAPK),
                            scalar2=None, op0=ALU.is_ge)
            e.drain()
            e.tensor_scalar(out=t_npr[:, :], in0=t_tot[:, :], scalar1=float(CAPK),
                            scalar2=None, op0=ALU.is_lt)
            e.drain()
            e.copy_predicated(out=t_lo[:, :], mask=t_pru[:, :], data=t_mid[:, :])
            e.drain()
            return e.copy_predicated(out=t_hi[:, :], mask=t_npr[:, :], data=t_mid[:, :])
        TK("vector", mk_sel, reads=["tot", "mid"], writes=["lo", "hi", "pru", "npr"])

    for e_ in range(EPC):
        for s_ in range(3):
            for q in range(4):
                def mk_val(e_, s_, q):
                    def f(e):
                        e.tensor_scalar(out=t_msk[:, :], in0=scwv[:, e_, q * FQ:(q + 1) * FQ],
                                        scalar1=t_lo[0:16, e_:e_ + 1],
                                        scalar2=None, op0=ALU.is_lt)
                        e.drain()
                        if s_ == 0:
                            e.tensor_copy(out=valq, in_=tabv[:, 0, q * FQ:(q + 1) * FQ])
                        elif s_ == 1:
                            e.tensor_copy(out=valq, in_=tabv[:, 1, q * FQ:(q + 1) * FQ])
                        else:
                            e.tensor_copy(out=valq, in_=scwv[:, e_, q * FQ:(q + 1) * FQ])
                        e.drain()
                        return e.copy_predicated(out=valq, mask=t_msk[:, :], data=neg1)
                    return f
                TK("vector", mk_val(e_, s_, q),
                   reads=["tabN", "tabP", "scw", "neg1", "lo"], writes=["val", "msk"])
                TK("gpsimd", (lambda q: lambda e: e.sparse_gather(
                    s1q[:, q * CAPQ:(q + 1) * CAPQ], valq,
                    num_found=t_nf[:, q:q + 1]))(q),
                    reads=["val"], writes=["s1", "nf"])
            if s_ == 0:
                TK("vector", lambda e: e.tensor_copy(out=t_nff[:, :], in_=t_nf[:, 0:4]),
                   reads=["nf"], writes=["nff"])
                TK("gpsimd", lambda e: e.partition_broadcast(out_ap=t_nfb[:, :], in_ap=t_nff[:, :]),
                   reads=["nff"], writes=["nfb"])
                for q in range(4):
                    TK("vector", (lambda q: lambda e: e.tensor_scalar(
                        out=t_msk1[:, q * CAPQ:(q + 1) * CAPQ], in0=slpq,
                        scalar1=t_nfb[0:16, q:q + 1], scalar2=None, op0=ALU.is_ge))(q),
                        reads=["nfb", "slotpos"], writes=["msk1"])
            for q in range(4):
                TK("vector", (lambda q: lambda e: e.copy_predicated(
                    out=s1q[:, q * CAPQ:(q + 1) * CAPQ],
                    mask=t_msk1[:, q * CAPQ:(q + 1) * CAPQ], data=neg1[:, 0:CAPQ]))(q),
                    reads=["msk1", "neg1"], writes=["s1"])
            TK("gpsimd", (lambda s_: lambda e: e.sparse_gather(
                s2q, s1q, num_found=t_nf[:, 12 + s_:13 + s_]))(s_),
                reads=["s1"], writes=["s2", "nf"])
            if s_ < 2:
                TK("vector", (lambda e_, s_: lambda e: e.tensor_copy(
                    out=idxv[0:16, e_, s_, :], in_=s2q[:, 0:CAPK // 16]))(e_, s_),
                    reads=["s2"], writes=["idx"])
            else:
                TK("sync", (lambda e_: lambda e: e.dma_start(
                    out=d_wflat[e_, :].rearrange("(f q) -> q f", q=16),
                    in_=s2q[:, 0:CAPK // 16],
                    allow_slow_non_contiguous=True))(e_),
                    reads=["s2"], writes=["wflatD"], sem="qsync", inc=16)
                TK("sync", (lambda e_: lambda e: e.dma_start(
                    out=wcolv[:, e_, :], in_=d_wflat[e_, :].rearrange("(q p) -> p q", p=128),
                    allow_slow_non_contiguous=True))(e_),
                    reads=["wflatD"], writes=["wcol"], sem="qsync", inc=16)
    for grp in range(1, 8):
        TK("sync", (lambda grp: lambda e: e.dma_start(
            out=t_idx[grp * 16:(grp + 1) * 16, :], in_=t_idx[0:16, :]))(grp),
            reads=["idx"], writes=["idxR"], sem="qsync", inc=16)

    FENCE(["wmv"] + [f"keff{d}" for d in range(DC)] + ["amat"]
          + [f"cqT{q}_{t}" for q in range(LC) for t in range(NTB)]
          + [f"ckvT{q}_{t}" for q in range(LC) for t in range(NTB)]
          + ["mh"] + [f"ckvtm{s}" for s in range(NX)]
          + [f"ybuf{q}" for q in range(4)] + ["masks", "veff", "wl", "wl2"]
          + [f"H{ic}" for ic in range(ICH)]
          + ["xg0", "xg1", "wdT0", "wdT1", "wgT0", "wgT1", "wuT0", "wuT1", "eoS"]
          + ["qlat", "qlatB", "EA0", "EA1", "ctx"])
    allAG = [f"x1allD{i}" for i in range(NTB)]
    EOPK = ["ps0", "ps1", "pc0", "pc1"]

    def emit_gather(e_, sc8):
        par = (e_ * NGC + sc8) % 2
        TK("gpsimd", (lambda e_, sc8, par: lambda e: e.dma_gather(
            xgv[:, par, :, :], d_x1all[:, :], idxv[:, e_, 1, sc8 * 32:(sc8 + 1) * 32],
            GCH, GCH, DM, transpose=True, single_packet=False))(e_, sc8, par),
            reads=allAG + ["idx", "idxR"], writes=[f"xg{par}"], sem="qgp", inc=16)

    emit_gather(0, 0)
    for e_ in range(EPC):
        for sc8 in range(NGC):
            par = (e_ * NGC + sc8) % 2
            for ic in range(ICH):
                wpar = ic % 2
                TK("sync", (lambda e_, ic, wpar: lambda e: e.dma_start(
                    out=wg0[:, wpar, :, :],
                    in_=d_wgB[e_, :, ic * 128:(ic + 1) * 128].rearrange("(c p) i -> p c i", p=128)))(e_, ic, wpar),
                    reads=[f"wgB{e_}"], writes=[f"wgT{wpar}"], sem="qsync", inc=16)
                TK("sync", (lambda e_, ic, wpar: lambda e: e.dma_start(
                    out=wu0[:, wpar, :, :],
                    in_=d_wuB[e_, :, ic * 128:(ic + 1) * 128].rearrange("(c p) i -> p c i", p=128)))(e_, ic, wpar),
                    reads=[f"wuB{e_}"], writes=[f"wuT{wpar}"], sem="qsync", inc=16)

                def mk_g(par, wpar, wv, pk):
                    def f(e):
                        mm = None
                        for dc in range(DC):
                            mm = e.matmul(PB[pk][:, 0:GCH], wv[:, wpar, dc, :],
                                          xgv[:, par, dc, :],
                                          start=(dc == 0), stop=(dc == DC - 1))
                        return mm
                    return f
                TK("tensor", mk_g(par, wpar, wg0, "pg0"),
                   reads=[f"xg{par}", f"wgT{wpar}"], writes=["pg0"])
                TK("tensor", mk_g(par, wpar, wu0, "pg1"),
                   reads=[f"xg{par}", f"wuT{wpar}"], writes=["pg1"])
                TK("scalar", (lambda ic: lambda e: e.activation(
                    out=Hs(ic), in_=p_g0[:, 0:GCH], func=AF.Gelu))(ic),
                    reads=["pg0"], writes=[f"H{ic}"])
                TK("vector", (lambda ic: lambda e: e.tensor_tensor(
                    out=Hs(ic), in0=Hs(ic), in1=p_g1[:, 0:GCH], op=ALU.mult))(ic),
                    reads=["pg1"], writes=[f"H{ic}"])
            if sc8 + 1 < NGC:
                emit_gather(e_, sc8 + 1)
            elif e_ + 1 < EPC:
                emit_gather(e_ + 1, 0)
            for dh in range(2):
                for ich in range(2):
                    wdpar = ich
                    TK("sync", (lambda e_, dh, ich, wdpar: lambda e: e.dma_start(
                        out=(wd0a if wdpar == 0 else wd0b)[:, :, :],
                        in_=d_wdB[e_, ich * 2048:(ich + 1) * 2048, dh * 512:(dh + 1) * 512]
                        .rearrange("(c p) d -> p c d", p=128)))(e_, dh, ich, wdpar),
                        reads=[f"wdB{e_}"], writes=[f"wdT{wdpar}"], sem="qsync", inc=16)
                    for tc4 in range(NTC):
                        def mk_dn(ich, wdpar, tc4, pk):
                            def f(e):
                                wsrc = wd0a if wdpar == 0 else wd0b
                                mm = None
                                for k in range(16):
                                    mm = e.matmul(PB[pk][:, 0:512],
                                                  Hs(ich * 16 + k)[:, tc4 * 128:(tc4 + 1) * 128],
                                                  wsrc[:, k, :],
                                                  start=(ich == 0 and k == 0),
                                                  stop=(ich == 1 and k == 15))
                                return mm
                            return f
                        TK("tensor", mk_dn(ich, wdpar, tc4, EOPK[tc4]),
                           reads=[f"H{ich * 16 + k}" for k in range(16)] + [f"wdT{wdpar}"],
                           writes=[EOPK[tc4]])
                for tc4 in range(NTC):
                    TK("vector", (lambda e_, sc8, dh, tc4: lambda e: e.tensor_scalar(
                        out=eov[:, tc4, dh * 512:(dh + 1) * 512],
                        in0=PB[EOPK[tc4]][:, 0:512],
                        scalar1=wcolv[:, e_, sc8 * NTC + tc4:sc8 * NTC + tc4 + 1],
                        scalar2=None, op0=ALU.mult))(e_, sc8, dh, tc4),
                        reads=[EOPK[tc4], "wcol"], writes=["eoS"])
            TK("gpsimd", (lambda e_, sc8: lambda e: e.dma_scatter_add(
                d_acc[:, :], eov[:, :, :],
                idxv[:, e_, 0, sc8 * 32:(sc8 + 1) * 32],
                GCH, GCH, DM, single_packet=False))(e_, sc8),
                reads=["eoS", "idx", "idxR"], writes=["accD"], sem="qgp", inc=16)

    TK("gpsimd", lambda e: e.collective_compute(
        "ReduceScatter", ALU.add, replica_groups=RGRP, ins=[d_acc[:, :]], outs=[d_moe[:, :]]),
        reads=["accD"], writes=["moeD"], sem="cc", inc=1)

    FENCE(["eoS"] + [f"hs{k}" for k in range(SIC)] + [f"H{ic}" for ic in range(ICH)]
          + ["EA0", "EA1", "EB0", "EB1", "ctx", "qlatB", "shb0", "shb1", "qlat"])
    for i8 in range(NTB):
        for ic in range(SIC):
            wpar = ic % 2
            TK("sync", (lambda ic, wpar: lambda e: e.dma_start(
                out=wg0[:, wpar, :, :],
                in_=d_sgB[:, ic * 128:(ic + 1) * 128].rearrange("(c p) i -> p c i", p=128)))(ic, wpar),
                reads=["sgB"], writes=[f"wgT{wpar}"], sem="qsync", inc=16)
            TK("sync", (lambda ic, wpar: lambda e: e.dma_start(
                out=wu0[:, wpar, :, :],
                in_=d_suB[:, ic * 128:(ic + 1) * 128].rearrange("(c p) i -> p c i", p=128)))(ic, wpar),
                reads=["suB"], writes=[f"wuT{wpar}"], sem="qsync", inc=16)

            def mk_gs(i8, wpar, wv, pk):
                def f(e):
                    mm = None
                    for dc in range(DC):
                        mm = e.matmul(PB[pk][:, 0:TB], wv[:, wpar, dc, :],
                                      bigv[:, dc, i8 * TB:(i8 + 1) * TB],
                                      start=(dc == 0), stop=(dc == DC - 1))
                    return mm
                return f
            xTr = [f"xT{i8 * NTC + q}" for q in range(NTC)]
            TK("tensor", mk_gs(i8, wpar, wg0, "pg0"), reads=xTr + [f"wgT{wpar}"], writes=["pg0"])
            TK("tensor", mk_gs(i8, wpar, wu0, "pg1"), reads=xTr + [f"wuT{wpar}"], writes=["pg1"])
            TK("scalar", (lambda ic: lambda e: e.activation(
                out=hsl[:, ic, :], in_=p_g0[:, 0:TB], func=AF.Gelu))(ic),
                reads=["pg0"], writes=[f"hs{ic}"])
            TK("vector", (lambda ic: lambda e: e.tensor_tensor(
                out=hsl[:, ic, :], in0=hsl[:, ic, :], in1=p_g1[:, 0:TB], op=ALU.mult))(ic),
                reads=["pg1"], writes=[f"hs{ic}"])
        for tc4 in range(NTC):
            for dh in range(2):
                wdpar = (tc4 * 2 + dh) % 2
                spk = "py0" if wdpar == 0 else "py1"
                TK("sync", (lambda dh, wdpar: lambda e: e.dma_start(
                    out=(wd0a if wdpar == 0 else wd0b)[:, 0:SIC, :],
                    in_=d_sdB[:, dh * 512:(dh + 1) * 512].rearrange("(c p) d -> p c d", p=128)))(dh, wdpar),
                    reads=["sdB"], writes=[f"wdT{wdpar}"], sem="qsync", inc=16)

                def mk_ds(tc4, dh, wdpar, spk):
                    def f(e):
                        wsrc = wd0a if wdpar == 0 else wd0b
                        mm = None
                        for k in range(SIC):
                            mm = e.matmul(PB[spk][:, 0:512],
                                          hsl[:, k, tc4 * 128:(tc4 + 1) * 128],
                                          wsrc[:, k, :],
                                          start=(k == 0), stop=(k == SIC - 1))
                        return mm
                    return f
                TK("tensor", mk_ds(tc4, dh, wdpar, spk),
                   reads=[f"hs{k}" for k in range(SIC)] + [f"wdT{wdpar}"], writes=[spk])
                TK("vector", (lambda tc4, dh, spk: lambda e: e.tensor_copy(
                    out=Shb(tc4 % 2)[:, dh * 512:(dh + 1) * 512], in_=PB[spk][:, 0:512]))(tc4, dh, spk),
                    reads=[spk], writes=[f"shb{tc4 % 2}"])
            gtc = i8 * NTC + tc4
            TK("sync", (lambda gtc, tc4: lambda e: e.dma_start(
                out=d_sh[gtc * 128:(gtc + 1) * 128, :], in_=Shb(tc4 % 2)))(gtc, tc4),
                reads=[f"shb{tc4 % 2}"], writes=["shD"], sem="qsync", inc=16)

    FENCE([f"sco{q}" for q in range(NX)] + ["cmpSk", "sq", "r1", "mo0", "mo1",
                                            "shb0", "shb1", "H0", "H1", "H2", "H3"])
    for gtc in range(NX):
        par = gtc % 2
        TK("sync", (lambda gtc: lambda e: e.dma_start(
            out=t_r1[:, :], in_=d_x1f[gtc * 128:(gtc + 1) * 128, :]))(gtc),
            reads=["x1fD"], writes=["r1"], sem="qsync", inc=16)
        TK("sync", (lambda gtc, par: lambda e: e.dma_start(
            out=mov[:, par, :], in_=d_moe[gtc * 128:(gtc + 1) * 128, :]))(gtc, par),
            reads=["moeD"], writes=[f"mo{par}"], sem="qsync", inc=16)
        TK("sync", (lambda gtc, par: lambda e: e.dma_start(
            out=Shb(par), in_=d_sh[gtc * 128:(gtc + 1) * 128, :]))(gtc, par),
            reads=["shD"], writes=[f"shb{par}"], sem="qsync", inc=16)

        def mk_cmb(par):
            def f(e):
                e.tensor_tensor(out=t_r1[:, :], in0=t_r1[:, :], in1=mov[:, par, :], op=ALU.add)
                e.drain()
                e.tensor_tensor(out=t_r1[:, :], in0=t_r1[:, :], in1=Shb(par), op=ALU.add)
                e.drain()
                e.reduce_sum(out=t_st[:, 0:1], in_=t_r1[:, :], axis=AX.X)
                e.drain()
                e.tensor_scalar(out=t_st[:, 1:2], in0=t_st[:, 0:1], scalar1=1.0 / DM,
                                scalar2=None, op0=ALU.mult)
                e.drain()
                return e.tensor_scalar(out=t_sq[:, :], in0=t_r1[:, :], scalar1=t_st[:, 1:2],
                                       scalar2=None, op0=ALU.subtract)
            return f
        TK("vector", mk_cmb(par), reads=[f"mo{par}", f"shb{par}"],
           writes=["r1", "sq", "st"])

        def mk_ln2B(e):
            e.activation(out=t_sq[:, :], in_=t_sq[:, :], func=AF.Square,
                         accum_out=t_st[:, 2:3])
            e.drain()
            e.activation(out=t_st[:, 3:4], in_=t_st[:, 2:3], func=AF.Ln,
                         scale=1.0 / DM, bias=t_st[:, 8:9])
            e.drain()
            return e.activation(out=t_st[:, 4:5], in_=t_st[:, 3:4], func=AF.Exp, scale=-0.5)
        TK("scalar", mk_ln2B, reads=["sq", "epsc"], writes=["sq", "st"])
        TK("vector", lambda e: e.tensor_scalar(
            out=t_r1[:, :], in0=t_r1[:, :], scalar1=t_st[:, 1:2],
            scalar2=t_st[:, 4:5], op0=ALU.subtract, op1=ALU.mult),
            reads=["st"], writes=["r1"])
        TK("sync", (lambda gtc: lambda e: e.dma_start(
            out=d_out[gtc * 128:(gtc + 1) * 128, :], in_=t_r1[:, :]))(gtc),
            reads=["r1"], writes=["outD"], sem="qsync", inc=16)

    TK("sync", lambda e: None, reads=["outD"])

    block = ctx.enter_context(nc.Block())
    R.replay(block)
    ctx.close()
    nc.finalize()
    return nc


def _host_tables():
    p = np.arange(128)[:, None]
    v = np.arange(896)[None, :]
    masks = ((p + 384) <= v).astype(np.float32).astype(BF)
    identf = np.eye(128, dtype=np.float32)
    P, Fw = 16, TALL // 16
    t_of = (np.arange(P)[:, None] * Fw + np.arange(Fw)[None, :]).astype(np.int64)
    tabN = t_of.astype(np.float32)
    tt = t_of.ravel()
    c = tt // T
    ib = (tt % T) // TB
    r = tt % TB
    rows = ib * (B * TB) + c * TB + r
    tabP = rows.reshape(P, Fw).astype(np.float32)
    sp = np.zeros((16, CAPQ), np.float32)
    for j in range(16 * CAPQ):
        sp[j % 16, j // 16] = j
    return masks, identf, tabN, tabP, sp


def _eidx_table(c):
    j = np.arange(128)
    e_, b, k = j // 64, (j // 8) % 8, j % 8
    rows = ((b * NE + 2 * c + e_) * 8 + k).astype(np.int16)
    wrap = np.zeros((16, 8), np.int16)
    wrap[j % 16, j // 16] = rows
    return np.tile(wrap, (8, 1))


_NCCACHE = {}


def kernel(**inputs):
    x = np.ascontiguousarray(np.asarray(inputs["L0_attn_x"], dtype=np.float32))
    W_dq = np.asarray(inputs["L0_attn_W_dq"], np.float32)
    W_uq = np.asarray(inputs["L0_attn_W_uq"], np.float32)
    W_dkv = np.asarray(inputs["L0_attn_W_dkv"], np.float32)
    W_uk = np.asarray(inputs["L0_attn_W_uk"], np.float32)
    W_uv = np.asarray(inputs["L0_attn_W_uv"], np.float32)
    W_o = np.asarray(inputs["L0_attn_W_o"], np.float32)
    gate_w = np.asarray(inputs["L0_ffn_gate_w"], np.float32)
    Wg = np.asarray(inputs["L0_ffn_Wg"], np.float32)
    Wu = np.asarray(inputs["L0_ffn_Wu"], np.float32)
    Wd = np.asarray(inputs["L0_ffn_Wd"], np.float32)
    Sg = np.asarray(inputs["L0_ffn_Sg"], np.float32)
    Su = np.asarray(inputs["L0_ffn_Su"], np.float32)
    Sd = np.asarray(inputs["L0_ffn_Sd"], np.float32)

    masks, identf, tabN, tabP, slotpos = _host_tables()
    cc = np.ascontiguousarray
    shared_in = {
        "wdq": cc(W_dq), "wdqT": cc(W_dq.T), "wuq": cc(W_uq), "wuk": cc(W_uk),
        "wdkvT": cc(W_dkv.T), "wuv": cc(W_uv), "woT": cc(W_o.T),
        "sgT": cc(Sg.T), "suT": cc(Su.T), "sdT": cc(Sd.T), "gwT": cc(gate_w.T),
        "masks": masks, "identf": identf, "tabN": tabN, "tabP": tabP,
        "slotpos": slotpos,
    }
    in_maps = []
    for c in range(B):
        m = dict(shared_in)
        m["x"] = cc(x[c])
        m["wgT"] = cc(Wg[2 * c:2 * c + 2].transpose(0, 2, 1))
        m["wuT"] = cc(Wu[2 * c:2 * c + 2].transpose(0, 2, 1))
        m["wdT"] = cc(Wd[2 * c:2 * c + 2].transpose(0, 2, 1))
        m["eidx"] = _eidx_table(c)
        in_maps.append(m)

    if "full" not in _NCCACHE:
        _NCCACHE["full"] = build()
    nc = _NCCACHE["full"]
    res = run_bass_kernel_spmd(nc, in_maps, core_ids=list(range(B)))
    out = np.empty((B, T, DM), np.float32)
    for c in range(B):
        out[c] = res.results[c]["out"]
    return out


# version 03
# speedup vs baseline: 1.5401x; 1.0672x over previous
import numpy as np
import ml_dtypes
import concourse.mybir as mybir
from concourse.bacc import Bacc
from concourse.bass_utils import run_bass_kernel_spmd
from contextlib import ExitStack
from bass_rust import ReduceOp

F32 = mybir.dt.float32
BF16 = mybir.dt.bfloat16
I16 = mybir.dt.int16
U8 = mybir.dt.uint8
U32 = mybir.dt.uint32
AF = mybir.ActivationFunctionType
ALU = mybir.AluOpType
AX = mybir.AxisListType
BF = ml_dtypes.bfloat16

B = 8
DM = 1024
NH = 16
LAT = 256
NE = 16
EPC = 2
DC = DM // 128
LC = LAT // 128
EPS = 1e-5
T = 4096
TB = 512
NTB = T // TB
NTC = TB // 128
TALL = B * T
ISZ = 4096
SHZ = 2048
ICH = ISZ // 128
SIC = SHZ // 128
CAPK = 4096
GCH = 512
NGC = CAPK // GCH
CAPQ = 128
CAP2 = 320
BISN = 50
NX = T // 128
FQ = TALL // 16 // 4


class Rec:
    STREAMS = ("sync", "scalar", "vector", "tensor", "gpsimd")

    def __init__(self, nc, ctx):
        self.nc = nc
        self.sem = {}
        for s in ("sync", "scalar", "vector", "tensor", "gpsimd",
                  "qsync", "qscal", "qgp", "cc", "fence"):
            self.sem[s] = ctx.enter_context(nc.semaphore("sem_" + s))
        self.cnt = {k: 0 for k in self.sem}
        self.chain_stream = {}
        self.tasks = {s: [] for s in self.STREAMS}
        self.writer = {}
        self.readers = {}

    def task(self, stream, emit, reads=(), writes=(), sem=None, inc=1):
        sname = sem or stream
        deps = {}
        for r in reads:
            w = self.writer.get(r)
            if w:
                deps[w[0]] = max(deps.get(w[0], 0), w[1])
        for r in writes:
            w = self.writer.get(r)
            if w:
                deps[w[0]] = max(deps.get(w[0], 0), w[1])
            for rd in self.readers.get(r, ()):
                deps[rd[0]] = max(deps.get(rd[0], 0), rd[1])
        waits, drain = [], False
        for dsem, dval in deps.items():
            if dsem == stream:
                if stream != "tensor":
                    drain = True
            else:
                waits.append((dsem, dval))
        self.cnt[sname] += inc
        val = self.cnt[sname]
        chain = None
        if sname[0] == "q" or sname == "cc":
            prev = self.chain_stream.setdefault(sname, stream)
            assert prev == stream, (sname, prev, stream)
            if val - inc > 0:
                chain = (sname, val - inc)
        self.tasks[stream].append((waits, drain, emit, self.sem[sname], inc, chain))
        token = (sname, val)
        for r in writes:
            self.writer[r] = token
            self.readers[r] = []
        for r in reads:
            self.readers.setdefault(r, []).append(token)
        return token

    def replay(self, block):
        for stream in self.STREAMS:
            tasks = self.tasks[stream]
            if not tasks:
                continue

            def mk(tasks):
                def body(eng):
                    for waits, drain, emit, semh, inc, chain in tasks:
                        for dsem, dval in waits:
                            eng.wait_ge(self.sem[dsem], dval)
                        if chain is not None:
                            eng.wait_ge(self.sem[chain[0]], chain[1])
                        if drain:
                            eng.drain()
                        instr = emit(eng)
                        if instr is not None:
                            instr.then_inc(semh, inc)
                        else:
                            eng.nop().then_inc(semh, inc)
                return body
            getattr(block, stream)(mk(tasks))


def build(dbg=None):
    nc = Bacc("TRN2", target_bir_lowering=False, debug=False)
    ctx = ExitStack()
    sb = lambda name, shape, dt: ctx.enter_context(nc.sbuf_tensor(name, shape, dt))
    ps = lambda name: ctx.enter_context(nc.psum_tensor(name, [128, 512], F32))

    di = lambda n, s: nc.dram_tensor(n, s, F32, kind="ExternalInput").ap()
    d_x = di("x", [T, DM])
    d_wdq = di("wdq", [LAT, DM])
    d_wdqT = di("wdqT", [DM, LAT])
    d_wuq = di("wuq", [DM, LAT])
    d_wuk = di("wuk", [DM, LAT])
    d_wdkvT = di("wdkvT", [DM, LAT])
    d_wuv = di("wuv", [DM, LAT])
    d_woT = di("woT", [DM, DM])
    d_gwT = di("gwT", [DM, NE])
    d_wgT = di("wgT", [EPC, DM, ISZ])
    d_wuT = di("wuT", [EPC, DM, ISZ])
    d_wdT = di("wdT", [EPC, ISZ, DM])
    d_sgT = di("sgT", [DM, SHZ])
    d_suT = di("suT", [DM, SHZ])
    d_sdT = di("sdT", [SHZ, DM])
    d_masks = nc.dram_tensor("masks", [128, 896], BF16, kind="ExternalInput").ap()
    d_identf = di("identf", [128, 128])
    d_tabN = di("tabN", [16, TALL // 16])
    d_tabP = di("tabP", [16, TALL // 16])
    d_slotpos = di("slotpos", [16, CAPQ])

    d_out = nc.dram_tensor("out", [T, DM], F32, kind="ExternalOutput").ap()

    d_x1f = nc.dram_tensor("x1f", [T, DM], F32, kind="Internal").ap()
    d_agin = nc.dram_tensor("agin", [T, DM], BF16, kind="Internal").ap()
    d_x1all = nc.dram_tensor("x1all", [TALL, DM], BF16, kind="Internal").ap()
    d_scT = nc.dram_tensor("scT", [NE, T], F32, kind="Internal").ap()
    d_scA = nc.dram_tensor("scA", [B * NE, T], F32, kind="Internal").ap()
    d_scB = nc.dram_tensor("scB", [EPC, B, T], F32, kind="Internal").ap()
    d_eidx = nc.dram_tensor("eidx", [128, 8], I16, kind="ExternalInput").ap()
    d_acc = nc.dram_tensor("acc", [TALL, DM], BF16, kind="Internal").ap()
    d_moe = nc.dram_tensor("moe", [T, DM], BF16, kind="Internal").ap()
    d_sh = nc.dram_tensor("sh", [T, DM], BF16, kind="Internal").ap()
    d_wgB = nc.dram_tensor("wgB", [EPC, ISZ // 128, 128, DM], BF16, kind="Internal").ap()
    d_wuB = nc.dram_tensor("wuB", [EPC, ISZ // 128, 128, DM], BF16, kind="Internal").ap()
    d_wdB = nc.dram_tensor("wdB", [EPC, ISZ, DM], BF16, kind="Internal").ap()
    d_sgB = nc.dram_tensor("sgB", [SHZ // 128, 128, DM], BF16, kind="Internal").ap()
    d_suB = nc.dram_tensor("suB", [SHZ // 128, 128, DM], BF16, kind="Internal").ap()
    d_sdB = nc.dram_tensor("sdB", [SHZ, DM], BF16, kind="Internal").ap()
    d_wflat = nc.dram_tensor("wflat", [EPC, CAPK], F32, kind="Internal").ap()

    RGRP = [list(range(B))]

    t_big = sb("t_big", [128, DC * T], BF16)
    bigv = t_big.rearrange("p (c t) -> p c t", c=DC)
    t_ckvT = sb("t_ckvT", [128, LC * T], BF16)
    ckvT = t_ckvT.rearrange("p (c s) -> p c s", c=LC)
    wd0b = t_ckvT.rearrange("p (c d) -> p c d", c=16)
    t_ckvtm = sb("t_ckvtm", [128, NX * LAT], BF16)
    ckvtm = t_ckvtm.rearrange("p (c l) -> p c l", l=LAT)
    eov = t_ckvtm[:, 0:NTC * DM].rearrange("p (c d) -> p c d", c=NTC)
    hsl = t_ckvtm.rearrange("p (c t) -> p c t", c=SIC)
    t_cqT = sb("t_cqT", [128, LC * T], BF16)
    cqT = t_cqT.rearrange("p (c t) -> p c t", c=LC)
    xgv = t_cqT.rearrange("p (a c t) -> p a c t", a=2, c=DC)
    t_mh = sb("t_mh", [128, NH * 2 * LAT], BF16)
    mhv = t_mh.rearrange("p (h a l) -> p h a l", h=NH, a=2)
    wd0a = t_mh.rearrange("p (c d) -> p c d", c=16)
    t_hal = sb("t_hal", [128, 8192], BF16)
    wmvv = t_hal.rearrange("p (c d) -> p c d", c=DC)
    amatv = t_hal[:, 0:512].rearrange("p (c l) -> p c l", c=2)
    keffv = t_hal[:, 512:2560].rearrange("p (c l) -> p c l", c=DC)
    t_y = sb("t_y", [128, NTC * DM], BF16)
    yv = t_y.rearrange("p (c d) -> p c d", c=NTC)
    t_masks = sb("t_masks", [128, 896], BF16)

    def maskap(m):
        return t_masks[:, (3 - m) * 128:(3 - m) * 128 + TB]
    t_veff = sb("t_veff", [128, LC * DM], BF16)
    veffv = t_veff.rearrange("p (c d) -> p c d", c=LC)

    def Hs(ic):
        if ic < 16:
            return t_hal[:, ic * GCH:(ic + 1) * GCH]
        if ic < 24:
            return t_y[:, (ic - 16) * GCH:(ic - 15) * GCH]
        if ic < 28:
            return t_veff[:, (ic - 24) * GCH:(ic - 23) * GCH]
        if ic < 30:
            return t_qlat[:, (ic - 28) * GCH:(ic - 27) * GCH]
        return t_E[:, (ic - 30) * GCH:(ic - 29) * GCH]

    t_qlat = sb("t_qlat", [128, LC * TB], BF16)
    qlatv = t_qlat.rearrange("p (c t) -> p c t", c=LC)
    mov = t_hal.rearrange("p (a d) -> p a d", a=8)
    t_E = sb("t_E", [128, 2 * TB], BF16)
    Ev = t_E.rearrange("p (a t) -> p a t", a=2)
    t_ctx = sb("t_ctx", [128, LC * TB], BF16)
    ctxv = t_ctx.rearrange("p (c t) -> p c t", c=LC)

    def Shb(par):
        return (t_E if par == 0 else t_ctx)[:, 0:DM]

    t_acc = sb("t_acc", [128, TB], F32)
    t_r1 = sb("t_r1", [128, DM], F32)
    t_x1b = sb("t_x1b", [128, DM], BF16)
    t_sq = sb("t_sq", [128, DM], F32)
    scov = t_sq[:, 0:NX * NE].rearrange("p (c e) -> p c e", e=NE)
    cmpS = t_sq[:, NX * NE:NX * NE + 256]
    t_st = sb("t_st", [128, 16], F32)
    t_ones = sb("t_ones", [128, 4], F32)
    t_dcol = sb("t_dcol", [128, 2 * NTC], F32)
    t_identb = sb("t_identb", [128, 128], BF16)
    t_identf = sb("t_identf", [128, 128], F32)
    t_zero = sb("t_zero", [128, 512], BF16)
    t_gw = sb("t_gw", [128, DC * NE], BF16)
    gwv = t_gw.rearrange("p (c e) -> p c e", c=DC)
    t_pack = sb("t_pack", [128, 4096], F32)
    valq = t_pack[0:16, 0:FQ]
    s1q = t_pack[0:16, 512:1024]
    s2q = t_pack[0:16, 1024:1024 + CAP2]
    slpq = t_pack[0:16, 1408:1408 + CAPQ]
    neg1 = t_pack[0:16, 1536:2048]
    sctq = t_pack[0:16, 2048:2176]
    scwv = t_pack[32:48, :].rearrange("q (e f) -> q e f", e=EPC)
    tabv = t_pack[64:80, :].rearrange("q (a f) -> q a f", a=2)
    t_scp = sb("t_scp", [128, EPC * 256], F32)
    scpv = t_scp.rearrange("p (e f) -> p e f", e=EPC)
    t_msk = sb("t_msk", [16, FQ], U8)
    t_msk1 = sb("t_msk1", [16, 4 * CAPQ], U8)
    t_lo = sb("t_lo", [128, EPC], F32)
    t_hi = sb("t_hi", [128, EPC], F32)
    t_mid = sb("t_mid", [128, EPC], F32)
    t_prt = sb("t_prt", [128, EPC], F32)
    t_tot = sb("t_tot", [128, EPC], F32)
    t_pru = sb("t_pru", [128, EPC], U8)
    t_npr = sb("t_npr", [128, EPC], U8)
    t_nf = sb("t_nf", [1, 16], U32)
    t_nff = sb("t_nff", [1, 4], F32)
    t_nfb = sb("t_nfb", [128, 4], F32)
    t_eidx = sb("t_eidx", [128, 8], I16)
    t_idx = sb("t_idx", [128, EPC * 2 * (CAPK // 16)], I16)
    idxv = t_idx.rearrange("p (e a f) -> p e a f", e=EPC, a=2)
    t_wcol = sb("t_wcol", [128, EPC * (CAPK // 128)], F32)
    wcolv = t_wcol.rearrange("p (e f) -> p e f", e=EPC)
    t_wg0 = sb("t_wg0", [128, 2 * DC * 128], BF16)
    wg0 = t_wg0.rearrange("p (a c i) -> p a c i", a=2, c=DC)
    wlv = t_wg0.rearrange("p (c l) -> p c l", c=DC)
    t_wu0 = sb("t_wu0", [128, 2 * DC * 128], BF16)
    wu0 = t_wu0.rearrange("p (a c i) -> p a c i", a=2, c=DC)
    wl2v = t_wu0.rearrange("p (c l) -> p c l", c=DC)
    wl2q = t_wu0.rearrange("p (c d) -> p c d", c=2)

    p_s0, p_s1 = ps("p_s0"), ps("p_s1")
    p_c0, p_c1 = ps("p_c0"), ps("p_c1")
    p_g0, p_g1 = ps("p_g0"), ps("p_g1")
    p_y0, p_y1 = ps("p_y0"), ps("p_y1")
    PB = {"ps0": p_s0, "ps1": p_s1, "pc0": p_c0, "pc1": p_c1,
          "pg0": p_g0, "pg1": p_g1, "py0": p_y0, "py1": p_y1}

    R = Rec(nc, ctx)
    TK = R.task

    def FENCE(keys):
        TK("vector", lambda e: None, writes=list(keys), sem="fence")

    accz = d_acc.rearrange("(n p a) d -> n p (a d)", p=128, a=1).rearrange("n p f -> n p f") if False else d_acc.rearrange("(n p) d -> n p d", p=64).rearrange("n p (a f) -> n (p a) f", a=2)
    NZERO = TALL * DM // (128 * 512)
    zq = [0]

    def emit_zero(n):
        for _ in range(n):
            i = zq[0]
            if i >= NZERO:
                return
            zq[0] += 1
            TK("scalar", (lambda i: lambda e: e.dma_start(out=accz[i, :, :], in_=t_zero[:, :]))(i),
               reads=["zero"], writes=["accD"], sem="qscal", inc=16)

    TK("sync", lambda e: e.dma_start(out=t_masks[:, :], in_=d_masks[:, :]),
       writes=["masks"], sem="qsync", inc=16)
    TK("sync", lambda e: e.dma_start(out=t_identf[:, :], in_=d_identf[:, :]),
       writes=["identf"], sem="qsync", inc=16)
    TK("sync", lambda e: e.dma_start(out=tabv[:, 0, :], in_=d_tabN[:, :]),
       writes=["tabN"], sem="qsync", inc=16)
    TK("sync", lambda e: e.dma_start(out=tabv[:, 1, :], in_=d_tabP[:, :]),
       writes=["tabP"], sem="qsync", inc=16)
    TK("sync", lambda e: e.dma_start(out=slpq, in_=d_slotpos[:, :]),
       writes=["slotpos"], sem="qsync", inc=16)
    TK("vector", lambda e: e.memset(t_ones[:, :], 1.0), writes=["ones"])
    TK("vector", lambda e: e.memset(neg1, -1.0), writes=["neg1"])
    TK("vector", lambda e: e.memset(t_zero[:, :], 0.0), writes=["zero"])
    TK("vector", lambda e: e.memset(t_st[:, 8:9], EPS), writes=["epsc"])
    TK("vector", lambda e: e.tensor_copy(out=t_identb[:, :], in_=t_identf[:, :]),
       reads=["identf"], writes=["identb"])

    for ch in range(NX):
        xsrc = t_r1 if ch % 2 == 0 else t_sq
        xkey = "r1" if ch % 2 == 0 else "sq"
        TK("sync", (lambda ch, xsrc: lambda e: e.dma_start(
            out=xsrc[:, :], in_=d_x[ch * 128:(ch + 1) * 128, :]))(ch, xsrc),
            writes=[xkey], sem="qsync", inc=16)
        pka, pkb = ("pg0", "pg1") if ch % 2 == 0 else ("py0", "py1")

        def mk_tr(ch, xsrc, pka, pkb):
            def f(e):
                pa, pb = PB[pka], PB[pkb]
                mm = None
                for dc in range(DC):
                    dst = (pa if dc < 4 else pb)[:, (dc % 4) * 128:(dc % 4) * 128 + 128]
                    mm = e.transpose(dst, xsrc[:, dc * 128:(dc + 1) * 128], t_identf[:, :])
                return mm
            return f
        TK("tensor", mk_tr(ch, xsrc, pka, pkb), reads=[xkey, "identf"],
           writes=[pka, pkb])

        def mk_ev(ch, pka, pkb):
            def f(e):
                e.tensor_copy(out=bigv[:, 0:4, ch * 128:(ch + 1) * 128],
                              in_=PB[pka][:, :].rearrange("p (c t) -> p c t", c=4))
                e.drain()
                return e.tensor_copy(out=bigv[:, 4:8, ch * 128:(ch + 1) * 128],
                                     in_=PB[pkb][:, :].rearrange("p (c t) -> p c t", c=4))
            return f
        TK("vector", mk_ev(ch, pka, pkb), reads=[pka, pkb], writes=[f"xT{ch}"])

    TK("gpsimd", lambda e: e.dma_start(out=wl2v[:, :, :], in_=d_wuv.rearrange("(c p) l -> p c l", p=128)),
       writes=["wl2"], sem="qgp", inc=16)
    TK("gpsimd", lambda e: e.dma_start(out=wmvv[:, :, :], in_=d_woT.rearrange("(c p) d -> p c d", p=128)),
       writes=["wmv"], sem="qgp", inc=16)
    for lc in range(LC):
        for dh in range(2):
            def mk_ve(lc, dh):
                def f(e):
                    mm = None
                    for dc in range(DC):
                        mm = e.matmul(p_c1[:, :], wl2v[:, dc, lc * 128:(lc + 1) * 128],
                                      wmvv[:, dc, dh * 512:(dh + 1) * 512],
                                      start=(dc == 0), stop=(dc == DC - 1))
                    return mm
                return f
            TK("tensor", mk_ve(lc, dh), reads=["wl2", "wmv"], writes=["pc1"])
            TK("vector", (lambda lc, dh: lambda e: e.tensor_copy(
                out=veffv[:, lc, dh * 512:(dh + 1) * 512], in_=p_c1[:, :]))(lc, dh),
                reads=["pc1"], writes=["veff"])

    TK("gpsimd", lambda e: e.dma_start(out=wlv[:, :, :], in_=d_wuq.rearrange("(c p) l -> p c l", p=128)),
       writes=["wl"], sem="qgp", inc=16)
    TK("gpsimd", lambda e: e.dma_start(out=wl2v[:, :, :], in_=d_wuk.rearrange("(c p) l -> p c l", p=128)),
       writes=["wl2"], sem="qgp", inc=16)

    def mm_amat(e):
        mm = None
        for qc in range(2):
            for dc in range(DC):
                mm = e.matmul(p_c0[:, qc * 256:(qc + 1) * 256],
                              wlv[:, dc, qc * 128:(qc + 1) * 128], wl2v[:, dc, :],
                              start=(dc == 0), stop=(dc == DC - 1))
        return mm
    TK("tensor", mm_amat, reads=["wl", "wl2"], writes=["pc0"])
    TK("vector", lambda e: e.tensor_copy(out=t_hal[:, 0:512], in_=p_c0[:, :]),
       reads=["pc0"], writes=["amat", "wmv"])

    TK("gpsimd", lambda e: e.dma_start(out=wl2q[:, :, :], in_=d_wdq.rearrange("(c p) d -> p c d", p=128)),
       writes=["wl2"], sem="qgp", inc=16)
    for dc in range(DC):
        def mk_keff(dc):
            def f(e):
                mm = None
                for qc in range(2):
                    mm = e.matmul(p_c1[:, (dc % 2) * 256:(dc % 2) * 256 + 256],
                                  wl2q[:, qc, dc * 128:(dc + 1) * 128], amatv[:, qc, :],
                                  start=(qc == 0), stop=(qc == 1))
                return mm
            return f
        TK("tensor", mk_keff(dc), reads=["wl2", "amat"], writes=["pc1"])
        TK("vector", (lambda dc: lambda e: e.tensor_copy(
            out=keffv[:, dc, :], in_=p_c1[:, (dc % 2) * 256:(dc % 2) * 256 + 256]))(dc),
            reads=["pc1"], writes=[f"keff{dc}", "wmv"])

    for h in range(NH):
        hc, hp = h // 2, (h % 2) * 64
        for qh in range(2):
            def mk_mh(hc, hp, qh):
                def f(e):
                    return e.matmul(p_c0[:, (qh % 2) * 256:(qh % 2) * 256 + 256],
                                    wlv[hp:hp + 64, hc, qh * 128:(qh + 1) * 128],
                                    keffv[hp:hp + 64, hc, :], start=True, stop=True)
                return f
            TK("tensor", mk_mh(hc, hp, qh), reads=["wl", f"keff{hc}"], writes=["pc0"])
            TK("vector", (lambda h, qh: lambda e: e.tensor_copy(
                out=mhv[:, h, qh, :], in_=p_c0[:, (qh % 2) * 256:(qh % 2) * 256 + 256]))(h, qh),
                reads=["pc0"], writes=["mh"])

    TK("gpsimd", lambda e: e.dma_start(out=wlv[:, :, :], in_=d_wdqT.rearrange("(c p) l -> p c l", p=128)),
       writes=["wl"], sem="qgp", inc=16)
    TK("gpsimd", lambda e: e.dma_start(out=wl2v[:, :, :], in_=d_wdkvT.rearrange("(c p) l -> p c l", p=128)),
       writes=["wl2"], sem="qgp", inc=16)
    TK("gpsimd", lambda e: e.dma_start(out=gwv[:, :, :], in_=d_gwT.rearrange("(c p) e -> p c e", p=128)),
       writes=["gw"], sem="qgp", inc=16)

    for name, wkey, wsrc, dst, dkey in (("cq", "wl", wlv, cqT, "cqT"),
                                        ("ckv", "wl2", wl2v, ckvT, "ckvT")):
        for qc in range(LC):
            for tbl in range(T // 512):
                pk = "pg0" if tbl % 2 == 0 else "pg1"

                def mk_cq(wsrc, qc, tbl, pk):
                    def f(e):
                        mm = None
                        for dc in range(DC):
                            mm = e.matmul(PB[pk][:, :],
                                          wsrc[:, dc, qc * 128:(qc + 1) * 128],
                                          bigv[:, dc, tbl * 512:(tbl + 1) * 512],
                                          start=(dc == 0), stop=(dc == DC - 1))
                        return mm
                    return f
                rd = [wkey] + [f"xT{ch}" for ch in range(tbl * 4, tbl * 4 + 4)]
                TK("tensor", mk_cq(wsrc, qc, tbl, pk), reads=rd, writes=[pk])
                TK("vector", (lambda dst, qc, tbl, pk: lambda e: e.tensor_copy(
                    out=dst[:, qc, tbl * 512:(tbl + 1) * 512], in_=PB[pk][:, :]))(dst, qc, tbl, pk),
                    reads=[pk], writes=[f"{dkey}{qc}_{tbl}"])

    for sc in range(NX):
        pk = "py0" if sc % 2 == 0 else "py1"

        def mk_tm(sc, pk):
            def f(e):
                mm = None
                for dc in range(DC):
                    mm = e.matmul(PB[pk][:, 0:LAT],
                                  bigv[:, dc, sc * 128:(sc + 1) * 128],
                                  wl2v[:, dc, :],
                                  start=(dc == 0), stop=(dc == DC - 1))
                return mm
            return f
        TK("tensor", mk_tm(sc, pk),
           reads=["wl2", f"xT{sc}"], writes=[pk])
        TK("vector", (lambda sc, pk: lambda e: e.tensor_copy(
            out=ckvtm[:, sc, :], in_=PB[pk][:, 0:LAT]))(sc, pk),
            reads=[pk], writes=[f"ckvtm{sc}"])

    for nm, src, dstd in (("wgB", d_wgT, d_wgB), ("wuB", d_wuT, d_wuB)):
        for ee in range(EPC):
            for icq in range(ICH):
                TK("gpsimd", (lambda src, dstd, ee, icq: lambda e: e.dma_start(
                    out=dstd[ee, icq, :, :].rearrange("p (c i) -> p c i", c=DC),
                    in_=src[ee, :, icq * 128:(icq + 1) * 128]
                    .rearrange("(c p) i -> p c i", p=128)))(src, dstd, ee, icq),
                    sem="qgp", inc=16, writes=[f"{nm}{ee}"])
    for ee in range(EPC):
        for half in range(4):
            TK("gpsimd", (lambda ee, half: lambda e: e.dma_start(
                out=d_wdB[ee, half * (ISZ // 4):(half + 1) * (ISZ // 4), :],
                in_=d_wdT[ee, half * (ISZ // 4):(half + 1) * (ISZ // 4), :],
                max_dma_last_dim=8192))(ee, half),
                sem="qgp", inc=16, writes=[f"wdB{ee}"])
    for nm, src, dstd in (("sgB", d_sgT, d_sgB), ("suB", d_suT, d_suB)):
        for icq in range(SIC):
            TK("gpsimd", (lambda src, dstd, icq: lambda e: e.dma_start(
                out=dstd[icq, :, :].rearrange("p (c i) -> p c i", c=DC),
                in_=src[:, icq * 128:(icq + 1) * 128]
                .rearrange("(c p) i -> p c i", p=128)))(src, dstd, icq),
                sem="qgp", inc=16, writes=[nm])
    TK("gpsimd", lambda e: e.dma_start(out=d_sdB[:, :], in_=d_sdT[:, :], max_dma_last_dim=8192),
       sem="qgp", inc=16, writes=["sdB"])

    INV8 = 1.0 / np.sqrt(DM / NH)
    E2v = t_x1b.rearrange("p (a t) -> p a t", a=2)
    accBt = t_sq[:, 0:TB]
    XKEYS = ["x1b", "EB0", "EB1", "sq", "accB"]
    qlatB = ctxv

    def emit_exp(psk, dstv, par, key):
        TK("scalar", (lambda psk, par: lambda e: e.activation(
            out=dstv[:, par, :], in_=PB[psk][:, 0:TB], func=AF.Exp, scale=INV8))(psk, par),
            reads=[psk], writes=[f"{key}{par}"])

    def emit_mask_acc(i, sc, dstv, par, key, acct, akey):
        diag = sc >= 4 * i
        if diag:
            TK("vector", (lambda i, sc, par: lambda e: e.tensor_tensor(
                out=dstv[:, par, :], in0=dstv[:, par, :],
                in1=maskap(sc - 4 * i), op=ALU.mult))(i, sc, par),
                reads=["masks"], writes=[f"{key}{par}"])
        if sc == 0:
            TK("vector", (lambda par: lambda e: e.tensor_copy(
                out=acct[:, :], in_=dstv[:, par, :]))(par),
                reads=[f"{key}{par}"], writes=[akey])
        else:
            TK("vector", (lambda par: lambda e: e.tensor_tensor(
                out=acct[:, :], in0=acct[:, :], in1=dstv[:, par, :], op=ALU.add))(par),
                reads=[f"{key}{par}"], writes=[akey])

    for i in range(NTB):
        nsc = (i + 1) * NTC
        FENCE(XKEYS)
        for hp in range(NH // 2):
            hA, hB = 2 * hp, 2 * hp + 1
            FENCE(["ctx", "qlatB"])
            for h, banks, qdst, qkey in ((hA, ("ps0", "ps1"), qlatv, "qlat"),
                                         (hB, ("py0", "py1"), qlatB, "qlatB")):
                for lcc in range(LC):
                    qpk = banks[lcc]

                    def mk_ql(h, lcc, qpk):
                        def f(e):
                            mm = None
                            for qh in range(2):
                                mm = e.matmul(PB[qpk][:, 0:TB],
                                              mhv[:, h, qh, lcc * 128:(lcc + 1) * 128],
                                              cqT[:, qh, i * TB:(i + 1) * TB],
                                              start=(qh == 0), stop=(qh == 1))
                            return mm
                        return f
                    TK("tensor", mk_ql(h, lcc, qpk),
                       reads=["mh", f"cqT0_{i}", f"cqT1_{i}"], writes=[qpk])
                    TK("vector", (lambda lcc, qpk, qdst: lambda e: e.tensor_copy(
                        out=qdst[:, lcc, :], in_=PB[qpk][:, 0:TB]))(lcc, qpk, qdst),
                        reads=[qpk], writes=[qkey])

            def mk_sc(sc, psA, psB):
                def f(e):
                    mm = None
                    for lcc in range(LC):
                        st = ckvT[:, lcc, sc * 128:(sc + 1) * 128]
                        mm = e.matmul(PB[psA][:, 0:TB], st, qlatv[:, lcc, :],
                                      start=(lcc == 0), stop=(lcc == 1))
                        mm = e.matmul(PB[psB][:, 0:TB], st, qlatB[:, lcc, :],
                                      start=(lcc == 0), stop=(lcc == 1))
                    return mm
                return f

            def mk_cx(sc, par, nsc):
                def f(e):
                    mm = None
                    for lcc in range(LC):
                        st = ckvtm[:, sc, lcc * 128:(lcc + 1) * 128]
                        pA = (p_c0 if lcc == 0 else p_c1)
                        pB = PB["pg0" if lcc == 0 else "pg1"]
                        mm = e.matmul(pA[:, 0:TB], st, Ev[:, par, :],
                                      start=(sc == 0), stop=(sc == nsc - 1))
                        mm = e.matmul(pB[:, 0:TB], st, E2v[:, par, :],
                                      start=(sc == 0), stop=(sc == nsc - 1))
                    return mm
                return f

            def emit_scores(sc):
                par = sc % 2
                psA = "ps0" if par == 0 else "ps1"
                psB = "py0" if par == 0 else "py1"
                TK("tensor", mk_sc(sc, psA, psB),
                   reads=["qlat", "qlatB", f"ckvT0_{sc // 4}", f"ckvT1_{sc // 4}"],
                   writes=[psA, psB])
                emit_exp(psA, Ev, par, "EA")
                emit_exp(psB, E2v, par, "EB")
                emit_mask_acc(i, sc, Ev, par, "EA", t_acc, "accS")
                emit_mask_acc(i, sc, E2v, par, "EB", accBt, "accB")

            def emit_ctx(sc):
                par = sc % 2
                TK("tensor", mk_cx(sc, par, nsc),
                   reads=[f"EA{par}", f"EB{par}", f"ckvtm{sc}"],
                   writes=["pc0", "pc1", "pg0", "pg1"])

            for sc in range(nsc):
                emit_scores(sc)
                if sc >= 1:
                    emit_ctx(sc - 1)
            emit_ctx(nsc - 1)

            for hh, acct, akey, coff in ((hA, t_acc, "accS", 0), (hB, accBt, "accB", NTC)):
                for tc in range(NTC):
                    dpk = "py0" if tc % 2 == 0 else "py1"
                    TK("tensor", (lambda tc, dpk, acct: lambda e: e.matmul(
                        PB[dpk][:, 0:1], acct[:, tc * 128:(tc + 1) * 128], t_ones[:, 0:1],
                        start=True, stop=True))(tc, dpk, acct),
                        reads=[akey, "ones"], writes=[dpk])
                    TK("vector", (lambda tc, dpk, coff: lambda e: e.reciprocal(
                        out=t_dcol[:, coff + tc:coff + tc + 1], in_=PB[dpk][:, 0:1]))(tc, dpk, coff),
                        reads=[dpk], writes=[f"dcol{coff + tc}"])

            FENCE(["qlatB", "ctx"])
            for hh, pk0, pk1, coff in ((hA, "pc0", "pc1", 0), (hB, "pg0", "pg1", NTC)):
                def mk_evcx(pk0, pk1):
                    def f(e):
                        e.tensor_copy(out=ctxv[:, 0, :], in_=PB[pk0][:, 0:TB])
                        e.drain()
                        return e.tensor_copy(out=ctxv[:, 1, :], in_=PB[pk1][:, 0:TB])
                    return f
                TK("vector", mk_evcx(pk0, pk1), reads=[pk0, pk1], writes=["ctx"])
                for tc in range(NTC):
                    ypk = "py0" if tc % 2 == 0 else "py1"

                    def mk_y(hh, tc, ypk):
                        def f(e):
                            mm = None
                            for lcc in range(LC):
                                mm = e.matmul(PB[ypk][:, 64:128],
                                              ctxv[:, lcc, tc * 128:(tc + 1) * 128],
                                              veffv[:, lcc, hh * 64:(hh + 1) * 64],
                                              start=(lcc == 0), stop=(lcc == 1))
                            return mm
                        return f
                    TK("tensor", mk_y(hh, tc, ypk), reads=["ctx", "veff"], writes=[ypk])
                    TK("vector", (lambda hh, tc, ypk, coff: lambda e: e.tensor_scalar(
                        out=yv[:, tc, hh * 64:(hh + 1) * 64], in0=PB[ypk][:, 64:128],
                        scalar1=t_dcol[:, coff + tc:coff + tc + 1], scalar2=None,
                        op0=ALU.mult))(hh, tc, ypk, coff),
                        reads=[ypk, f"dcol{coff + tc}"], writes=[f"ybuf{tc}"])
            emit_zero(8)

        FENCE(XKEYS)
        for tc in range(NTC):
            gtc = i * NTC + tc
            TK("sync", (lambda gtc: lambda e: e.dma_start(
                out=t_r1[:, :], in_=d_x[gtc * 128:(gtc + 1) * 128, :]))(gtc),
                writes=["r1"], sem="qsync", inc=16)

            def mk_lnA(tc):
                def f(e):
                    e.tensor_tensor(out=t_r1[:, :], in0=t_r1[:, :], in1=yv[:, tc, :], op=ALU.add)
                    e.drain()
                    e.reduce_sum(out=t_st[:, 0:1], in_=t_r1[:, :], axis=AX.X)
                    e.drain()
                    e.tensor_scalar(out=t_st[:, 1:2], in0=t_st[:, 0:1], scalar1=1.0 / DM,
                                    scalar2=None, op0=ALU.mult)
                    e.drain()
                    return e.tensor_scalar(out=t_sq[:, :], in0=t_r1[:, :], scalar1=t_st[:, 1:2],
                                           scalar2=None, op0=ALU.subtract)
                return f
            TK("vector", mk_lnA(tc), reads=[f"ybuf{tc}"], writes=["r1", "sq", "st"])

            def mk_lnB(e):
                e.activation(out=t_sq[:, :], in_=t_sq[:, :], func=AF.Square,
                             accum_out=t_st[:, 2:3])
                e.drain()
                e.activation(out=t_st[:, 3:4], in_=t_st[:, 2:3], func=AF.Ln,
                             scale=1.0 / DM, bias=t_st[:, 8:9])
                e.drain()
                return e.activation(out=t_st[:, 4:5], in_=t_st[:, 3:4], func=AF.Exp, scale=-0.5)
            TK("scalar", mk_lnB, reads=["sq", "epsc"], writes=["sq", "st"])

            def mk_lnC(e):
                e.tensor_scalar(out=t_r1[:, :], in0=t_r1[:, :], scalar1=t_st[:, 1:2],
                                scalar2=t_st[:, 4:5], op0=ALU.subtract, op1=ALU.mult)
                e.drain()
                return e.tensor_copy(out=t_x1b[:, :], in_=t_r1[:, :])
            TK("vector", mk_lnC, reads=["st"], writes=["r1", "x1b"])

            TK("sync", (lambda gtc: lambda e: e.dma_start(
                out=d_x1f[gtc * 128:(gtc + 1) * 128, :], in_=t_r1[:, :]))(gtc),
                reads=["r1"], writes=["x1fD"], sem="qsync", inc=16)
            if dbg == "x1":
                TK("sync", (lambda gtc: lambda e: e.dma_start(
                    out=d_out[gtc * 128:(gtc + 1) * 128, :], in_=t_r1[:, :]))(gtc),
                    reads=["r1"], writes=["outD"], sem="qsync", inc=16)
            TK("sync", (lambda gtc: lambda e: e.dma_start(
                out=d_agin[gtc * 128:(gtc + 1) * 128, :], in_=t_x1b[:, :]))(gtc),
                reads=["x1b"], writes=[f"aginD{i}"], sem="qsync", inc=16)

            pka, pkb = ("pg0", "pg1") if gtc % 2 == 0 else ("py0", "py1")

            def mk_tx1(pka, pkb):
                def f(e):
                    mm = None
                    for dc in range(DC):
                        dst = (PB[pka] if dc < 4 else PB[pkb])[:, (dc % 4) * 128:(dc % 4) * 128 + 128]
                        mm = e.transpose(dst, t_r1[:, dc * 128:(dc + 1) * 128], t_identf[:, :])
                    return mm
                return f
            TK("tensor", mk_tx1(pka, pkb), reads=["r1", "identf"], writes=[pka, pkb])

            def mk_evx1(gtc, pka, pkb):
                def f(e):
                    e.tensor_copy(out=bigv[:, 0:4, gtc * 128:(gtc + 1) * 128],
                                  in_=PB[pka][:, :].rearrange("p (c t) -> p c t", c=4))
                    e.drain()
                    return e.tensor_copy(out=bigv[:, 4:8, gtc * 128:(gtc + 1) * 128],
                                         in_=PB[pkb][:, :].rearrange("p (c t) -> p c t", c=4))
                return f
            TK("vector", mk_evx1(gtc, pka, pkb), reads=[pka, pkb], writes=[f"xT{gtc}"])

        TK("gpsimd", (lambda i: lambda e: e.collective_compute(
            "AllGather", ALU.bypass, replica_groups=RGRP,
            ins=[d_agin[i * TB:(i + 1) * TB, :]],
            outs=[d_x1all[i * B * TB:(i + 1) * B * TB, :]]))(i),
            reads=[f"aginD{i}"], writes=[f"x1allD{i}"], sem="cc", inc=1)

    if dbg == "x1":
        TK("sync", lambda e: None, reads=["outD", "x1fD"])
        block = ctx.enter_context(nc.Block())
        R.replay(block)
        ctx.close()
        nc.finalize()
        return nc

    FENCE(["sq", "r1"])
    for tch in range(NX):
        gpk = "pg0" if tch % 2 == 0 else "pg1"

        def mk_lg(tch, gpk):
            def f(e):
                mm = None
                for dc in range(DC):
                    mm = e.matmul(PB[gpk][:, 0:NE], bigv[:, dc, tch * 128:(tch + 1) * 128],
                                  gwv[:, dc, :], start=(dc == 0), stop=(dc == DC - 1))
                return mm
            return f
        TK("tensor", mk_lg(tch, gpk), reads=[f"xT{tch}", "gw"], writes=[gpk])
        TK("vector", (lambda gpk: lambda e: e.reduce_max(
            out=t_st[:, 5:6], in_=PB[gpk][:, 0:NE], axis=AX.X, negate=True))(gpk),
            reads=[gpk], writes=["st"])
        TK("scalar", (lambda tch, gpk: lambda e: e.activation(
            out=scov[:, tch, :], in_=PB[gpk][:, 0:NE], func=AF.Exp,
            bias=t_st[:, 5:6], accum_out=t_st[:, 6:7]))(tch, gpk),
            reads=[gpk, "st"], writes=[f"sco{tch}", "st"])

        def mk_sms(tch):
            def f(e):
                e.reciprocal(out=t_st[:, 7:8], in_=t_st[:, 6:7])
                e.drain()
                return e.tensor_scalar(out=scov[:, tch, :], in0=scov[:, tch, :],
                                       scalar1=t_st[:, 7:8], scalar2=None, op0=ALU.mult)
            return f
        TK("vector", mk_sms(tch), reads=["st"], writes=[f"sco{tch}", "st"])
        spk2 = "py0" if tch % 2 == 0 else "py1"
        TK("tensor", (lambda tch, spk2: lambda e: e.transpose(
            PB[spk2][0:16, 0:128], scov[:, tch, :], t_identf[:, :]))(tch, spk2),
            reads=[f"sco{tch}", "identf"], writes=[spk2])
        TK("vector", (lambda spk2: lambda e: e.tensor_copy(
            out=sctq, in_=PB[spk2][0:16, 0:128]))(spk2),
            reads=[spk2], writes=["sct"])
        TK("sync", (lambda tch: lambda e: e.dma_start(
            out=d_scT[:, tch * 128:(tch + 1) * 128], in_=sctq))(tch),
            reads=["sct"], writes=["scTD"], sem="qsync", inc=16)

    TK("gpsimd", lambda e: e.collective_compute(
        "AllGather", ALU.bypass, replica_groups=RGRP, ins=[d_scT[:, :]], outs=[d_scA[:, :]]),
        reads=["scTD"], writes=["scAD"], sem="cc", inc=1)
    TK("sync", lambda e: e.dma_start(out=t_eidx[:, :], in_=d_eidx[:, :]),
       writes=["eidx"], sem="qsync", inc=16)
    FENCE(["accS", "gsc"])
    TK("gpsimd", lambda e: e.dma_gather(
        t_acc.rearrange("p (a f) -> p a f", a=1),
        d_scA.rearrange("r (k f) -> (r k) f", k=8),
        t_eidx[:, :], 128, 128, 512, transpose=False, single_packet=False),
        reads=["scAD", "eidx"], writes=["gsc"], sem="qgp", inc=16)
    TK("sync", lambda e: e.dma_start(
        out=d_scB.rearrange("e b (k f) -> (e b k) f", k=8), in_=t_acc[:, :]),
        reads=["gsc"], writes=["scBD"], sem="qsync", inc=16)
    for e_ in range(EPC):
        for b_ in range(B):
            TK("sync", (lambda e_, b_: lambda e: e.dma_start(
                out=scwv[2 * b_:2 * b_ + 2, e_, :],
                in_=d_scB[e_, b_, :].rearrange("(q f) -> q f", q=2)))(e_, b_),
                reads=["scBD"], writes=["scw"], sem="qsync", inc=16)
            TK("sync", (lambda e_, b_: lambda e: e.dma_start(
                out=scpv[16 * b_:16 * b_ + 16, e_, :],
                in_=d_scB[e_, b_, :].rearrange("(q f) -> q f", q=16)))(e_, b_),
                reads=["scBD"], writes=["scp"], sem="qsync", inc=16)

    TK("vector", lambda e: e.memset(t_lo[:, :], 0.0), writes=["lo"])
    TK("vector", lambda e: e.memset(t_hi[:, :], 1.0), writes=["hi"])
    for it in range(BISN):
        def mk_mid(e):
            e.tensor_tensor(out=t_mid[:, :], in0=t_lo[:, :], in1=t_hi[:, :], op=ALU.add)
            e.drain()
            return e.tensor_scalar(out=t_mid[:, :], in0=t_mid[:, :], scalar1=0.5,
                                   scalar2=None, op0=ALU.mult)
        TK("vector", mk_mid, reads=["lo", "hi"], writes=["mid"])
        for e_ in range(EPC):
            TK("vector", (lambda e_: lambda e: e.tensor_scalar(
                out=cmpS, in0=scpv[:, e_, :], scalar1=t_mid[:, e_:e_ + 1],
                scalar2=None, op0=ALU.is_ge, op1=ALU.add, accum_out=t_prt[:, e_:e_ + 1]))(e_),
                reads=["scp", "mid"], writes=["cmpSk", "prt"])
        TK("gpsimd", lambda e: e.partition_all_reduce(
            out_ap=t_tot[:, :], in_ap=t_prt[:, :], channels=128, reduce_op=ReduceOp.add),
            reads=["prt"], writes=["tot"])

        def mk_sel(e):
            e.tensor_scalar(out=t_pru[:, :], in0=t_tot[:, :], scalar1=float(CAPK),
                            scalar2=None, op0=ALU.is_ge)
            e.drain()
            e.tensor_scalar(out=t_npr[:, :], in0=t_tot[:, :], scalar1=float(CAPK),
                            scalar2=None, op0=ALU.is_lt)
            e.drain()
            e.copy_predicated(out=t_lo[:, :], mask=t_pru[:, :], data=t_mid[:, :])
            e.drain()
            return e.copy_predicated(out=t_hi[:, :], mask=t_npr[:, :], data=t_mid[:, :])
        TK("vector", mk_sel, reads=["tot", "mid"], writes=["lo", "hi", "pru", "npr"])

    for e_ in range(EPC):
        for s_ in range(3):
            for q in range(4):
                def mk_val(e_, s_, q):
                    def f(e):
                        e.tensor_scalar(out=t_msk[:, :], in0=scwv[:, e_, q * FQ:(q + 1) * FQ],
                                        scalar1=t_lo[0:16, e_:e_ + 1],
                                        scalar2=None, op0=ALU.is_lt)
                        e.drain()
                        if s_ == 0:
                            e.tensor_copy(out=valq, in_=tabv[:, 0, q * FQ:(q + 1) * FQ])
                        elif s_ == 1:
                            e.tensor_copy(out=valq, in_=tabv[:, 1, q * FQ:(q + 1) * FQ])
                        else:
                            e.tensor_copy(out=valq, in_=scwv[:, e_, q * FQ:(q + 1) * FQ])
                        e.drain()
                        return e.copy_predicated(out=valq, mask=t_msk[:, :], data=neg1)
                    return f
                TK("vector", mk_val(e_, s_, q),
                   reads=["tabN", "tabP", "scw", "neg1", "lo"], writes=["val", "msk"])
                TK("gpsimd", (lambda q: lambda e: e.sparse_gather(
                    s1q[:, q * CAPQ:(q + 1) * CAPQ], valq,
                    num_found=t_nf[:, q:q + 1]))(q),
                    reads=["val"], writes=["s1", "nf"])
            if s_ == 0:
                TK("vector", lambda e: e.tensor_copy(out=t_nff[:, :], in_=t_nf[:, 0:4]),
                   reads=["nf"], writes=["nff"])
                TK("gpsimd", lambda e: e.partition_broadcast(out_ap=t_nfb[:, :], in_ap=t_nff[:, :]),
                   reads=["nff"], writes=["nfb"])
                for q in range(4):
                    TK("vector", (lambda q: lambda e: e.tensor_scalar(
                        out=t_msk1[:, q * CAPQ:(q + 1) * CAPQ], in0=slpq,
                        scalar1=t_nfb[0:16, q:q + 1], scalar2=None, op0=ALU.is_ge))(q),
                        reads=["nfb", "slotpos"], writes=["msk1"])
            for q in range(4):
                TK("vector", (lambda q: lambda e: e.copy_predicated(
                    out=s1q[:, q * CAPQ:(q + 1) * CAPQ],
                    mask=t_msk1[:, q * CAPQ:(q + 1) * CAPQ], data=neg1[:, 0:CAPQ]))(q),
                    reads=["msk1", "neg1"], writes=["s1"])
            TK("gpsimd", (lambda s_: lambda e: e.sparse_gather(
                s2q, s1q, num_found=t_nf[:, 12 + s_:13 + s_]))(s_),
                reads=["s1"], writes=["s2", "nf"])
            if s_ < 2:
                TK("vector", (lambda e_, s_: lambda e: e.tensor_copy(
                    out=idxv[0:16, e_, s_, :], in_=s2q[:, 0:CAPK // 16]))(e_, s_),
                    reads=["s2"], writes=["idx"])
            else:
                TK("sync", (lambda e_: lambda e: e.dma_start(
                    out=d_wflat[e_, :].rearrange("(f q) -> q f", q=16),
                    in_=s2q[:, 0:CAPK // 16],
                    allow_slow_non_contiguous=True))(e_),
                    reads=["s2"], writes=["wflatD"], sem="qsync", inc=16)
                TK("sync", (lambda e_: lambda e: e.dma_start(
                    out=wcolv[:, e_, :], in_=d_wflat[e_, :].rearrange("(q p) -> p q", p=128),
                    allow_slow_non_contiguous=True))(e_),
                    reads=["wflatD"], writes=["wcol"], sem="qsync", inc=16)
    for grp in range(1, 8):
        TK("sync", (lambda grp: lambda e: e.dma_start(
            out=t_idx[grp * 16:(grp + 1) * 16, :], in_=t_idx[0:16, :]))(grp),
            reads=["idx"], writes=["idxR"], sem="qsync", inc=16)

    FENCE(["mh"] + [f"ckvT0_{t}" for t in range(NTB)] + [f"ckvT1_{t}" for t in range(NTB)]
          + ["wl", "wl2", "x1b", "EB0", "EB1"]
          + [f"ckvtm{s}" for s in range(NX)]
          + ["wdT0", "wdT1", "wgT0", "wgT1", "wuT0", "wuT1",
             "shh0", "shh1"] + [f"hs{k}" for k in range(SIC)])
    TK("sync", lambda e: e.dma_start(
        out=wd0a[:, 0:SIC, :],
        in_=d_sdB[:, 0:512].rearrange("(c p) d -> p c d", p=128)),
        reads=["sdB"], writes=["wdT0"], sem="qsync", inc=16)
    TK("sync", lambda e: e.dma_start(
        out=wd0b[:, 0:SIC, :],
        in_=d_sdB[:, 512:1024].rearrange("(c p) d -> p c d", p=128)),
        reads=["sdB"], writes=["wdT1"], sem="qsync", inc=16)
    for i8 in range(NTB):
        for ic in range(SIC):
            wpar = ic % 2
            TK("sync", (lambda ic, wpar: lambda e: e.dma_start(
                out=wg0[:, wpar, :, :],
                in_=d_sgB[ic, :, :].rearrange("p (c i) -> p c i", c=DC)))(ic, wpar),
                reads=["sgB"], writes=[f"wgT{wpar}"], sem="qsync", inc=16)
            TK("sync", (lambda ic, wpar: lambda e: e.dma_start(
                out=wu0[:, wpar, :, :],
                in_=d_suB[ic, :, :].rearrange("p (c i) -> p c i", c=DC)))(ic, wpar),
                reads=["suB"], writes=[f"wuT{wpar}"], sem="qsync", inc=16)

            def mk_gs(i8, wpar, wv, pk):
                def f(e):
                    mm = None
                    for dc in range(DC):
                        mm = e.matmul(PB[pk][:, 0:TB], wv[:, wpar, dc, :],
                                      bigv[:, dc, i8 * TB:(i8 + 1) * TB],
                                      start=(dc == 0), stop=(dc == DC - 1))
                    return mm
                return f
            xTr = [f"xT{i8 * NTC + q}" for q in range(NTC)]
            TK("tensor", mk_gs(i8, wpar, wg0, "pg0"), reads=xTr + [f"wgT{wpar}"], writes=["pg0"])
            TK("tensor", mk_gs(i8, wpar, wu0, "pg1"), reads=xTr + [f"wuT{wpar}"], writes=["pg1"])
            TK("scalar", (lambda ic: lambda e: e.activation(
                out=hsl[:, ic, :], in_=p_g0[:, 0:TB], func=AF.Gelu))(ic),
                reads=["pg0"], writes=[f"hs{ic}"])
            TK("vector", (lambda ic: lambda e: e.tensor_tensor(
                out=hsl[:, ic, :], in0=hsl[:, ic, :], in1=p_g1[:, 0:TB], op=ALU.mult))(ic),
                reads=["pg1"], writes=[f"hs{ic}"])
        for tc4 in range(NTC):
            gtc = i8 * NTC + tc4
            for dh in range(2):
                spk = "py0" if dh == 0 else "py1"

                def mk_ds(tc4, dh, spk):
                    def f(e):
                        wsrc = wd0a if dh == 0 else wd0b
                        mm = None
                        for k in range(SIC):
                            mm = e.matmul(PB[spk][:, 0:512],
                                          hsl[:, k, tc4 * 128:(tc4 + 1) * 128],
                                          wsrc[:, k, :],
                                          start=(k == 0), stop=(k == SIC - 1))
                        return mm
                    return f
                TK("tensor", mk_ds(tc4, dh, spk),
                   reads=[f"hs{k}" for k in range(SIC)] + [f"wdT{dh}"], writes=[spk])
                TK("vector", (lambda dh, spk: lambda e: e.tensor_copy(
                    out=E2v[:, dh, :], in_=PB[spk][:, 0:512]))(dh, spk),
                    reads=[spk], writes=[f"shh{dh}"])
                TK("sync", (lambda gtc, dh: lambda e: e.dma_start(
                    out=d_sh[gtc * 128:(gtc + 1) * 128, dh * 512:(dh + 1) * 512],
                    in_=E2v[:, dh, :]))(gtc, dh),
                    reads=[f"shh{dh}"], writes=["shD"], sem="qsync", inc=16)

    FENCE(["wmv"] + [f"keff{d}" for d in range(DC)] + ["amat"]
          + [f"cqT{q}_{t}" for q in range(LC) for t in range(NTB)]
          + [f"ckvT{q}_{t}" for q in range(LC) for t in range(NTB)]
          + ["mh"] + [f"ckvtm{s}" for s in range(NX)]
          + [f"ybuf{q}" for q in range(4)] + ["masks", "veff", "wl", "wl2"]
          + [f"H{ic}" for ic in range(ICH)]
          + ["xg0", "xg1", "wdT0", "wdT1", "wgT0", "wgT1", "wuT0", "wuT1", "eoS"]
          + ["qlat", "qlatB", "EA0", "EA1", "ctx"] + [f"hs{k}" for k in range(SIC)])
    allAG = [f"x1allD{i}" for i in range(NTB)]
    EOPK = ["ps0", "ps1", "pc0", "pc1"]

    def emit_gather(e_, sc8):
        par = (e_ * NGC + sc8) % 2
        TK("gpsimd", (lambda e_, sc8, par: lambda e: e.dma_gather(
            xgv[:, par, :, :], d_x1all[:, :], idxv[:, e_, 1, sc8 * 32:(sc8 + 1) * 32],
            GCH, GCH, DM, transpose=True, single_packet=False))(e_, sc8, par),
            reads=allAG + ["idx", "idxR"], writes=[f"xg{par}"], sem="qgp", inc=16)

    emit_gather(0, 0)
    for e_ in range(EPC):
        for sc8 in range(NGC):
            par = (e_ * NGC + sc8) % 2
            for ic in range(ICH):
                wpar = ic % 2
                TK("sync", (lambda e_, ic, wpar: lambda e: e.dma_start(
                    out=wg0[:, wpar, :, :],
                    in_=d_wgB[e_, ic, :, :].rearrange("p (c i) -> p c i", c=DC)))(e_, ic, wpar),
                    reads=[f"wgB{e_}"], writes=[f"wgT{wpar}"], sem="qsync", inc=16)
                TK("sync", (lambda e_, ic, wpar: lambda e: e.dma_start(
                    out=wu0[:, wpar, :, :],
                    in_=d_wuB[e_, ic, :, :].rearrange("p (c i) -> p c i", c=DC)))(e_, ic, wpar),
                    reads=[f"wuB{e_}"], writes=[f"wuT{wpar}"], sem="qsync", inc=16)

                def mk_g(par, wpar, wv, pk):
                    def f(e):
                        mm = None
                        for dc in range(DC):
                            mm = e.matmul(PB[pk][:, 0:GCH], wv[:, wpar, dc, :],
                                          xgv[:, par, dc, :],
                                          start=(dc == 0), stop=(dc == DC - 1))
                        return mm
                    return f
                TK("tensor", mk_g(par, wpar, wg0, "pg0"),
                   reads=[f"xg{par}", f"wgT{wpar}"], writes=["pg0"])
                TK("tensor", mk_g(par, wpar, wu0, "pg1"),
                   reads=[f"xg{par}", f"wuT{wpar}"], writes=["pg1"])
                TK("scalar", (lambda ic: lambda e: e.activation(
                    out=Hs(ic), in_=p_g0[:, 0:GCH], func=AF.Gelu))(ic),
                    reads=["pg0"], writes=[f"H{ic}"])
                TK("vector", (lambda ic: lambda e: e.tensor_tensor(
                    out=Hs(ic), in0=Hs(ic), in1=p_g1[:, 0:GCH], op=ALU.mult))(ic),
                    reads=["pg1"], writes=[f"H{ic}"])
            if sc8 + 1 < NGC:
                emit_gather(e_, sc8 + 1)
            elif e_ + 1 < EPC:
                emit_gather(e_ + 1, 0)
            for dh in range(2):
                for ich in range(2):
                    wdpar = ich
                    TK("sync", (lambda e_, dh, ich, wdpar: lambda e: e.dma_start(
                        out=(wd0a if wdpar == 0 else wd0b)[:, :, :],
                        in_=d_wdB[e_, ich * 2048:(ich + 1) * 2048, dh * 512:(dh + 1) * 512]
                        .rearrange("(c p) d -> p c d", p=128)))(e_, dh, ich, wdpar),
                        reads=[f"wdB{e_}"], writes=[f"wdT{wdpar}"], sem="qsync", inc=16)
                    for tc4 in range(NTC):
                        def mk_dn(ich, wdpar, tc4, pk):
                            def f(e):
                                wsrc = wd0a if wdpar == 0 else wd0b
                                mm = None
                                for k in range(16):
                                    mm = e.matmul(PB[pk][:, 0:512],
                                                  Hs(ich * 16 + k)[:, tc4 * 128:(tc4 + 1) * 128],
                                                  wsrc[:, k, :],
                                                  start=(ich == 0 and k == 0),
                                                  stop=(ich == 1 and k == 15))
                                return mm
                            return f
                        TK("tensor", mk_dn(ich, wdpar, tc4, EOPK[tc4]),
                           reads=[f"H{ich * 16 + k}" for k in range(16)] + [f"wdT{wdpar}"],
                           writes=[EOPK[tc4]])
                for tc4 in range(NTC):
                    TK("vector", (lambda e_, sc8, dh, tc4: lambda e: e.tensor_scalar(
                        out=eov[:, tc4, dh * 512:(dh + 1) * 512],
                        in0=PB[EOPK[tc4]][:, 0:512],
                        scalar1=wcolv[:, e_, sc8 * NTC + tc4:sc8 * NTC + tc4 + 1],
                        scalar2=None, op0=ALU.mult))(e_, sc8, dh, tc4),
                        reads=[EOPK[tc4], "wcol"], writes=["eoS"])
            TK("gpsimd", (lambda e_, sc8: lambda e: e.dma_scatter_add(
                d_acc[:, :], eov[:, :, :],
                idxv[:, e_, 0, sc8 * 32:(sc8 + 1) * 32],
                GCH, GCH, DM, single_packet=False))(e_, sc8),
                reads=["eoS", "idx", "idxR"], writes=["accD"], sem="qgp", inc=16)

    TK("gpsimd", lambda e: e.collective_compute(
        "ReduceScatter", ALU.add, replica_groups=RGRP, ins=[d_acc[:, :]], outs=[d_moe[:, :]]),
        reads=["accD"], writes=["moeD"], sem="cc", inc=1)

    FENCE([f"sco{q}" for q in range(NX)] + ["cmpSk", "sq", "r1", "mo0", "mo1",
                                            "shb0", "shb1", "H0", "H1", "H2", "H3",
                                            "H28", "H29", "H30", "H31", "EA0", "EA1",
                                            "ctx", "qlatB", "qlat",
                                            "val", "s1", "s2", "neg1", "sct", "scw",
                                            "slotpos", "tabN", "tabP", "r1b", "sqb",
                                            "st", "stb"])
    r1b_ = (t_r1, t_pack[:, 0:DM])
    sqb_ = (t_sq, t_pack[:, DM:2 * DM])
    stc_ = (0, 10)
    for gtc in range(NX):
        par = gtc % 2
        r1t, sqt, s0 = r1b_[par], sqb_[par], stc_[par]
        rk, qk, sk = ("r1", "sq", "st") if par == 0 else ("r1b", "sqb", "stb")
        TK("sync", (lambda gtc, r1t: lambda e: e.dma_start(
            out=r1t[:, :], in_=d_x1f[gtc * 128:(gtc + 1) * 128, :]))(gtc, r1t),
            reads=["x1fD"], writes=[rk], sem="qsync", inc=16)
        TK("gpsimd", (lambda gtc, par: lambda e: e.dma_start(
            out=mov[:, par, :], in_=d_moe[gtc * 128:(gtc + 1) * 128, :]))(gtc, par),
            reads=["moeD"], writes=[f"mo{par}"], sem="qgp", inc=16)
        TK("gpsimd", (lambda gtc, par: lambda e: e.dma_start(
            out=Shb(par), in_=d_sh[gtc * 128:(gtc + 1) * 128, :]))(gtc, par),
            reads=["shD"], writes=[f"shb{par}"], sem="qgp", inc=16)

        def mk_cmb(par, r1t, sqt, s0):
            def f(e):
                e.tensor_tensor(out=r1t[:, :], in0=r1t[:, :], in1=mov[:, par, :], op=ALU.add)
                e.drain()
                e.tensor_tensor(out=r1t[:, :], in0=r1t[:, :], in1=Shb(par), op=ALU.add)
                e.drain()
                e.reduce_sum(out=t_st[:, s0:s0 + 1], in_=r1t[:, :], axis=AX.X)
                e.drain()
                e.tensor_scalar(out=t_st[:, s0 + 1:s0 + 2], in0=t_st[:, s0:s0 + 1],
                                scalar1=1.0 / DM, scalar2=None, op0=ALU.mult)
                e.drain()
                return e.tensor_scalar(out=sqt[:, :], in0=r1t[:, :],
                                       scalar1=t_st[:, s0 + 1:s0 + 2],
                                       scalar2=None, op0=ALU.subtract)
            return f
        TK("vector", mk_cmb(par, r1t, sqt, s0), reads=[f"mo{par}", f"shb{par}"],
           writes=[rk, qk, sk])

        def mk_ln2B(sqt, s0):
            def f(e):
                e.activation(out=sqt[:, :], in_=sqt[:, :], func=AF.Square,
                             accum_out=t_st[:, s0 + 2:s0 + 3])
                e.drain()
                e.activation(out=t_st[:, s0 + 3:s0 + 4], in_=t_st[:, s0 + 2:s0 + 3],
                             func=AF.Ln, scale=1.0 / DM, bias=t_st[:, 8:9])
                e.drain()
                return e.activation(out=t_st[:, s0 + 4:s0 + 5], in_=t_st[:, s0 + 3:s0 + 4],
                                    func=AF.Exp, scale=-0.5)
            return f
        TK("scalar", mk_ln2B(sqt, s0), reads=[qk, "epsc"], writes=[qk, sk])
        TK("vector", (lambda r1t, s0: lambda e: e.tensor_scalar(
            out=r1t[:, :], in0=r1t[:, :], scalar1=t_st[:, s0 + 1:s0 + 2],
            scalar2=t_st[:, s0 + 4:s0 + 5], op0=ALU.subtract, op1=ALU.mult))(r1t, s0),
            reads=[sk], writes=[rk])
        TK("sync", (lambda gtc, r1t: lambda e: e.dma_start(
            out=d_out[gtc * 128:(gtc + 1) * 128, :], in_=r1t[:, :]))(gtc, r1t),
            reads=[rk], writes=["outD"], sem="qsync", inc=16)

    TK("sync", lambda e: None, reads=["outD"])

    block = ctx.enter_context(nc.Block())
    R.replay(block)
    ctx.close()
    nc.finalize()
    return nc


def _host_tables():
    p = np.arange(128)[:, None]
    v = np.arange(896)[None, :]
    masks = ((p + 384) <= v).astype(np.float32).astype(BF)
    identf = np.eye(128, dtype=np.float32)
    P, Fw = 16, TALL // 16
    t_of = (np.arange(P)[:, None] * Fw + np.arange(Fw)[None, :]).astype(np.int64)
    tabN = t_of.astype(np.float32)
    tt = t_of.ravel()
    c = tt // T
    ib = (tt % T) // TB
    r = tt % TB
    rows = ib * (B * TB) + c * TB + r
    tabP = rows.reshape(P, Fw).astype(np.float32)
    sp = np.zeros((16, CAPQ), np.float32)
    for j in range(16 * CAPQ):
        sp[j % 16, j // 16] = j
    return masks, identf, tabN, tabP, sp


def _eidx_table(c):
    j = np.arange(128)
    e_, b, k = j // 64, (j // 8) % 8, j % 8
    rows = ((b * NE + 2 * c + e_) * 8 + k).astype(np.int16)
    wrap = np.zeros((16, 8), np.int16)
    wrap[j % 16, j // 16] = rows
    return np.tile(wrap, (8, 1))


_NCCACHE = {}


def kernel(**inputs):
    x = np.ascontiguousarray(np.asarray(inputs["L0_attn_x"], dtype=np.float32))
    W_dq = np.asarray(inputs["L0_attn_W_dq"], np.float32)
    W_uq = np.asarray(inputs["L0_attn_W_uq"], np.float32)
    W_dkv = np.asarray(inputs["L0_attn_W_dkv"], np.float32)
    W_uk = np.asarray(inputs["L0_attn_W_uk"], np.float32)
    W_uv = np.asarray(inputs["L0_attn_W_uv"], np.float32)
    W_o = np.asarray(inputs["L0_attn_W_o"], np.float32)
    gate_w = np.asarray(inputs["L0_ffn_gate_w"], np.float32)
    Wg = np.asarray(inputs["L0_ffn_Wg"], np.float32)
    Wu = np.asarray(inputs["L0_ffn_Wu"], np.float32)
    Wd = np.asarray(inputs["L0_ffn_Wd"], np.float32)
    Sg = np.asarray(inputs["L0_ffn_Sg"], np.float32)
    Su = np.asarray(inputs["L0_ffn_Su"], np.float32)
    Sd = np.asarray(inputs["L0_ffn_Sd"], np.float32)

    masks, identf, tabN, tabP, slotpos = _host_tables()
    cc = np.ascontiguousarray
    shared_in = {
        "wdq": cc(W_dq), "wdqT": cc(W_dq.T), "wuq": cc(W_uq), "wuk": cc(W_uk),
        "wdkvT": cc(W_dkv.T), "wuv": cc(W_uv), "woT": cc(W_o.T),
        "sgT": cc(Sg.T), "suT": cc(Su.T), "sdT": cc(Sd.T), "gwT": cc(gate_w.T),
        "masks": masks, "identf": identf, "tabN": tabN, "tabP": tabP,
        "slotpos": slotpos,
    }
    in_maps = []
    for c in range(B):
        m = dict(shared_in)
        m["x"] = cc(x[c])
        m["wgT"] = cc(Wg[2 * c:2 * c + 2].transpose(0, 2, 1))
        m["wuT"] = cc(Wu[2 * c:2 * c + 2].transpose(0, 2, 1))
        m["wdT"] = cc(Wd[2 * c:2 * c + 2].transpose(0, 2, 1))
        m["eidx"] = _eidx_table(c)
        in_maps.append(m)

    if "full" not in _NCCACHE:
        _NCCACHE["full"] = build()
    nc = _NCCACHE["full"]
    res = run_bass_kernel_spmd(nc, in_maps, core_ids=list(range(B)))
    out = np.empty((B, T, DM), np.float32)
    for c in range(B):
        out[c] = res.results[c]["out"]
    return out


# version 05 (final)
# speedup vs baseline: 1.5958x; 1.0361x over previous
import numpy as np
import ml_dtypes
import concourse.mybir as mybir
from concourse.bacc import Bacc
from concourse.bass_utils import run_bass_kernel_spmd
from contextlib import ExitStack
from bass_rust import ReduceOp

F32 = mybir.dt.float32
BF16 = mybir.dt.bfloat16
I16 = mybir.dt.int16
U8 = mybir.dt.uint8
U32 = mybir.dt.uint32
AF = mybir.ActivationFunctionType
ALU = mybir.AluOpType
AX = mybir.AxisListType
BF = ml_dtypes.bfloat16

B = 8
DM = 1024
NH = 16
LAT = 256
NE = 16
EPC = 2
DC = DM // 128
LC = LAT // 128
EPS = 1e-5
T = 4096
TB = 512
NTB = T // TB
NTC = TB // 128
TALL = B * T
ISZ = 4096
SHZ = 2048
ICH = ISZ // 128
SIC = SHZ // 128
CAPK = 4096
GCH = 512
NGC = CAPK // GCH
CAPQ = 128
CAP2 = 320
BISN = 50
NX = T // 128
FQ = TALL // 16 // 4


class Rec:
    STREAMS = ("sync", "scalar", "vector", "tensor", "gpsimd")

    def __init__(self, nc, ctx):
        self.nc = nc
        self.sem = {}
        for s in ("sync", "scalar", "vector", "tensor", "gpsimd",
                  "qsync", "qscal", "qgp", "cc", "fence"):
            self.sem[s] = ctx.enter_context(nc.semaphore("sem_" + s))
        self.cnt = {k: 0 for k in self.sem}
        self.chain_stream = {}
        self.tasks = {s: [] for s in self.STREAMS}
        self.writer = {}
        self.readers = {}

    def task(self, stream, emit, reads=(), writes=(), sem=None, inc=1):
        sname = sem or stream
        deps = {}
        for r in reads:
            w = self.writer.get(r)
            if w:
                deps[w[0]] = max(deps.get(w[0], 0), w[1])
        for r in writes:
            w = self.writer.get(r)
            if w:
                deps[w[0]] = max(deps.get(w[0], 0), w[1])
            for rd in self.readers.get(r, ()):
                deps[rd[0]] = max(deps.get(rd[0], 0), rd[1])
        waits, drain = [], False
        for dsem, dval in deps.items():
            if dsem == stream:
                if stream != "tensor":
                    drain = True
            else:
                waits.append((dsem, dval))
        self.cnt[sname] += inc
        val = self.cnt[sname]
        chain = None
        if sname[0] == "q" or sname == "cc":
            prev = self.chain_stream.setdefault(sname, stream)
            assert prev == stream, (sname, prev, stream)
            if val - inc > 0:
                chain = (sname, val - inc)
        self.tasks[stream].append((waits, drain, emit, self.sem[sname], inc, chain))
        token = (sname, val)
        for r in writes:
            self.writer[r] = token
            self.readers[r] = []
        for r in reads:
            self.readers.setdefault(r, []).append(token)
        return token

    def replay(self, block):
        for stream in self.STREAMS:
            tasks = self.tasks[stream]
            if not tasks:
                continue

            def mk(tasks):
                def body(eng):
                    for waits, drain, emit, semh, inc, chain in tasks:
                        for dsem, dval in waits:
                            eng.wait_ge(self.sem[dsem], dval)
                        if chain is not None:
                            eng.wait_ge(self.sem[chain[0]], chain[1])
                        if drain:
                            eng.drain()
                        instr = emit(eng)
                        if instr is not None:
                            instr.then_inc(semh, inc)
                        else:
                            eng.nop().then_inc(semh, inc)
                return body
            getattr(block, stream)(mk(tasks))


def build(dbg=None):
    nc = Bacc("TRN2", target_bir_lowering=False, debug=False)
    ctx = ExitStack()
    sb = lambda name, shape, dt: ctx.enter_context(nc.sbuf_tensor(name, shape, dt))
    ps = lambda name: ctx.enter_context(nc.psum_tensor(name, [128, 512], F32))

    di = lambda n, s: nc.dram_tensor(n, s, F32, kind="ExternalInput").ap()
    d_x = di("x", [T, DM])
    d_wdq = di("wdq", [LAT, DM])
    d_wdqT = di("wdqT", [DM, LAT])
    d_wuq = di("wuq", [DM, LAT])
    d_wuk = di("wuk", [DM, LAT])
    d_wdkvT = di("wdkvT", [DM, LAT])
    d_wuv = di("wuv", [DM, LAT])
    d_woT = di("woT", [DM, DM])
    d_gwT = di("gwT", [DM, NE])
    d_wgT = di("wgT", [EPC, DM, ISZ])
    d_wuT = di("wuT", [EPC, DM, ISZ])
    d_wdT = di("wdT", [EPC, ISZ, DM])
    d_sgT = di("sgT", [DM, SHZ])
    d_suT = di("suT", [DM, SHZ])
    d_sdT = di("sdT", [SHZ, DM])
    d_masks = nc.dram_tensor("masks", [128, 896], BF16, kind="ExternalInput").ap()
    d_identf = di("identf", [128, 128])
    d_tabN = di("tabN", [16, TALL // 16])
    d_tabP = di("tabP", [16, TALL // 16])
    d_slotpos = di("slotpos", [16, CAPQ])

    d_out = nc.dram_tensor("out", [T, DM], F32, kind="ExternalOutput").ap()

    d_x1f = nc.dram_tensor("x1f", [T, DM], F32, kind="Internal").ap()
    d_agin = nc.dram_tensor("agin", [T, DM], BF16, kind="Internal").ap()
    d_x1all = nc.dram_tensor("x1all", [TALL, DM], BF16, kind="Internal").ap()
    d_scT = nc.dram_tensor("scT", [NE, T], F32, kind="Internal").ap()
    d_scA = nc.dram_tensor("scA", [B * NE, T], F32, kind="Internal").ap()
    d_scB = nc.dram_tensor("scB", [EPC, B, T], F32, kind="Internal").ap()
    d_eidx = nc.dram_tensor("eidx", [128, 8], I16, kind="ExternalInput").ap()
    d_acc = nc.dram_tensor("acc", [TALL, DM], BF16, kind="Internal").ap()
    d_moe = nc.dram_tensor("moe", [T, DM], BF16, kind="Internal").ap()
    d_sh = nc.dram_tensor("sh", [T, DM], BF16, kind="Internal").ap()
    d_wgB = nc.dram_tensor("wgB", [EPC, ISZ // 128, 128, DM], BF16, kind="Internal").ap()
    d_wuB = nc.dram_tensor("wuB", [EPC, ISZ // 128, 128, DM], BF16, kind="Internal").ap()
    d_wdB = nc.dram_tensor("wdB", [EPC, ISZ, DM], BF16, kind="Internal").ap()
    d_sgB = nc.dram_tensor("sgB", [SHZ // 128, 128, DM], BF16, kind="Internal").ap()
    d_suB = nc.dram_tensor("suB", [SHZ // 128, 128, DM], BF16, kind="Internal").ap()
    d_sdB = nc.dram_tensor("sdB", [SHZ, DM], BF16, kind="Internal").ap()
    d_wflat = nc.dram_tensor("wflat", [EPC, CAPK], F32, kind="Internal").ap()

    RGRP = [list(range(B))]

    t_big = sb("t_big", [128, DC * T], BF16)
    bigv = t_big.rearrange("p (c t) -> p c t", c=DC)
    t_ckvT = sb("t_ckvT", [128, LC * T], BF16)
    ckvT = t_ckvT.rearrange("p (c s) -> p c s", c=LC)
    wd0b = t_ckvT.rearrange("p (c d) -> p c d", c=16)
    t_ckvtm = sb("t_ckvtm", [128, NX * LAT], BF16)
    ckvtm = t_ckvtm.rearrange("p (c l) -> p c l", l=LAT)
    eov = t_ckvtm[:, 0:NTC * DM].rearrange("p (c d) -> p c d", c=NTC)
    hsl = t_ckvtm.rearrange("p (c t) -> p c t", c=SIC)
    t_cqT = sb("t_cqT", [128, LC * T], BF16)
    cqT = t_cqT.rearrange("p (c t) -> p c t", c=LC)
    xgv = t_cqT.rearrange("p (a c t) -> p a c t", a=2, c=DC)
    t_mh = sb("t_mh", [128, NH * 2 * LAT], BF16)
    mhv = t_mh.rearrange("p (h a l) -> p h a l", h=NH, a=2)
    wd0a = t_mh.rearrange("p (c d) -> p c d", c=16)
    t_hal = sb("t_hal", [128, 8192], BF16)
    wmvv = t_hal.rearrange("p (c d) -> p c d", c=DC)
    amatv = t_hal[:, 0:512].rearrange("p (c l) -> p c l", c=2)
    keffv = t_hal[:, 512:2560].rearrange("p (c l) -> p c l", c=DC)
    t_y = sb("t_y", [128, NTC * DM], BF16)
    yv = t_y.rearrange("p (c d) -> p c d", c=NTC)
    t_masks = sb("t_masks", [128, 896], BF16)

    def maskap(m):
        return t_masks[:, (3 - m) * 128:(3 - m) * 128 + TB]
    t_veff = sb("t_veff", [128, LC * DM], BF16)
    veffv = t_veff.rearrange("p (c d) -> p c d", c=LC)

    def Hs(ic):
        if ic < 16:
            return t_hal[:, ic * GCH:(ic + 1) * GCH]
        if ic < 24:
            return t_y[:, (ic - 16) * GCH:(ic - 15) * GCH]
        if ic < 28:
            return t_veff[:, (ic - 24) * GCH:(ic - 23) * GCH]
        if ic < 30:
            return t_qlat[:, (ic - 28) * GCH:(ic - 27) * GCH]
        return t_E[:, (ic - 30) * GCH:(ic - 29) * GCH]

    t_qlat = sb("t_qlat", [128, LC * TB], BF16)
    qlatv = t_qlat.rearrange("p (c t) -> p c t", c=LC)
    mov = t_hal.rearrange("p (a d) -> p a d", a=8)
    t_E = sb("t_E", [128, 2 * TB], BF16)
    Ev = t_E.rearrange("p (a t) -> p a t", a=2)
    t_ctx = sb("t_ctx", [128, LC * TB], BF16)
    ctxv = t_ctx.rearrange("p (c t) -> p c t", c=LC)

    def Shb(par):
        return (t_E if par == 0 else t_ctx)[:, 0:DM]

    t_acc = sb("t_acc", [128, TB], F32)
    t_r1 = sb("t_r1", [128, DM], F32)
    t_x1b = sb("t_x1b", [128, DM], BF16)
    t_sq = sb("t_sq", [128, DM], F32)
    scov = t_sq[:, 0:NX * NE].rearrange("p (c e) -> p c e", e=NE)
    cmpS = t_sq[:, NX * NE:NX * NE + 256]
    t_st = sb("t_st", [128, 16], F32)
    t_ones = sb("t_ones", [128, 4], F32)
    t_dcol = sb("t_dcol", [128, 2 * NTC], F32)
    t_identb = sb("t_identb", [128, 128], BF16)
    t_identf = sb("t_identf", [128, 128], F32)
    t_zero = sb("t_zero", [128, 512], BF16)
    t_gw = sb("t_gw", [128, DC * NE], BF16)
    gwv = t_gw.rearrange("p (c e) -> p c e", c=DC)
    t_pack = sb("t_pack", [128, 4096], F32)
    valq = t_pack[0:16, 0:FQ]
    s1q = t_pack[0:16, 512:1024]
    s2q = t_pack[0:16, 1024:1024 + CAP2]
    slpq = t_pack[0:16, 1408:1408 + CAPQ]
    neg1 = t_pack[0:16, 1536:2048]
    sctq = t_pack[0:16, 2048:2176]
    scwv = t_pack[32:48, :].rearrange("q (e f) -> q e f", e=EPC)
    tabv = t_pack[64:80, :].rearrange("q (a f) -> q a f", a=2)
    t_scp = sb("t_scp", [128, EPC * 256], F32)
    scpv = t_scp.rearrange("p (e f) -> p e f", e=EPC)
    t_msk = sb("t_msk", [16, FQ], U8)
    t_msk1 = sb("t_msk1", [16, 4 * CAPQ], U8)
    t_lo = sb("t_lo", [128, EPC], F32)
    t_hi = sb("t_hi", [128, EPC], F32)
    t_mid = sb("t_mid", [128, EPC], F32)
    t_prt = sb("t_prt", [128, EPC], F32)
    t_tot = sb("t_tot", [128, EPC], F32)
    t_pru = sb("t_pru", [128, EPC], U8)
    t_npr = sb("t_npr", [128, EPC], U8)
    t_nf = sb("t_nf", [1, 16], U32)
    t_nff = sb("t_nff", [1, 4], F32)
    t_nfb = sb("t_nfb", [128, 4], F32)
    t_eidx = sb("t_eidx", [128, 8], I16)
    t_idx = sb("t_idx", [128, EPC * 2 * (CAPK // 16)], I16)
    idxv = t_idx.rearrange("p (e a f) -> p e a f", e=EPC, a=2)
    t_wcol = sb("t_wcol", [128, EPC * (CAPK // 128)], F32)
    wcolv = t_wcol.rearrange("p (e f) -> p e f", e=EPC)
    t_wg0 = sb("t_wg0", [128, 2 * DC * 128], BF16)
    wg0 = t_wg0.rearrange("p (a c i) -> p a c i", a=2, c=DC)
    wlv = t_wg0.rearrange("p (c l) -> p c l", c=DC)
    t_wu0 = sb("t_wu0", [128, 2 * DC * 128], BF16)
    wu0 = t_wu0.rearrange("p (a c i) -> p a c i", a=2, c=DC)
    wl2v = t_wu0.rearrange("p (c l) -> p c l", c=DC)
    wl2q = t_wu0.rearrange("p (c d) -> p c d", c=2)

    p_s0, p_s1 = ps("p_s0"), ps("p_s1")
    p_c0, p_c1 = ps("p_c0"), ps("p_c1")
    p_g0, p_g1 = ps("p_g0"), ps("p_g1")
    p_y0, p_y1 = ps("p_y0"), ps("p_y1")
    PB = {"ps0": p_s0, "ps1": p_s1, "pc0": p_c0, "pc1": p_c1,
          "pg0": p_g0, "pg1": p_g1, "py0": p_y0, "py1": p_y1}

    R = Rec(nc, ctx)
    TK = R.task

    def FENCE(keys):
        TK("vector", lambda e: None, writes=list(keys), sem="fence")

    accz = d_acc.rearrange("(n p a) d -> n p (a d)", p=128, a=1).rearrange("n p f -> n p f") if False else d_acc.rearrange("(n p) d -> n p d", p=64).rearrange("n p (a f) -> n (p a) f", a=2)
    NZERO = TALL * DM // (128 * 512)
    zq = [0]

    def emit_zero(n):
        for _ in range(n):
            i = zq[0]
            if i >= NZERO:
                return
            zq[0] += 1
            TK("scalar", (lambda i: lambda e: e.dma_start(out=accz[i, :, :], in_=t_zero[:, :]))(i),
               reads=["zero"], writes=["accD"], sem="qscal", inc=16)

    TK("sync", lambda e: e.dma_start(out=t_masks[:, :], in_=d_masks[:, :]),
       writes=["masks"], sem="qsync", inc=16)
    TK("sync", lambda e: e.dma_start(out=t_identf[:, :], in_=d_identf[:, :]),
       writes=["identf"], sem="qsync", inc=16)
    TK("sync", lambda e: e.dma_start(out=tabv[:, 0, :], in_=d_tabN[:, :]),
       writes=["tabN"], sem="qsync", inc=16)
    TK("sync", lambda e: e.dma_start(out=tabv[:, 1, :], in_=d_tabP[:, :]),
       writes=["tabP"], sem="qsync", inc=16)
    TK("sync", lambda e: e.dma_start(out=slpq, in_=d_slotpos[:, :]),
       writes=["slotpos"], sem="qsync", inc=16)
    TK("vector", lambda e: e.memset(t_ones[:, :], 1.0), writes=["ones"])
    TK("vector", lambda e: e.memset(neg1, -1.0), writes=["neg1"])
    TK("vector", lambda e: e.memset(t_zero[:, :], 0.0), writes=["zero"])
    TK("vector", lambda e: e.memset(t_st[:, 8:9], EPS), writes=["epsc"])
    TK("vector", lambda e: e.tensor_copy(out=t_identb[:, :], in_=t_identf[:, :]),
       reads=["identf"], writes=["identb"])

    for ch in range(NX):
        xsrc = t_r1 if ch % 2 == 0 else t_sq
        xkey = "r1" if ch % 2 == 0 else "sq"
        TK("sync", (lambda ch, xsrc: lambda e: e.dma_start(
            out=xsrc[:, :], in_=d_x[ch * 128:(ch + 1) * 128, :]))(ch, xsrc),
            writes=[xkey], sem="qsync", inc=16)
        pka, pkb = ("pg0", "pg1") if ch % 2 == 0 else ("py0", "py1")

        def mk_tr(ch, xsrc, pka, pkb):
            def f(e):
                pa, pb = PB[pka], PB[pkb]
                mm = None
                for dc in range(DC):
                    dst = (pa if dc < 4 else pb)[:, (dc % 4) * 128:(dc % 4) * 128 + 128]
                    mm = e.transpose(dst, xsrc[:, dc * 128:(dc + 1) * 128], t_identf[:, :])
                return mm
            return f
        TK("tensor", mk_tr(ch, xsrc, pka, pkb), reads=[xkey, "identf"],
           writes=[pka, pkb])

        def mk_ev(ch, pka, pkb):
            def f(e):
                e.tensor_copy(out=bigv[:, 0:4, ch * 128:(ch + 1) * 128],
                              in_=PB[pka][:, :].rearrange("p (c t) -> p c t", c=4))
                e.drain()
                return e.tensor_copy(out=bigv[:, 4:8, ch * 128:(ch + 1) * 128],
                                     in_=PB[pkb][:, :].rearrange("p (c t) -> p c t", c=4))
            return f
        TK("vector", mk_ev(ch, pka, pkb), reads=[pka, pkb], writes=[f"xT{ch}"])

    TK("gpsimd", lambda e: e.dma_start(out=wl2v[:, :, :], in_=d_wuv.rearrange("(c p) l -> p c l", p=128)),
       writes=["wl2"], sem="qgp", inc=16)
    TK("gpsimd", lambda e: e.dma_start(out=wmvv[:, :, :], in_=d_woT.rearrange("(c p) d -> p c d", p=128)),
       writes=["wmv"], sem="qgp", inc=16)
    for lc in range(LC):
        for dh in range(2):
            def mk_ve(lc, dh):
                def f(e):
                    mm = None
                    for dc in range(DC):
                        mm = e.matmul(p_c1[:, :], wl2v[:, dc, lc * 128:(lc + 1) * 128],
                                      wmvv[:, dc, dh * 512:(dh + 1) * 512],
                                      start=(dc == 0), stop=(dc == DC - 1))
                    return mm
                return f
            TK("tensor", mk_ve(lc, dh), reads=["wl2", "wmv"], writes=["pc1"])
            TK("vector", (lambda lc, dh: lambda e: e.tensor_copy(
                out=veffv[:, lc, dh * 512:(dh + 1) * 512], in_=p_c1[:, :]))(lc, dh),
                reads=["pc1"], writes=["veff"])

    TK("gpsimd", lambda e: e.dma_start(out=wlv[:, :, :], in_=d_wuq.rearrange("(c p) l -> p c l", p=128)),
       writes=["wl"], sem="qgp", inc=16)
    TK("gpsimd", lambda e: e.dma_start(out=wl2v[:, :, :], in_=d_wuk.rearrange("(c p) l -> p c l", p=128)),
       writes=["wl2"], sem="qgp", inc=16)

    def mm_amat(e):
        mm = None
        for qc in range(2):
            for dc in range(DC):
                mm = e.matmul(p_c0[:, qc * 256:(qc + 1) * 256],
                              wlv[:, dc, qc * 128:(qc + 1) * 128], wl2v[:, dc, :],
                              start=(dc == 0), stop=(dc == DC - 1))
        return mm
    TK("tensor", mm_amat, reads=["wl", "wl2"], writes=["pc0"])
    TK("vector", lambda e: e.tensor_copy(out=t_hal[:, 0:512], in_=p_c0[:, :]),
       reads=["pc0"], writes=["amat", "wmv"])

    TK("gpsimd", lambda e: e.dma_start(out=wl2q[:, :, :], in_=d_wdq.rearrange("(c p) d -> p c d", p=128)),
       writes=["wl2"], sem="qgp", inc=16)
    for dc in range(DC):
        def mk_keff(dc):
            def f(e):
                mm = None
                for qc in range(2):
                    mm = e.matmul(p_c1[:, (dc % 2) * 256:(dc % 2) * 256 + 256],
                                  wl2q[:, qc, dc * 128:(dc + 1) * 128], amatv[:, qc, :],
                                  start=(qc == 0), stop=(qc == 1))
                return mm
            return f
        TK("tensor", mk_keff(dc), reads=["wl2", "amat"], writes=["pc1"])
        TK("vector", (lambda dc: lambda e: e.tensor_copy(
            out=keffv[:, dc, :], in_=p_c1[:, (dc % 2) * 256:(dc % 2) * 256 + 256]))(dc),
            reads=["pc1"], writes=[f"keff{dc}", "wmv"])

    for h in range(NH):
        hc, hp = h // 2, (h % 2) * 64
        for qh in range(2):
            def mk_mh(hc, hp, qh):
                def f(e):
                    return e.matmul(p_c0[:, (qh % 2) * 256:(qh % 2) * 256 + 256],
                                    wlv[hp:hp + 64, hc, qh * 128:(qh + 1) * 128],
                                    keffv[hp:hp + 64, hc, :], start=True, stop=True)
                return f
            TK("tensor", mk_mh(hc, hp, qh), reads=["wl", f"keff{hc}"], writes=["pc0"])
            TK("vector", (lambda h, qh: lambda e: e.tensor_copy(
                out=mhv[:, h, qh, :], in_=p_c0[:, (qh % 2) * 256:(qh % 2) * 256 + 256]))(h, qh),
                reads=["pc0"], writes=["mh"])

    TK("gpsimd", lambda e: e.dma_start(out=wlv[:, :, :], in_=d_wdqT.rearrange("(c p) l -> p c l", p=128)),
       writes=["wl"], sem="qgp", inc=16)
    TK("gpsimd", lambda e: e.dma_start(out=wl2v[:, :, :], in_=d_wdkvT.rearrange("(c p) l -> p c l", p=128)),
       writes=["wl2"], sem="qgp", inc=16)
    TK("gpsimd", lambda e: e.dma_start(out=gwv[:, :, :], in_=d_gwT.rearrange("(c p) e -> p c e", p=128)),
       writes=["gw"], sem="qgp", inc=16)

    for name, wkey, wsrc, dst, dkey in (("cq", "wl", wlv, cqT, "cqT"),
                                        ("ckv", "wl2", wl2v, ckvT, "ckvT")):
        for qc in range(LC):
            for tbl in range(T // 512):
                pk = "pg0" if tbl % 2 == 0 else "pg1"

                def mk_cq(wsrc, qc, tbl, pk):
                    def f(e):
                        mm = None
                        for dc in range(DC):
                            mm = e.matmul(PB[pk][:, :],
                                          wsrc[:, dc, qc * 128:(qc + 1) * 128],
                                          bigv[:, dc, tbl * 512:(tbl + 1) * 512],
                                          start=(dc == 0), stop=(dc == DC - 1))
                        return mm
                    return f
                rd = [wkey] + [f"xT{ch}" for ch in range(tbl * 4, tbl * 4 + 4)]
                TK("tensor", mk_cq(wsrc, qc, tbl, pk), reads=rd, writes=[pk])
                TK("vector", (lambda dst, qc, tbl, pk: lambda e: e.tensor_copy(
                    out=dst[:, qc, tbl * 512:(tbl + 1) * 512], in_=PB[pk][:, :]))(dst, qc, tbl, pk),
                    reads=[pk], writes=[f"{dkey}{qc}_{tbl}"])

    for sc in range(NX):
        pk = "py0" if sc % 2 == 0 else "py1"

        def mk_tm(sc, pk):
            def f(e):
                mm = None
                for dc in range(DC):
                    mm = e.matmul(PB[pk][:, 0:LAT],
                                  bigv[:, dc, sc * 128:(sc + 1) * 128],
                                  wl2v[:, dc, :],
                                  start=(dc == 0), stop=(dc == DC - 1))
                return mm
            return f
        TK("tensor", mk_tm(sc, pk),
           reads=["wl2", f"xT{sc}"], writes=[pk])
        TK("vector", (lambda sc, pk: lambda e: e.tensor_copy(
            out=ckvtm[:, sc, :], in_=PB[pk][:, 0:LAT]))(sc, pk),
            reads=[pk], writes=[f"ckvtm{sc}"])

    for nm, src, dstd in (("wgB", d_wgT, d_wgB), ("wuB", d_wuT, d_wuB)):
        for ee in range(EPC):
            for icq in range(ICH):
                TK("gpsimd", (lambda src, dstd, ee, icq: lambda e: e.dma_start(
                    out=dstd[ee, icq, :, :].rearrange("p (c i) -> p c i", c=DC),
                    in_=src[ee, :, icq * 128:(icq + 1) * 128]
                    .rearrange("(c p) i -> p c i", p=128)))(src, dstd, ee, icq),
                    sem="qgp", inc=16, writes=[f"{nm}{ee}"])
    for ee in range(EPC):
        for half in range(4):
            TK("gpsimd", (lambda ee, half: lambda e: e.dma_start(
                out=d_wdB[ee, half * (ISZ // 4):(half + 1) * (ISZ // 4), :],
                in_=d_wdT[ee, half * (ISZ // 4):(half + 1) * (ISZ // 4), :],
                max_dma_last_dim=8192))(ee, half),
                sem="qgp", inc=16, writes=[f"wdB{ee}"])
    for nm, src, dstd in (("sgB", d_sgT, d_sgB), ("suB", d_suT, d_suB)):
        for icq in range(SIC):
            TK("gpsimd", (lambda src, dstd, icq: lambda e: e.dma_start(
                out=dstd[icq, :, :].rearrange("p (c i) -> p c i", c=DC),
                in_=src[:, icq * 128:(icq + 1) * 128]
                .rearrange("(c p) i -> p c i", p=128)))(src, dstd, icq),
                sem="qgp", inc=16, writes=[nm])
    TK("gpsimd", lambda e: e.dma_start(out=d_sdB[:, :], in_=d_sdT[:, :], max_dma_last_dim=8192),
       sem="qgp", inc=16, writes=["sdB"])

    INV8 = 1.0 / np.sqrt(DM / NH)
    E2v = t_x1b.rearrange("p (a t) -> p a t", a=2)
    accBt = t_sq[:, 0:TB]
    XKEYS = ["x1b", "EB0", "EB1", "sq", "accB"]
    qlatB = ctxv

    def emit_exp(psk, dstv, par, key):
        TK("scalar", (lambda psk, par: lambda e: e.activation(
            out=dstv[:, par, :], in_=PB[psk][:, 0:TB], func=AF.Exp, scale=INV8))(psk, par),
            reads=[psk], writes=[f"{key}{par}"])

    def emit_mask_acc(i, sc, dstv, par, key, acct, akey):
        diag = sc >= 4 * i
        if diag:
            TK("vector", (lambda i, sc, par: lambda e: e.tensor_tensor(
                out=dstv[:, par, :], in0=dstv[:, par, :],
                in1=maskap(sc - 4 * i), op=ALU.mult))(i, sc, par),
                reads=["masks"], writes=[f"{key}{par}"])
        if sc == 0:
            TK("vector", (lambda par: lambda e: e.tensor_copy(
                out=acct[:, :], in_=dstv[:, par, :]))(par),
                reads=[f"{key}{par}"], writes=[akey])
        else:
            TK("vector", (lambda par: lambda e: e.tensor_tensor(
                out=acct[:, :], in0=acct[:, :], in1=dstv[:, par, :], op=ALU.add))(par),
                reads=[f"{key}{par}"], writes=[akey])

    for i in range(NTB):
        nsc = (i + 1) * NTC
        FENCE(XKEYS)
        for hp in range(NH // 2):
            hA, hB = 2 * hp, 2 * hp + 1
            FENCE(["ctx", "qlatB"])
            for h, banks, qdst, qkey in ((hA, ("ps0", "ps1"), qlatv, "qlat"),
                                         (hB, ("py0", "py1"), qlatB, "qlatB")):
                for lcc in range(LC):
                    qpk = banks[lcc]

                    def mk_ql(h, lcc, qpk):
                        def f(e):
                            mm = None
                            for qh in range(2):
                                mm = e.matmul(PB[qpk][:, 0:TB],
                                              mhv[:, h, qh, lcc * 128:(lcc + 1) * 128],
                                              cqT[:, qh, i * TB:(i + 1) * TB],
                                              start=(qh == 0), stop=(qh == 1))
                            return mm
                        return f
                    TK("tensor", mk_ql(h, lcc, qpk),
                       reads=["mh", f"cqT0_{i}", f"cqT1_{i}"], writes=[qpk])
                    TK("vector", (lambda lcc, qpk, qdst: lambda e: e.tensor_copy(
                        out=qdst[:, lcc, :], in_=PB[qpk][:, 0:TB]))(lcc, qpk, qdst),
                        reads=[qpk], writes=[qkey])

            def mk_sc(sc, psA, psB):
                def f(e):
                    mm = None
                    for lcc in range(LC):
                        st = ckvT[:, lcc, sc * 128:(sc + 1) * 128]
                        mm = e.matmul(PB[psA][:, 0:TB], st, qlatv[:, lcc, :],
                                      start=(lcc == 0), stop=(lcc == 1))
                        mm = e.matmul(PB[psB][:, 0:TB], st, qlatB[:, lcc, :],
                                      start=(lcc == 0), stop=(lcc == 1))
                    return mm
                return f

            def mk_cx(sc, par, nsc):
                def f(e):
                    mm = None
                    for lcc in range(LC):
                        st = ckvtm[:, sc, lcc * 128:(lcc + 1) * 128]
                        pA = (p_c0 if lcc == 0 else p_c1)
                        pB = PB["pg0" if lcc == 0 else "pg1"]
                        mm = e.matmul(pA[:, 0:TB], st, Ev[:, par, :],
                                      start=(sc == 0), stop=(sc == nsc - 1))
                        mm = e.matmul(pB[:, 0:TB], st, E2v[:, par, :],
                                      start=(sc == 0), stop=(sc == nsc - 1))
                    return mm
                return f

            def emit_scores(sc):
                par = sc % 2
                psA = "ps0" if par == 0 else "ps1"
                psB = "py0" if par == 0 else "py1"
                TK("tensor", mk_sc(sc, psA, psB),
                   reads=["qlat", "qlatB", f"ckvT0_{sc // 4}", f"ckvT1_{sc // 4}"],
                   writes=[psA, psB])
                emit_exp(psA, Ev, par, "EA")
                emit_exp(psB, E2v, par, "EB")
                emit_mask_acc(i, sc, Ev, par, "EA", t_acc, "accS")
                emit_mask_acc(i, sc, E2v, par, "EB", accBt, "accB")

            def emit_ctx(sc):
                par = sc % 2
                TK("tensor", mk_cx(sc, par, nsc),
                   reads=[f"EA{par}", f"EB{par}", f"ckvtm{sc}"],
                   writes=["pc0", "pc1", "pg0", "pg1"])

            for sc in range(nsc):
                emit_scores(sc)
                if sc >= 1:
                    emit_ctx(sc - 1)
            emit_ctx(nsc - 1)

            for hh, acct, akey, coff in ((hA, t_acc, "accS", 0), (hB, accBt, "accB", NTC)):
                for tc in range(NTC):
                    dpk = "py0" if tc % 2 == 0 else "py1"
                    TK("tensor", (lambda tc, dpk, acct: lambda e: e.matmul(
                        PB[dpk][:, 0:1], acct[:, tc * 128:(tc + 1) * 128], t_ones[:, 0:1],
                        start=True, stop=True))(tc, dpk, acct),
                        reads=[akey, "ones"], writes=[dpk])
                    TK("vector", (lambda tc, dpk, coff: lambda e: e.reciprocal(
                        out=t_dcol[:, coff + tc:coff + tc + 1], in_=PB[dpk][:, 0:1]))(tc, dpk, coff),
                        reads=[dpk], writes=[f"dcol{coff + tc}"])

            FENCE(["qlatB", "ctx"])
            for hh, pk0, pk1, coff in ((hA, "pc0", "pc1", 0), (hB, "pg0", "pg1", NTC)):
                def mk_evcx(pk0, pk1):
                    def f(e):
                        e.tensor_copy(out=ctxv[:, 0, :], in_=PB[pk0][:, 0:TB])
                        e.drain()
                        return e.tensor_copy(out=ctxv[:, 1, :], in_=PB[pk1][:, 0:TB])
                    return f
                TK("vector", mk_evcx(pk0, pk1), reads=[pk0, pk1], writes=["ctx"])
                for tc in range(NTC):
                    ypk = "py0" if tc % 2 == 0 else "py1"

                    def mk_y(hh, tc, ypk):
                        def f(e):
                            mm = None
                            for lcc in range(LC):
                                mm = e.matmul(PB[ypk][:, 64:128],
                                              ctxv[:, lcc, tc * 128:(tc + 1) * 128],
                                              veffv[:, lcc, hh * 64:(hh + 1) * 64],
                                              start=(lcc == 0), stop=(lcc == 1))
                            return mm
                        return f
                    TK("tensor", mk_y(hh, tc, ypk), reads=["ctx", "veff"], writes=[ypk])
                    TK("vector", (lambda hh, tc, ypk, coff: lambda e: e.tensor_scalar(
                        out=yv[:, tc, hh * 64:(hh + 1) * 64], in0=PB[ypk][:, 64:128],
                        scalar1=t_dcol[:, coff + tc:coff + tc + 1], scalar2=None,
                        op0=ALU.mult))(hh, tc, ypk, coff),
                        reads=[ypk, f"dcol{coff + tc}"], writes=[f"ybuf{tc}"])
            emit_zero(8)

        FENCE(XKEYS)
        for tc in range(NTC):
            gtc = i * NTC + tc
            TK("sync", (lambda gtc: lambda e: e.dma_start(
                out=t_r1[:, :], in_=d_x[gtc * 128:(gtc + 1) * 128, :]))(gtc),
                writes=["r1"], sem="qsync", inc=16)

            def mk_lnA(tc):
                def f(e):
                    e.tensor_tensor(out=t_r1[:, :], in0=t_r1[:, :], in1=yv[:, tc, :], op=ALU.add)
                    e.drain()
                    e.reduce_sum(out=t_st[:, 0:1], in_=t_r1[:, :], axis=AX.X)
                    e.drain()
                    e.tensor_scalar(out=t_st[:, 1:2], in0=t_st[:, 0:1], scalar1=1.0 / DM,
                                    scalar2=None, op0=ALU.mult)
                    e.drain()
                    return e.tensor_scalar(out=t_sq[:, :], in0=t_r1[:, :], scalar1=t_st[:, 1:2],
                                           scalar2=None, op0=ALU.subtract)
                return f
            TK("vector", mk_lnA(tc), reads=[f"ybuf{tc}"], writes=["r1", "sq", "st"])

            def mk_lnB(e):
                e.activation(out=t_sq[:, :], in_=t_sq[:, :], func=AF.Square,
                             accum_out=t_st[:, 2:3])
                e.drain()
                e.activation(out=t_st[:, 3:4], in_=t_st[:, 2:3], func=AF.Ln,
                             scale=1.0 / DM, bias=t_st[:, 8:9])
                e.drain()
                return e.activation(out=t_st[:, 4:5], in_=t_st[:, 3:4], func=AF.Exp, scale=-0.5)
            TK("scalar", mk_lnB, reads=["sq", "epsc"], writes=["sq", "st"])

            def mk_lnC(e):
                e.tensor_scalar(out=t_r1[:, :], in0=t_r1[:, :], scalar1=t_st[:, 1:2],
                                scalar2=t_st[:, 4:5], op0=ALU.subtract, op1=ALU.mult)
                e.drain()
                return e.tensor_copy(out=t_x1b[:, :], in_=t_r1[:, :])
            TK("vector", mk_lnC, reads=["st"], writes=["r1", "x1b"])

            TK("sync", (lambda gtc: lambda e: e.dma_start(
                out=d_x1f[gtc * 128:(gtc + 1) * 128, :], in_=t_r1[:, :]))(gtc),
                reads=["r1"], writes=["x1fD"], sem="qsync", inc=16)
            if dbg == "x1":
                TK("sync", (lambda gtc: lambda e: e.dma_start(
                    out=d_out[gtc * 128:(gtc + 1) * 128, :], in_=t_r1[:, :]))(gtc),
                    reads=["r1"], writes=["outD"], sem="qsync", inc=16)
            TK("sync", (lambda gtc: lambda e: e.dma_start(
                out=d_agin[gtc * 128:(gtc + 1) * 128, :], in_=t_x1b[:, :]))(gtc),
                reads=["x1b"], writes=[f"aginD{i}"], sem="qsync", inc=16)

            pka, pkb = ("pg0", "pg1") if gtc % 2 == 0 else ("py0", "py1")

            def mk_tx1(pka, pkb):
                def f(e):
                    mm = None
                    for dc in range(DC):
                        dst = (PB[pka] if dc < 4 else PB[pkb])[:, (dc % 4) * 128:(dc % 4) * 128 + 128]
                        mm = e.transpose(dst, t_r1[:, dc * 128:(dc + 1) * 128], t_identf[:, :])
                    return mm
                return f
            TK("tensor", mk_tx1(pka, pkb), reads=["r1", "identf"], writes=[pka, pkb])

            def mk_evx1(gtc, pka, pkb):
                def f(e):
                    e.tensor_copy(out=bigv[:, 0:4, gtc * 128:(gtc + 1) * 128],
                                  in_=PB[pka][:, :].rearrange("p (c t) -> p c t", c=4))
                    e.drain()
                    return e.tensor_copy(out=bigv[:, 4:8, gtc * 128:(gtc + 1) * 128],
                                         in_=PB[pkb][:, :].rearrange("p (c t) -> p c t", c=4))
                return f
            TK("vector", mk_evx1(gtc, pka, pkb), reads=[pka, pkb], writes=[f"xT{gtc}"])

        TK("gpsimd", (lambda i: lambda e: e.collective_compute(
            "AllGather", ALU.bypass, replica_groups=RGRP,
            ins=[d_agin[i * TB:(i + 1) * TB, :]],
            outs=[d_x1all[i * B * TB:(i + 1) * B * TB, :]]))(i),
            reads=[f"aginD{i}"], writes=[f"x1allD{i}"], sem="cc", inc=1)

    if dbg == "x1":
        TK("sync", lambda e: None, reads=["outD", "x1fD"])
        block = ctx.enter_context(nc.Block())
        R.replay(block)
        ctx.close()
        nc.finalize()
        return nc

    FENCE(["sq", "r1"])
    for tch in range(NX):
        gpk = "pg0" if tch % 2 == 0 else "pg1"

        def mk_lg(tch, gpk):
            def f(e):
                mm = None
                for dc in range(DC):
                    mm = e.matmul(PB[gpk][:, 0:NE], bigv[:, dc, tch * 128:(tch + 1) * 128],
                                  gwv[:, dc, :], start=(dc == 0), stop=(dc == DC - 1))
                return mm
            return f
        TK("tensor", mk_lg(tch, gpk), reads=[f"xT{tch}", "gw"], writes=[gpk])
        TK("vector", (lambda gpk: lambda e: e.reduce_max(
            out=t_st[:, 5:6], in_=PB[gpk][:, 0:NE], axis=AX.X, negate=True))(gpk),
            reads=[gpk], writes=["st"])
        TK("scalar", (lambda tch, gpk: lambda e: e.activation(
            out=scov[:, tch, :], in_=PB[gpk][:, 0:NE], func=AF.Exp,
            bias=t_st[:, 5:6], accum_out=t_st[:, 6:7]))(tch, gpk),
            reads=[gpk, "st"], writes=[f"sco{tch}", "st"])

        def mk_sms(tch):
            def f(e):
                e.reciprocal(out=t_st[:, 7:8], in_=t_st[:, 6:7])
                e.drain()
                return e.tensor_scalar(out=scov[:, tch, :], in0=scov[:, tch, :],
                                       scalar1=t_st[:, 7:8], scalar2=None, op0=ALU.mult)
            return f
        TK("vector", mk_sms(tch), reads=["st"], writes=[f"sco{tch}", "st"])
        spk2 = "py0" if tch % 2 == 0 else "py1"
        TK("tensor", (lambda tch, spk2: lambda e: e.transpose(
            PB[spk2][0:16, 0:128], scov[:, tch, :], t_identf[:, :]))(tch, spk2),
            reads=[f"sco{tch}", "identf"], writes=[spk2])
        TK("vector", (lambda spk2: lambda e: e.tensor_copy(
            out=sctq, in_=PB[spk2][0:16, 0:128]))(spk2),
            reads=[spk2], writes=["sct"])
        TK("sync", (lambda tch: lambda e: e.dma_start(
            out=d_scT[:, tch * 128:(tch + 1) * 128], in_=sctq))(tch),
            reads=["sct"], writes=["scTD"], sem="qsync", inc=16)

    TK("gpsimd", lambda e: e.collective_compute(
        "AllGather", ALU.bypass, replica_groups=RGRP, ins=[d_scT[:, :]], outs=[d_scA[:, :]]),
        reads=["scTD"], writes=["scAD"], sem="cc", inc=1)
    TK("sync", lambda e: e.dma_start(out=t_eidx[:, :], in_=d_eidx[:, :]),
       writes=["eidx"], sem="qsync", inc=16)
    FENCE(["accS", "gsc"])
    TK("gpsimd", lambda e: e.dma_gather(
        t_acc.rearrange("p (a f) -> p a f", a=1),
        d_scA.rearrange("r (k f) -> (r k) f", k=8),
        t_eidx[:, :], 128, 128, 512, transpose=False, single_packet=False),
        reads=["scAD", "eidx"], writes=["gsc"], sem="qgp", inc=16)
    TK("sync", lambda e: e.dma_start(
        out=d_scB.rearrange("e b (k f) -> (e b k) f", k=8), in_=t_acc[:, :]),
        reads=["gsc"], writes=["scBD"], sem="qsync", inc=16)
    for e_ in range(EPC):
        for b_ in range(B):
            TK("sync", (lambda e_, b_: lambda e: e.dma_start(
                out=scwv[2 * b_:2 * b_ + 2, e_, :],
                in_=d_scB[e_, b_, :].rearrange("(q f) -> q f", q=2)))(e_, b_),
                reads=["scBD"], writes=["scw"], sem="qsync", inc=16)
            TK("sync", (lambda e_, b_: lambda e: e.dma_start(
                out=scpv[16 * b_:16 * b_ + 16, e_, :],
                in_=d_scB[e_, b_, :].rearrange("(q f) -> q f", q=16)))(e_, b_),
                reads=["scBD"], writes=["scp"], sem="qsync", inc=16)

    FENCE(["mh"] + [f"ckvT0_{t}" for t in range(NTB)] + [f"ckvT1_{t}" for t in range(NTB)]
          + ["wl", "wl2", "x1b", "EB0", "EB1"]
          + [f"ckvtm{s}" for s in range(NX)]
          + ["wdT0", "wdT1", "wgT0", "wgT1", "wuT0", "wuT1",
             "shh0", "shh1"] + [f"hs{k}" for k in range(SIC)])
    TK("sync", lambda e: e.dma_start(
        out=wd0a[:, 0:SIC, :],
        in_=d_sdB[:, 0:512].rearrange("(c p) d -> p c d", p=128)),
        reads=["sdB"], writes=["wdT0"], sem="qsync", inc=16)
    TK("sync", lambda e: e.dma_start(
        out=wd0b[:, 0:SIC, :],
        in_=d_sdB[:, 512:1024].rearrange("(c p) d -> p c d", p=128)),
        reads=["sdB"], writes=["wdT1"], sem="qsync", inc=16)

    def shared_units():
        for i8 in range(NTB):
            for ic in range(SIC):
                wpar = ic % 2
                TK("sync", (lambda ic, wpar: lambda e: e.dma_start(
                    out=wg0[:, wpar, :, :],
                    in_=d_sgB[ic, :, :].rearrange("p (c i) -> p c i", c=DC)))(ic, wpar),
                    reads=["sgB"], writes=[f"wgT{wpar}"], sem="qsync", inc=16)
                TK("sync", (lambda ic, wpar: lambda e: e.dma_start(
                    out=wu0[:, wpar, :, :],
                    in_=d_suB[ic, :, :].rearrange("p (c i) -> p c i", c=DC)))(ic, wpar),
                    reads=["suB"], writes=[f"wuT{wpar}"], sem="qsync", inc=16)

                def mk_gs(i8, wpar, wv, pk):
                    def f(e):
                        mm = None
                        for dc in range(DC):
                            mm = e.matmul(PB[pk][:, 0:TB], wv[:, wpar, dc, :],
                                          bigv[:, dc, i8 * TB:(i8 + 1) * TB],
                                          start=(dc == 0), stop=(dc == DC - 1))
                        return mm
                    return f
                xTr = [f"xT{i8 * NTC + q}" for q in range(NTC)]
                TK("tensor", mk_gs(i8, wpar, wg0, "pg0"), reads=xTr + [f"wgT{wpar}"], writes=["pg0"])
                TK("tensor", mk_gs(i8, wpar, wu0, "pg1"), reads=xTr + [f"wuT{wpar}"], writes=["pg1"])
                TK("scalar", (lambda ic: lambda e: e.activation(
                    out=hsl[:, ic, :], in_=p_g0[:, 0:TB], func=AF.Gelu))(ic),
                    reads=["pg0"], writes=[f"hs{ic}"])
                TK("vector", (lambda ic: lambda e: e.tensor_tensor(
                    out=hsl[:, ic, :], in0=hsl[:, ic, :], in1=p_g1[:, 0:TB], op=ALU.mult))(ic),
                    reads=["pg1"], writes=[f"hs{ic}"])
                yield
            for tc4 in range(NTC):
                gtc = i8 * NTC + tc4
                for dh in range(2):
                    spk = "py0" if dh == 0 else "py1"

                    def mk_ds(tc4, dh, spk):
                        def f(e):
                            wsrc = wd0a if dh == 0 else wd0b
                            mm = None
                            for k in range(SIC):
                                mm = e.matmul(PB[spk][:, 0:512],
                                              hsl[:, k, tc4 * 128:(tc4 + 1) * 128],
                                              wsrc[:, k, :],
                                              start=(k == 0), stop=(k == SIC - 1))
                            return mm
                        return f
                    TK("tensor", mk_ds(tc4, dh, spk),
                       reads=[f"hs{k}" for k in range(SIC)] + [f"wdT{dh}"], writes=[spk])
                    TK("vector", (lambda dh, spk: lambda e: e.tensor_copy(
                        out=E2v[:, dh, :], in_=PB[spk][:, 0:512]))(dh, spk),
                        reads=[spk], writes=[f"shh{dh}"])
                    TK("sync", (lambda gtc, dh: lambda e: e.dma_start(
                        out=d_sh[gtc * 128:(gtc + 1) * 128, dh * 512:(dh + 1) * 512],
                        in_=E2v[:, dh, :]))(gtc, dh),
                        reads=[f"shh{dh}"], writes=["shD"], sem="qsync", inc=16)
                    yield

    _sh_it = shared_units()

    def pump_shared(n):
        for _ in range(n):
            if next(_sh_it, "END") == "END":
                return ()
        return ()

    TK("vector", lambda e: e.memset(t_lo[:, :], 0.0), writes=["lo"])
    TK("vector", lambda e: e.memset(t_hi[:, :], 1.0), writes=["hi"])
    for it in range(BISN):
        pump_shared(2)
        def mk_mid(e):
            e.tensor_tensor(out=t_mid[:, :], in0=t_lo[:, :], in1=t_hi[:, :], op=ALU.add)
            e.drain()
            return e.tensor_scalar(out=t_mid[:, :], in0=t_mid[:, :], scalar1=0.5,
                                   scalar2=None, op0=ALU.mult)
        TK("vector", mk_mid, reads=["lo", "hi"], writes=["mid"])
        for e_ in range(EPC):
            TK("vector", (lambda e_: lambda e: e.tensor_scalar(
                out=cmpS, in0=scpv[:, e_, :], scalar1=t_mid[:, e_:e_ + 1],
                scalar2=None, op0=ALU.is_ge, op1=ALU.add, accum_out=t_prt[:, e_:e_ + 1]))(e_),
                reads=["scp", "mid"], writes=["cmpSk", "prt"])
        TK("gpsimd", lambda e: e.partition_all_reduce(
            out_ap=t_tot[:, :], in_ap=t_prt[:, :], channels=128, reduce_op=ReduceOp.add),
            reads=["prt"], writes=["tot"])

        def mk_sel(e):
            e.tensor_scalar(out=t_pru[:, :], in0=t_tot[:, :], scalar1=float(CAPK),
                            scalar2=None, op0=ALU.is_ge)
            e.drain()
            e.tensor_scalar(out=t_npr[:, :], in0=t_tot[:, :], scalar1=float(CAPK),
                            scalar2=None, op0=ALU.is_lt)
            e.drain()
            e.copy_predicated(out=t_lo[:, :], mask=t_pru[:, :], data=t_mid[:, :])
            e.drain()
            return e.copy_predicated(out=t_hi[:, :], mask=t_npr[:, :], data=t_mid[:, :])
        TK("vector", mk_sel, reads=["tot", "mid"], writes=["lo", "hi", "pru", "npr"])

    for e_ in range(EPC):
        for s_ in range(3):
            pump_shared(3)
            for q in range(4):
                def mk_val(e_, s_, q):
                    def f(e):
                        e.tensor_scalar(out=t_msk[:, :], in0=scwv[:, e_, q * FQ:(q + 1) * FQ],
                                        scalar1=t_lo[0:16, e_:e_ + 1],
                                        scalar2=None, op0=ALU.is_lt)
                        e.drain()
                        if s_ == 0:
                            e.tensor_copy(out=valq, in_=tabv[:, 0, q * FQ:(q + 1) * FQ])
                        elif s_ == 1:
                            e.tensor_copy(out=valq, in_=tabv[:, 1, q * FQ:(q + 1) * FQ])
                        else:
                            e.tensor_copy(out=valq, in_=scwv[:, e_, q * FQ:(q + 1) * FQ])
                        e.drain()
                        return e.copy_predicated(out=valq, mask=t_msk[:, :], data=neg1)
                    return f
                TK("vector", mk_val(e_, s_, q),
                   reads=["tabN", "tabP", "scw", "neg1", "lo"], writes=["val", "msk"])
                TK("gpsimd", (lambda q: lambda e: e.sparse_gather(
                    s1q[:, q * CAPQ:(q + 1) * CAPQ], valq,
                    num_found=t_nf[:, q:q + 1]))(q),
                    reads=["val"], writes=["s1", "nf"])
            if s_ == 0:
                TK("vector", lambda e: e.tensor_copy(out=t_nff[:, :], in_=t_nf[:, 0:4]),
                   reads=["nf"], writes=["nff"])
                TK("gpsimd", lambda e: e.partition_broadcast(out_ap=t_nfb[:, :], in_ap=t_nff[:, :]),
                   reads=["nff"], writes=["nfb"])
                for q in range(4):
                    TK("vector", (lambda q: lambda e: e.tensor_scalar(
                        out=t_msk1[:, q * CAPQ:(q + 1) * CAPQ], in0=slpq,
                        scalar1=t_nfb[0:16, q:q + 1], scalar2=None, op0=ALU.is_ge))(q),
                        reads=["nfb", "slotpos"], writes=["msk1"])
            for q in range(4):
                TK("vector", (lambda q: lambda e: e.copy_predicated(
                    out=s1q[:, q * CAPQ:(q + 1) * CAPQ],
                    mask=t_msk1[:, q * CAPQ:(q + 1) * CAPQ], data=neg1[:, 0:CAPQ]))(q),
                    reads=["msk1", "neg1"], writes=["s1"])
            TK("gpsimd", (lambda s_: lambda e: e.sparse_gather(
                s2q, s1q, num_found=t_nf[:, 12 + s_:13 + s_]))(s_),
                reads=["s1"], writes=["s2", "nf"])
            if s_ < 2:
                TK("vector", (lambda e_, s_: lambda e: e.tensor_copy(
                    out=idxv[0:16, e_, s_, :], in_=s2q[:, 0:CAPK // 16]))(e_, s_),
                    reads=["s2"], writes=["idx"])
            else:
                TK("sync", (lambda e_: lambda e: e.dma_start(
                    out=d_wflat[e_, :].rearrange("(f q) -> q f", q=16),
                    in_=s2q[:, 0:CAPK // 16],
                    allow_slow_non_contiguous=True))(e_),
                    reads=["s2"], writes=["wflatD"], sem="qsync", inc=16)
                TK("sync", (lambda e_: lambda e: e.dma_start(
                    out=wcolv[:, e_, :], in_=d_wflat[e_, :].rearrange("(q p) -> p q", p=128),
                    allow_slow_non_contiguous=True))(e_),
                    reads=["wflatD"], writes=["wcol"], sem="qsync", inc=16)
    for grp in range(1, 8):
        TK("sync", (lambda grp: lambda e: e.dma_start(
            out=t_idx[grp * 16:(grp + 1) * 16, :], in_=t_idx[0:16, :]))(grp),
            reads=["idx"], writes=["idxR"], sem="qsync", inc=16)

    pump_shared(10 ** 6)

    FENCE(["wmv"] + [f"keff{d}" for d in range(DC)] + ["amat"]
          + [f"cqT{q}_{t}" for q in range(LC) for t in range(NTB)]
          + [f"ckvT{q}_{t}" for q in range(LC) for t in range(NTB)]
          + ["mh"] + [f"ckvtm{s}" for s in range(NX)]
          + [f"ybuf{q}" for q in range(4)] + ["masks", "veff", "wl", "wl2"]
          + [f"H{ic}" for ic in range(ICH)]
          + ["xg0", "xg1", "wdT0", "wdT1", "wgT0", "wgT1", "wuT0", "wuT1", "eoS"]
          + ["qlat", "qlatB", "EA0", "EA1", "ctx"] + [f"hs{k}" for k in range(SIC)]
          + [f"xT{g}" for g in range(NX)]
          + [f"wgc{b}" for b in range(4)] + [f"wuc{b}" for b in range(4)])
    allAG = [f"x1allD{i}" for i in range(NTB)]
    EOPK = ["ps0", "ps1", "pc0", "pc1"]
    wgbig = t_big[:, 0:4 * DM].rearrange("p (b c i) -> p b c i", b=4, c=DC)
    wubig = t_big[:, 4 * DM:8 * DM].rearrange("p (b c i) -> p b c i", b=4, c=DC)

    def emit_gather(e_, sc8):
        par = (e_ * NGC + sc8) % 2
        TK("gpsimd", (lambda e_, sc8, par: lambda e: e.dma_gather(
            xgv[:, par, :, :], d_x1all[:, :], idxv[:, e_, 1, sc8 * 32:(sc8 + 1) * 32],
            GCH, GCH, DM, transpose=True, single_packet=False))(e_, sc8, par),
            reads=allAG + ["idx", "idxR"], writes=[f"xg{par}"], sem="qgp", inc=16)

    emit_gather(0, 0)
    for e_ in range(EPC):
        for sc8 in range(NGC):
            par = (e_ * NGC + sc8) % 2
            for ic in range(ICH):
                wb = ic % 4
                TK("sync", (lambda e_, ic, wb: lambda e: e.dma_start(
                    out=wgbig[:, wb, :, :],
                    in_=d_wgB[e_, ic, :, :].rearrange("p (c i) -> p c i", c=DC)))(e_, ic, wb),
                    reads=[f"wgB{e_}"], writes=[f"wgc{wb}"], sem="qsync", inc=16)
                TK("sync", (lambda e_, ic, wb: lambda e: e.dma_start(
                    out=wubig[:, wb, :, :],
                    in_=d_wuB[e_, ic, :, :].rearrange("p (c i) -> p c i", c=DC)))(e_, ic, wb),
                    reads=[f"wuB{e_}"], writes=[f"wuc{wb}"], sem="qsync", inc=16)

                def mk_g(par, wb, wv, pk):
                    def f(e):
                        mm = None
                        for dc in range(DC):
                            mm = e.matmul(PB[pk][:, 0:GCH], wv[:, wb, dc, :],
                                          xgv[:, par, dc, :],
                                          start=(dc == 0), stop=(dc == DC - 1))
                        return mm
                    return f
                TK("tensor", mk_g(par, wb, wgbig, "pg0"),
                   reads=[f"xg{par}", f"wgc{wb}"], writes=["pg0"])
                TK("tensor", mk_g(par, wb, wubig, "pg1"),
                   reads=[f"xg{par}", f"wuc{wb}"], writes=["pg1"])
                TK("scalar", (lambda ic: lambda e: e.activation(
                    out=Hs(ic), in_=p_g0[:, 0:GCH], func=AF.Gelu))(ic),
                    reads=["pg0"], writes=[f"H{ic}"])
                TK("vector", (lambda ic: lambda e: e.tensor_tensor(
                    out=Hs(ic), in0=Hs(ic), in1=p_g1[:, 0:GCH], op=ALU.mult))(ic),
                    reads=["pg1"], writes=[f"H{ic}"])
            if sc8 + 1 < NGC:
                emit_gather(e_, sc8 + 1)
            elif e_ + 1 < EPC:
                emit_gather(e_ + 1, 0)
            for dh in range(2):
                for ich in range(2):
                    wdpar = ich
                    TK("sync", (lambda e_, dh, ich, wdpar: lambda e: e.dma_start(
                        out=(wd0a if wdpar == 0 else wd0b)[:, :, :],
                        in_=d_wdB[e_, ich * 2048:(ich + 1) * 2048, dh * 512:(dh + 1) * 512]
                        .rearrange("(c p) d -> p c d", p=128)))(e_, dh, ich, wdpar),
                        reads=[f"wdB{e_}"], writes=[f"wdT{wdpar}"], sem="qsync", inc=16)
                    for tc4 in range(NTC):
                        def mk_dn(ich, wdpar, tc4, pk):
                            def f(e):
                                wsrc = wd0a if wdpar == 0 else wd0b
                                mm = None
                                for k in range(16):
                                    mm = e.matmul(PB[pk][:, 0:512],
                                                  Hs(ich * 16 + k)[:, tc4 * 128:(tc4 + 1) * 128],
                                                  wsrc[:, k, :],
                                                  start=(ich == 0 and k == 0),
                                                  stop=(ich == 1 and k == 15))
                                return mm
                            return f
                        TK("tensor", mk_dn(ich, wdpar, tc4, EOPK[tc4]),
                           reads=[f"H{ich * 16 + k}" for k in range(16)] + [f"wdT{wdpar}"],
                           writes=[EOPK[tc4]])
                for tc4 in range(NTC):
                    TK("vector", (lambda e_, sc8, dh, tc4: lambda e: e.tensor_scalar(
                        out=eov[:, tc4, dh * 512:(dh + 1) * 512],
                        in0=PB[EOPK[tc4]][:, 0:512],
                        scalar1=wcolv[:, e_, sc8 * NTC + tc4:sc8 * NTC + tc4 + 1],
                        scalar2=None, op0=ALU.mult))(e_, sc8, dh, tc4),
                        reads=[EOPK[tc4], "wcol"], writes=["eoS"])
            TK("gpsimd", (lambda e_, sc8: lambda e: e.dma_scatter_add(
                d_acc[:, :], eov[:, :, :],
                idxv[:, e_, 1, sc8 * 32:(sc8 + 1) * 32],
                GCH, GCH, DM, single_packet=False))(e_, sc8),
                reads=["eoS", "idx", "idxR"], writes=["accD"], sem="qgp", inc=16)

    for k in range(NTB):
        TK("gpsimd", (lambda k: lambda e: e.collective_compute(
            "ReduceScatter", ALU.add, replica_groups=RGRP,
            ins=[d_acc[k * B * TB:(k + 1) * B * TB, :]],
            outs=[d_moe[k * TB:(k + 1) * TB, :]]))(k),
            reads=["accD"], writes=[f"moeD{k}"], sem="cc", inc=1)

    FENCE([f"sco{q}" for q in range(NX)] + ["cmpSk", "sq", "r1", "mo0", "mo1",
                                            "shb0", "shb1", "H0", "H1", "H2", "H3",
                                            "H28", "H29", "H30", "H31", "EA0", "EA1",
                                            "ctx", "qlatB", "qlat",
                                            "val", "s1", "s2", "neg1", "sct", "scw",
                                            "slotpos", "tabN", "tabP", "r1b", "sqb",
                                            "st", "stb"])
    r1b_ = (t_r1, t_pack[:, 0:DM])
    sqb_ = (t_sq, t_pack[:, DM:2 * DM])
    stc_ = (0, 10)
    for gtc in range(NX):
        par = gtc % 2
        r1t, sqt, s0 = r1b_[par], sqb_[par], stc_[par]
        rk, qk, sk = ("r1", "sq", "st") if par == 0 else ("r1b", "sqb", "stb")
        TK("sync", (lambda gtc, r1t: lambda e: e.dma_start(
            out=r1t[:, :], in_=d_x1f[gtc * 128:(gtc + 1) * 128, :]))(gtc, r1t),
            reads=["x1fD"], writes=[rk], sem="qsync", inc=16)
        TK("gpsimd", (lambda gtc, par: lambda e: e.dma_start(
            out=mov[:, par, :], in_=d_moe[gtc * 128:(gtc + 1) * 128, :]))(gtc, par),
            reads=[f"moeD{gtc // NTC}"], writes=[f"mo{par}"], sem="qgp", inc=16)
        TK("gpsimd", (lambda gtc, par: lambda e: e.dma_start(
            out=Shb(par), in_=d_sh[gtc * 128:(gtc + 1) * 128, :]))(gtc, par),
            reads=["shD"], writes=[f"shb{par}"], sem="qgp", inc=16)

        def mk_cmb(par, r1t, sqt, s0):
            def f(e):
                e.tensor_tensor(out=r1t[:, :], in0=r1t[:, :], in1=mov[:, par, :], op=ALU.add)
                e.drain()
                e.tensor_tensor(out=r1t[:, :], in0=r1t[:, :], in1=Shb(par), op=ALU.add)
                e.drain()
                e.reduce_sum(out=t_st[:, s0:s0 + 1], in_=r1t[:, :], axis=AX.X)
                e.drain()
                e.tensor_scalar(out=t_st[:, s0 + 1:s0 + 2], in0=t_st[:, s0:s0 + 1],
                                scalar1=1.0 / DM, scalar2=None, op0=ALU.mult)
                e.drain()
                return e.tensor_scalar(out=sqt[:, :], in0=r1t[:, :],
                                       scalar1=t_st[:, s0 + 1:s0 + 2],
                                       scalar2=None, op0=ALU.subtract)
            return f
        TK("vector", mk_cmb(par, r1t, sqt, s0), reads=[f"mo{par}", f"shb{par}"],
           writes=[rk, qk, sk])

        def mk_ln2B(sqt, s0):
            def f(e):
                e.activation(out=sqt[:, :], in_=sqt[:, :], func=AF.Square,
                             accum_out=t_st[:, s0 + 2:s0 + 3])
                e.drain()
                e.activation(out=t_st[:, s0 + 3:s0 + 4], in_=t_st[:, s0 + 2:s0 + 3],
                             func=AF.Ln, scale=1.0 / DM, bias=t_st[:, 8:9])
                e.drain()
                return e.activation(out=t_st[:, s0 + 4:s0 + 5], in_=t_st[:, s0 + 3:s0 + 4],
                                    func=AF.Exp, scale=-0.5)
            return f
        TK("scalar", mk_ln2B(sqt, s0), reads=[qk, "epsc"], writes=[qk, sk])
        TK("vector", (lambda r1t, s0: lambda e: e.tensor_scalar(
            out=r1t[:, :], in0=r1t[:, :], scalar1=t_st[:, s0 + 1:s0 + 2],
            scalar2=t_st[:, s0 + 4:s0 + 5], op0=ALU.subtract, op1=ALU.mult))(r1t, s0),
            reads=[sk], writes=[rk])
        TK("sync", (lambda gtc, r1t: lambda e: e.dma_start(
            out=d_out[gtc * 128:(gtc + 1) * 128, :], in_=r1t[:, :]))(gtc, r1t),
            reads=[rk], writes=["outD"], sem="qsync", inc=16)

    TK("sync", lambda e: None, reads=["outD"])

    block = ctx.enter_context(nc.Block())
    R.replay(block)
    ctx.close()
    nc.finalize()
    return nc


def _host_tables():
    p = np.arange(128)[:, None]
    v = np.arange(896)[None, :]
    masks = ((p + 384) <= v).astype(np.float32).astype(BF)
    identf = np.eye(128, dtype=np.float32)
    P, Fw = 16, TALL // 16
    t_of = (np.arange(P)[:, None] * Fw + np.arange(Fw)[None, :]).astype(np.int64)
    tabN = t_of.astype(np.float32)
    tt = t_of.ravel()
    c = tt // T
    ib = (tt % T) // TB
    r = tt % TB
    rows = ib * (B * TB) + c * TB + r
    tabP = rows.reshape(P, Fw).astype(np.float32)
    sp = np.zeros((16, CAPQ), np.float32)
    for j in range(16 * CAPQ):
        sp[j % 16, j // 16] = j
    return masks, identf, tabN, tabP, sp


def _eidx_table(c):
    j = np.arange(128)
    e_, b, k = j // 64, (j // 8) % 8, j % 8
    rows = ((b * NE + 2 * c + e_) * 8 + k).astype(np.int16)
    wrap = np.zeros((16, 8), np.int16)
    wrap[j % 16, j // 16] = rows
    return np.tile(wrap, (8, 1))


_NCCACHE = {}


def kernel(**inputs):
    x = np.ascontiguousarray(np.asarray(inputs["L0_attn_x"], dtype=np.float32))
    W_dq = np.asarray(inputs["L0_attn_W_dq"], np.float32)
    W_uq = np.asarray(inputs["L0_attn_W_uq"], np.float32)
    W_dkv = np.asarray(inputs["L0_attn_W_dkv"], np.float32)
    W_uk = np.asarray(inputs["L0_attn_W_uk"], np.float32)
    W_uv = np.asarray(inputs["L0_attn_W_uv"], np.float32)
    W_o = np.asarray(inputs["L0_attn_W_o"], np.float32)
    gate_w = np.asarray(inputs["L0_ffn_gate_w"], np.float32)
    Wg = np.asarray(inputs["L0_ffn_Wg"], np.float32)
    Wu = np.asarray(inputs["L0_ffn_Wu"], np.float32)
    Wd = np.asarray(inputs["L0_ffn_Wd"], np.float32)
    Sg = np.asarray(inputs["L0_ffn_Sg"], np.float32)
    Su = np.asarray(inputs["L0_ffn_Su"], np.float32)
    Sd = np.asarray(inputs["L0_ffn_Sd"], np.float32)

    masks, identf, tabN, tabP, slotpos = _host_tables()
    cc = np.ascontiguousarray
    shared_in = {
        "wdq": cc(W_dq), "wdqT": cc(W_dq.T), "wuq": cc(W_uq), "wuk": cc(W_uk),
        "wdkvT": cc(W_dkv.T), "wuv": cc(W_uv), "woT": cc(W_o.T),
        "sgT": cc(Sg.T), "suT": cc(Su.T), "sdT": cc(Sd.T), "gwT": cc(gate_w.T),
        "masks": masks, "identf": identf, "tabN": tabN, "tabP": tabP,
        "slotpos": slotpos,
    }
    in_maps = []
    for c in range(B):
        m = dict(shared_in)
        m["x"] = cc(x[c])
        m["wgT"] = cc(Wg[2 * c:2 * c + 2].transpose(0, 2, 1))
        m["wuT"] = cc(Wu[2 * c:2 * c + 2].transpose(0, 2, 1))
        m["wdT"] = cc(Wd[2 * c:2 * c + 2].transpose(0, 2, 1))
        m["eidx"] = _eidx_table(c)
        in_maps.append(m)

    if "full" not in _NCCACHE:
        _NCCACHE["full"] = build()
    nc = _NCCACHE["full"]
    res = run_bass_kernel_spmd(nc, in_maps, core_ids=list(range(B)))
    out = np.empty((B, T, DM), np.float32)
    for c in range(B):
        out[c] = res.results[c]["out"]
    return out
